# version 01
# baseline (speedup 1.0000x reference)
import numpy as np
import concourse.bass as bass
import concourse.mybir as mybir
from concourse.bass_utils import run_bass_kernel_spmd
from contextlib import ExitStack

F32 = mybir.dt.float32
BF16 = mybir.dt.bfloat16
AF = mybir.ActivationFunctionType
ALU = mybir.AluOpType
AXX = mybir.AxisListType.X

T = 4096
D = 1024
S = 512
NSTRIP = T // S
NA = D // 128

ALL_STAGES = ("ffn0a", "attn0", "ffn0b", "ffn1a", "attn1", "ffn1b")

REG_CONSTS = [-12.0, 0.125, 1.0, 0.25, 0.5]


class Sem:
    __slots__ = ("h", "v", "name")

    def __init__(self, nc, es, name):
        self.h = es.enter_context(nc.semaphore(name))
        self.v = 0
        self.name = name


class Sched:

    def __init__(self):
        self.ops = []

    def op(self, eng, fn=None, waits=(), incs=()):
        rincs = []
        for (sem, k) in incs:
            sem.v += k
            rincs.append((sem, k))
        self.ops.append((eng, fn, [(s, int(v)) for (s, v) in waits], rincs))

    def w(self, *sems):
        return [(s, s.v) for s in sems]

    def emit(self, eng_name, eng):
        seen = {}
        for e, fn, waits, incs in self.ops:
            if e != eng_name:
                continue
            for (sem, val) in waits:
                if val > 0 and seen.get(sem.name, -1) < val:
                    eng.wait_ge(sem.h, val)
                    seen[sem.name] = val
            inst = fn(eng) if fn is not None else None
            if incs:
                assert inst is not None and len(incs) == 1, \
                    "each sem inc must ride a real instruction"
                inst.then_inc(incs[0][0].h, incs[0][1])

    def emit_all(self, block):
        for name in ("sync", "tensor", "scalar", "vector", "gpsimd"):
            getattr(block, name)(lambda eng, n=name: self.emit(n, eng))


class WRing:

    def __init__(self, nc, es, P, name, nslots, slot_cols):
        self.sb = es.enter_context(nc.sbuf_tensor(name, [128, nslots * slot_cols], BF16))
        self.nslots = nslots
        self.cols = slot_cols
        self.P = P
        self.sw = [Sem(nc, es, f"{name}_w{i}") for i in range(nslots)]
        self.freed_by = {}
        self.seq = 0

    def load(self, dram_ap, kcnt, mcols, extra_waits=()):
        ncols = kcnt * mcols
        assert ncols <= self.cols
        seq = self.seq
        self.seq += 1
        slot = seq % self.nslots
        ap = self.sb[:, slot * self.cols: slot * self.cols + ncols]
        out = ap.rearrange("p (kt m) -> p kt m", m=mcols)
        waits = list(extra_waits)
        if seq >= self.nslots:
            waits.append(self.freed_by.pop(seq - self.nslots))
        self.P.op("sync", lambda e, out=out, d=dram_ap: e.dma_start(out, d),
                  waits=waits, incs=[(self.sw[slot], 16)])
        return ap, seq

    def use_wait(self, seq):
        return [(self.sw[seq % self.nslots], 16 * (seq // self.nslots + 1))]


def build_nc(stages=ALL_STAGES):
    nc = bass.Bass("TRN2", target_bir_lowering=False, debug=False)
    es = ExitStack()
    P = Sched()

    din = {}

    def dri(name, shape, dt=F32):
        din[name] = nc.dram_tensor(name, list(shape), dt, kind="ExternalInput").ap()
        return din[name]

    xT = dri("xT", (D, T))
    dri("cconst", (128, len(REG_CONSTS)))
    dri("ones_row", (1, 128 + S))
    dri("identc", (128, 128))

    need = set()
    for st in stages:
        need.add(st[:4] if st.startswith("ffn") else st)
    if "ffn0" in need:
        dri("gate", (D, 8))
        dri("esel", (9, 9 * 128))
        dri("moe_w1", (8, D, 2048))
        dri("moe_b1", (1, 8 * 2048))
        dri("moe_w2", (8, 2048, D))
        dri("moe_b2", (8, D))
    if "attn0" in need:
        dri("a0_wq", (D, D))
        dri("a0_wkv", (D, 128))
        dri("a0_wk", (128, 512))
        dri("a0_wv", (128, 512))
        dri("a0_wo", (768, D))
        dri("a0_qg", (128, 1))
        dri("a0_kg", (128, 1))
        dri("a0_qones", (D, 16))
        dri("a0_kones", (512, 16))
        dri("a0_qsel", (128, 11 * 128))
        dri("a0_ksel", (128, 8 * 128))
        dri("a0_hot", (16, 768))
        dri("dmask", (128, 4 * S))
    if "attn1" in need:
        dri("a1_wqT", (D, 768))
        dri("a1_wkT", (D, 768))
        dri("a1_wvT", (D, D))
        dri("a1_woT", (D, D))
        dri("hmask", (128, 5 * S))
    if "ffn1" in need:
        dri("f1_w1T", (D, 4096))
        dri("f1_w2T", (D, 4096))
        dri("f1_h1T", (4096, 4096))
        dri("f1_h2T", (4096, 4096))
        dri("f1_w3T", (4096, D))
        dri("f1_s1", (128, 32))
        dri("f1_s2", (128, 32))
        dri("f1_sh1", (128, 32))
        dri("f1_sh2", (128, 32))
        dri("f1_s3", (128, 8))

    xres = nc.dram_tensor("xres", [D, T], F32, kind="Internal").ap()
    outT = nc.dram_tensor("outT", [D, T], F32, kind="ExternalOutput").ap()

    wb = {}

    def drw(name, shape):
        wb[name] = nc.dram_tensor(name + "_b", list(shape), BF16, kind="Internal").ap()
        return wb[name]

    if "ffn0" in need:
        drw("gate", (D, 8))
        drw("moe_w1", (8 * D, 2048))
        drw("moe_w2", (8 * 2048, D))
    if "attn0" in need:
        for n, sh in [("a0_wq", (D, D)), ("a0_wkv", (D, 128)), ("a0_wk", (128, 512)),
                      ("a0_wv", (128, 512)), ("a0_wo", (768, D))]:
            drw(n, sh)
    if "attn1" in need:
        for n, sh in [("a1_wqT", (D, 768)), ("a1_wkT", (D, 768)),
                      ("a1_wvT", (D, D)), ("a1_woT", (D, D))]:
            drw(n, sh)
        ctx_vdram = nc.dram_tensor("vdram", [T, D], BF16, kind="Internal").ap()
    if "ffn1" in need:
        for n, sh in [("f1_w1T", (D, 4096)), ("f1_w2T", (D, 4096)),
                      ("f1_h1T", (4096, 4096)), ("f1_h2T", (4096, 4096)),
                      ("f1_w3T", (4096, D))]:
            drw(n, sh)

    def sb(name, shape, dt=F32):
        return es.enter_context(nc.sbuf_tensor(name, list(shape), dt))

    def psum(name, shape, dt=F32):
        return es.enter_context(nc.psum_tensor(name, list(shape), dt))

    c = {}
    c["cc"] = sb("cc", (128, len(REG_CONSTS)))
    for i, v in enumerate(REG_CONSTS):
        nc.const_aps.register(F32, v, c["cc"][:, i:i + 1])
    c["ones_col_f"] = sb("ones_col_f", (128, 1))
    c["ones_col_b"] = sb("ones_col_b", (128, 1), BF16)
    c["ones_row_f"] = sb("ones_row_f", (1, 128))
    c["ones_row_b"] = sb("ones_row_b", (1, S), BF16)
    c["ident_f"] = sb("ident_f", (128, 128))
    c["ones_mat"] = sb("ones_mat", (128, S), BF16)

    sboot = Sem(nc, es, "boot")
    sboot2 = Sem(nc, es, "boot2")

    P.op("gpsimd", lambda e: e.dma_start(c["cc"][:, :], din["cconst"]), incs=[(sboot, 16)])
    orow = din["ones_row"]
    P.op("gpsimd", lambda e: e.dma_start(c["ones_col_f"][:, :],
                                         orow[0:1, 0:128].rearrange("a p -> p a")), incs=[(sboot, 16)])
    P.op("gpsimd", lambda e: e.dma_start(c["ones_col_b"][:, :],
                                         orow[0:1, 0:128].rearrange("a p -> p a")), incs=[(sboot, 16)])
    P.op("gpsimd", lambda e: e.dma_start(c["ones_row_f"][:, :], orow[0:1, 0:128]), incs=[(sboot, 16)])
    P.op("gpsimd", lambda e: e.dma_start(c["ones_row_b"][:, :], orow[0:1, 128:128 + S]), incs=[(sboot, 16)])
    P.op("gpsimd", lambda e: e.dma_start(c["ident_f"][:, :], din["identc"]), incs=[(sboot, 16)])
    P.op("gpsimd", lambda e: e.memset(c["ones_mat"][:, :], 1.0), incs=[(sboot2, 1)])
    boot_n = sboot.v
    boot2_n = sboot2.v

    sres = Sem(nc, es, "resinit")
    P.op("gpsimd", lambda e: e.dma_start(xres, xT), incs=[(sres, 16)])

    prep_done = {}
    order = []
    for st in stages:
        key = st[:4] if st.startswith("ffn") else st
        if key not in order:
            order.append(key)
    group = {"ffn0": ["gate", "moe_w1", "moe_w2"],
             "attn0": ["a0_wq", "a0_wkv", "a0_wk", "a0_wv", "a0_wo"],
             "attn1": ["a1_wqT", "a1_wkT", "a1_wvT", "a1_woT"],
             "ffn1": ["f1_w1T", "f1_w2T", "f1_h1T", "f1_h2T", "f1_w3T"]}
    for key in order:
        sp = Sem(nc, es, "prep_" + key)
        for n in group[key]:
            src = din[n]
            if len(src.shape) == 3:
                src = src.rearrange("e k m -> (e k) m")
            P.op("gpsimd", lambda e, n=n, src=src: e.dma_start(wb[n], src),
                 incs=[(sp, 16)])
        prep_done[key] = (sp, sp.v)

    for e in ("tensor", "scalar", "vector", "sync"):
        P.op(e, None, waits=[(sboot, boot_n), (sboot2, boot2_n)])

    xs = sb("xs", (128, NA * S))
    x2b = sb("x2b", (128, NA * S), BF16)
    xn = sb("xn", (128, NA * S), BF16)
    xo = sb("xo", (128, NA * S))
    t1 = xo
    rws = sb("rws", (1, 4 * S))
    paux = psum("paux", (128, 2 * S))
    abp = psum("abp", (128, 2 * S))
    pp = psum("pp", (128, 4 * S))
    stpA = paux[0:1, 0:S]
    stpB = paux[0:1, S:2 * S]

    ring = WRing(nc, es, P, "wring", 2, 4096)

    sems = {n: Sem(nc, es, n) for n in ("sx", "evv", "evs", "smm", "sout")}

    ctx = dict(nc=nc, P=P, c=c, din=din, wb=wb, xres=xres, outT=outT,
               vdram=(ctx_vdram if "attn1" in need else None),
               xs=xs, x2b=x2b, xn=xn, t1=t1, xo=xo, rws=rws,
               stpA=stpA, stpB=stpB, abp=abp, pp=pp, aux=paux, ring=ring,
               sems=sems, sres=sres, prep_done=prep_done, es=es)

    with es:
        block = es.enter_context(nc.Block())
        P.emit_all(block)
        for si, st in enumerate(stages):
            is_last = (si == len(stages) - 1)
            src = xT if si == 0 else xres
            dst = outT if is_last else xres
            P = Sched()
            ctx["P"] = P
            ring.P = P
            with ExitStack() as stage_es:
                ctx["stage_es"] = stage_es
                if st in ("ffn1a", "ffn1b"):
                    emit_ffn1(ctx, src, dst)
                elif st in ("ffn0a", "ffn0b"):
                    emit_ffn0(ctx, src, dst)
                elif st == "attn0":
                    emit_attn0(ctx, src, dst)
                elif st == "attn1":
                    emit_attn1(ctx, src, dst)
                else:
                    raise ValueError(st)
                ctx["_res_ready"] = (sems["sout"], sems["sout"].v)
                P.emit_all(block)

        P = Sched()
        P.op("sync", None, waits=P.w(sems["sout"]))
        P.emit_all(block)
    return nc


def emit_ln(ctx, srcdram, s, xh, extra_x_waits=(), xh_guard=()):
    P, c = ctx["P"], ctx["c"]
    xs, x2b, t1, rws = ctx["xs"], ctx["x2b"], ctx["t1"], ctx["rws"]
    stpA, stpB, abp = ctx["stpA"], ctx["stpB"], ctx["abp"]
    sems = ctx["sems"]
    sx, evv, evs, smm = sems["sx"], sems["evv"], sems["evs"], sems["smm"]
    sres = ctx["sres"]

    src = srcdram.rearrange("(a p) t -> p a t", p=128)[:, :, s * S:(s + 1) * S]
    P.op("gpsimd", lambda e: e.dma_start(
        xs[:, :].rearrange("p (a t) -> p a t", a=NA), src),
        waits=[(sres, 16), ctx.get("_res_ready", (sres, 0))] + list(extra_x_waits),
        incs=[(sx, 16)])
    xw = [(sx, sx.v)]

    for a in range(NA):
        P.op("scalar", lambda e, a=a: e.activation(
            x2b[:, a * S:(a + 1) * S], xs[:, a * S:(a + 1) * S], AF.Square),
            waits=xw + [(smm, ctx.get("_ln_x2b_g", 0))], incs=[(evs, 1)])
    sqv = evs.v
    rowg = [(evv, ctx.get("_ln_rows_g", 0))]
    for a in range(NA):
        P.op("tensor", lambda e, a=a: e.matmul(
            stpA[0:1, :], c["ones_col_f"][:, :], xs[:, a * S:(a + 1) * S],
            start=(a == 0), stop=(a == NA - 1)),
            waits=xw + rowg, incs=[(smm, 1)] if a == NA - 1 else [])
    for a in range(NA):
        P.op("tensor", lambda e, a=a: e.matmul(
            stpB[0:1, :], c["ones_col_b"][:, :], x2b[:, a * S:(a + 1) * S],
            start=(a == 0), stop=(a == NA - 1)),
            waits=[(evs, sqv)] + rowg, incs=[(smm, 1)] if a == NA - 1 else [])
    mst = smm.v
    ctx["_ln_x2b_g"] = mst

    m = rws[0:1, 0:S]; arow = rws[0:1, S:2 * S]
    brow = rws[0:1, 2 * S:3 * S]; trow = rws[0:1, 3 * S:4 * S]
    P.op("vector", lambda e: e.tensor_scalar(m, stpA[0:1, :], 1.0 / D, None,
                                             op0=ALU.mult),
         waits=[(smm, mst)], incs=[(evv, 1)])
    P.op("vector", lambda e: e.tensor_scalar(trow, stpB[0:1, :], 1.0 / D, 1e-5,
                                             op0=ALU.mult, op1=ALU.add),
         incs=[(evv, 1)])
    ctx["_ln_rows_g"] = evv.v
    P.op("vector", lambda e: e.scalar_tensor_tensor(
        brow, m, -1.0, m, op0=ALU.mult, op1=ALU.mult),
        waits=P.w(evv), incs=[(evv, 1)])
    P.op("vector", lambda e: e.tensor_add(trow, trow, brow),
         waits=P.w(evv), incs=[(evv, 1)])
    P.op("vector", lambda e: e.reciprocal(brow, trow),
         waits=P.w(evv), incs=[(evv, 1)])
    P.op("scalar", lambda e: e.activation(arow, brow, AF.Sqrt),
         waits=P.w(evv), incs=[(evs, 1)])
    P.op("vector", lambda e: e.scalar_tensor_tensor(
        brow, m, -1.0, arow, op0=ALU.mult, op1=ALU.mult),
        waits=P.w(evs), incs=[(evv, 1)])
    evrow = evv.v

    P.op("tensor", lambda e: e.matmul(abp[:, 0:S], c["ones_row_f"][:, :], arow,
                                      start=True, stop=True),
         waits=[(evv, evrow)])
    P.op("tensor", lambda e: e.matmul(abp[:, S:2 * S], c["ones_row_f"][:, :], brow,
                                      start=True, stop=True), incs=[(smm, 1)])
    mbc = smm.v
    xog = [(ctx["sems"]["sout"], ctx.get("_xo_guard", 0))]
    for a in range(NA):
        P.op("vector", lambda e, a=a: e.tensor_mul(
            t1[:, a * S:(a + 1) * S], xs[:, a * S:(a + 1) * S], abp[:, 0:S]),
            waits=[(smm, mbc)] + list(xh_guard) + xog, incs=[(evv, 1)])
        P.op("vector", lambda e, a=a: e.tensor_add(
            xh[:, a * S:(a + 1) * S], t1[:, a * S:(a + 1) * S], abp[:, S:2 * S]),
            waits=P.w(evv), incs=[(evv, 1)])
    ln_done = evv.v
    return [(evv, ln_done)]


def emit_gemm(ctx, wdram, nk, nm, src_fn, evac, src_waits, extra_mm=None,
              prep_wait=()):
    P = ctx["P"]
    ring, pp = ctx["ring"], ctx["pp"]
    sems = ctx["sems"]
    smm, evv, evs = sems["smm"], sems["evv"], sems["evs"]
    wre = wdram.rearrange("(kt p) m -> p kt m", p=128)
    kb = 8
    for mb in range(0, nm, 4):
        mcnt = min(4, nm - mb)
        mcols = mcnt * 128
        batches = []
        for k0 in range(0, nk, kb):
            kcnt = min(kb, nk - k0)
            batches.append((k0, kcnt))

        def do_load(i):
            k0, kcnt = batches[i]
            dap = wre[:, k0:k0 + kcnt, mb * 128: mb * 128 + mcols]
            return ring.load(dap, kcnt, mcols, extra_waits=prep_wait)

        loaded = {0: do_load(0)}
        if len(batches) > 1:
            loaded[1] = do_load(1)
        guard = list(ctx.get("_gemm_guard", []))
        first = True
        for i, (k0, kcnt) in enumerate(batches):
            ap, seq = loaded.pop(i)
            for ki in range(kcnt):
                k = k0 + ki
                for mi in range(mcnt):
                    lhs = ap[:, ki * mcols + mi * 128: ki * mcols + (mi + 1) * 128]
                    last_of_slot = (ki == kcnt - 1 and mi == mcnt - 1)
                    stop = (k == nk - 1 and extra_mm is None)
                    P.op("tensor", lambda e, lhs=lhs, k=k, mi=mi, stop=stop:
                         e.matmul(pp[:, mi * S:(mi + 1) * S], lhs, src_fn(k),
                                  start=(k == 0), stop=stop),
                         waits=(ring.use_wait(seq) +
                                (src_waits + guard if first else [])),
                         incs=[(smm, 1)] if last_of_slot else [])
                    first = False
            ring.freed_by[seq] = (smm, smm.v)
            if i + 2 < len(batches):
                loaded[i + 2] = do_load(i + 2)
        if extra_mm is not None:
            for mi in range(mcnt):
                extra_mm(mb + mi, mi, last=(mi == mcnt - 1))
        gv = smm.v
        for mi in range(mcnt):
            evac(mb + mi, pp[:, mi * S:(mi + 1) * S], gv)
        ctx["_gemm_guard"] = [(evv, evv.v), (evs, evs.v)]
    return


def emit_ffn1(ctx, src, dst):
    P, wb, din = ctx["P"], ctx["wb"], ctx["din"]
    sems = ctx["sems"]
    evv, evs, smm, sout = sems["evv"], sems["evs"], sems["smm"], sems["sout"]
    xs, xo, xn = ctx["xs"], ctx["xo"], ctx["xn"]
    pd = [ctx["prep_done"]["ffn1"]]
    tag = str(smm.v)
    hA = ctx["stage_es"].enter_context(
        ctx["nc"].sbuf_tensor("hA" + tag, [128, 32 * S], BF16))
    hB = ctx["stage_es"].enter_context(
        ctx["nc"].sbuf_tensor("hB" + tag, [128, 32 * S], BF16))

    rdy = ctx.get("_res_ready", (ctx["sres"], 0))
    sc = ctx["stage_es"].enter_context(
        ctx["nc"].sbuf_tensor("f1sc" + tag, [128, 32 * 4 + 8 + 8], F32))
    sld = Sem(ctx["nc"], ctx["es"], "f1scl" + tag)
    for i, n in enumerate(("f1_s1", "f1_s2", "f1_sh1", "f1_sh2")):
        P.op("gpsimd", lambda e, i=i, n=n: e.dma_start(sc[:, i * 32:(i + 1) * 32], din[n]),
             waits=[rdy], incs=[(sld, 16)])
    P.op("gpsimd", lambda e: e.dma_start(sc[:, 128:136], din["f1_s3"]), incs=[(sld, 16)])
    P.op("vector", lambda e: e.tensor_scalar(sc[:, 136:144], sc[:, 128:136],
                                             0.5, None, op0=ALU.mult),
         waits=P.w(sld), incs=[(evv, 1)])
    screato = [(evv, evv.v)]
    s1 = lambda mt: sc[:, 0 * 32 + mt: 0 * 32 + mt + 1]
    s2 = lambda mt: sc[:, 1 * 32 + mt: 1 * 32 + mt + 1]
    sh1 = lambda mt: sc[:, 2 * 32 + mt: 2 * 32 + mt + 1]
    sh2 = lambda mt: sc[:, 3 * 32 + mt: 3 * 32 + mt + 1]
    s3h = lambda mt: sc[:, 136 + mt: 136 + mt + 1]

    ctx.setdefault("_xn_guard", 0)
    for s in range(NSTRIP):
        xh = xn[:, :]
        guard = list(ctx.get("_xs_guard", []))
        xnw = emit_ln(ctx, src, s, xh, extra_x_waits=guard,
                      xh_guard=[(smm, ctx["_xn_guard"])])
        xsrc = lambda k, xh=xh: xh[:, k * S:(k + 1) * S]

        def ev_w1(mt, pap, gv):
            P.op("scalar", lambda e, mt=mt, pap=pap: e.activation(
                hA[:, mt * S:(mt + 1) * S], pap, AF.Silu, scale=s1(mt)),
                waits=[(smm, gv)] + screato, incs=[(evs, 1)])
        emit_gemm(ctx, wb["f1_w1T"], NA, 32, xsrc, ev_w1, xnw, prep_wait=pd)
        w1_done = evs.v

        def ev_w2(mt, pap, gv):
            P.op("vector", lambda e, mt=mt, pap=pap: e.scalar_tensor_tensor(
                hB[:, mt * S:(mt + 1) * S], pap, s2(mt), hA[:, mt * S:(mt + 1) * S],
                op0=ALU.mult, op1=ALU.mult),
                waits=[(smm, gv), (evs, w1_done)], incs=[(evv, 1)])
        emit_gemm(ctx, wb["f1_w2T"], NA, 32, xsrc, ev_w2, xnw, prep_wait=pd)
        ctx["_xn_guard"] = smm.v
        hsrc = lambda k: hB[:, k * S:(k + 1) * S]
        hB_done = [(evv, evv.v)]

        def ev_h1(mt, pap, gv):
            P.op("scalar", lambda e, mt=mt, pap=pap: e.activation(
                hA[:, mt * S:(mt + 1) * S], pap, AF.Gelu, scale=sh1(mt)),
                waits=[(smm, gv)], incs=[(evs, 1)])
        emit_gemm(ctx, wb["f1_h1T"], 32, 32, hsrc, ev_h1, hB_done, prep_wait=pd)
        hA_done = [(evs, evs.v)]
        hsrcA = lambda k: hA[:, k * S:(k + 1) * S]

        def ev_h2(mt, pap, gv):
            P.op("scalar", lambda e, mt=mt, pap=pap: e.activation(
                hB[:, mt * S:(mt + 1) * S], pap, AF.Gelu, scale=sh2(mt)),
                waits=[(smm, gv)], incs=[(evs, 1)])
        emit_gemm(ctx, wb["f1_h2T"], 32, 32, hsrcA, ev_h2, hA_done, prep_wait=pd)
        hB2_done = [(evs, evs.v)]

        xog = [(sout, ctx.get("_xo_guard", 0))]

        def ev_w3(mt, pap, gv):
            P.op("vector", lambda e, mt=mt, pap=pap: e.scalar_tensor_tensor(
                xo[:, mt * S:(mt + 1) * S], pap, s3h(mt), xs[:, mt * S:(mt + 1) * S],
                op0=ALU.mult, op1=ALU.add),
                waits=[(smm, gv)] + xog, incs=[(evv, 1)])
        emit_gemm(ctx, wb["f1_w3T"], 32, NA, hsrc, ev_w3, hB2_done, prep_wait=pd)
        ctx["_xs_guard"] = [(evv, evv.v)]
        ddst = dst.rearrange("(a p) t -> p a t", p=128)[:, :, s * S:(s + 1) * S]
        P.op("sync", lambda e, ddst=ddst: e.dma_start(
            ddst, xo[:, :].rearrange("p (a t) -> p a t", a=NA)),
            waits=P.w(evv), incs=[(sout, 16)])
        ctx["_xo_guard"] = sout.v


def emit_ffn0(ctx, src, dst):
    nc, P, c, wb, din = ctx["nc"], ctx["P"], ctx["c"], ctx["wb"], ctx["din"]
    sems = ctx["sems"]
    evv, evs, smm, sout = sems["evv"], sems["evs"], sems["smm"], sems["sout"]
    xs, xo, xn = ctx["xs"], ctx["xo"], ctx["xn"]
    abp, aux = ctx["abp"], ctx["aux"]
    pd = [ctx["prep_done"]["ffn0"]]
    es2 = ctx["stage_es"]
    tag = str(smm.v)
    hB = es2.enter_context(nc.sbuf_tensor("moeh" + tag, [128, 16 * S], BF16))
    acc = es2.enter_context(nc.sbuf_tensor("moeacc" + tag, [128, NA * S], F32))
    rdy = ctx.get("_res_ready", (ctx["sres"], 0))
    gate_sb = es2.enter_context(nc.sbuf_tensor("gate_sb" + tag, [128, 64], BF16))
    b1_sb = es2.enter_context(nc.sbuf_tensor("b1_sb" + tag, [33, 8192], BF16))
    b2_sb = es2.enter_context(nc.sbuf_tensor("b2_sb" + tag, [8, D], BF16))
    gls = es2.enter_context(nc.sbuf_tensor("gls" + tag, [8, S], F32))
    gw = es2.enter_context(nc.sbuf_tensor("gw" + tag, [128, 4 * 32], F32))
    g9sf = es2.enter_context(nc.sbuf_tensor("g9sf" + tag, [9, S], F32))
    g9sb = es2.enter_context(nc.sbuf_tensor("g9sb" + tag, [8, S], BF16))
    gbe = es2.enter_context(nc.sbuf_tensor("gbe" + tag, [128, S], BF16))
    esel_sb = es2.enter_context(nc.sbuf_tensor("esel_sb" + tag, [9, 9 * 128], F32))
    sgl = Sem(nc, ctx["es"], "moe_ld" + tag)
    P.op("gpsimd", lambda e: e.dma_start(esel_sb[:, :], din["esel"]),
         waits=[rdy], incs=[(sgl, 16)])
    P.op("gpsimd", lambda e: e.dma_start(
        gate_sb[:, :].rearrange("p (k e) -> p k e", e=8),
        wb["gate"].rearrange("(k p) e -> p k e", p=128)),
        waits=[pd[0]], incs=[(sgl, 16)])
    for q in range(2):
        P.op("gpsimd", lambda e, q=q: e.dma_start(
            b1_sb[32 * q:32 * q + 1, :], din["moe_b1"][0:1, q * 8192:(q + 1) * 8192]),
            incs=[(sgl, 16)])
    P.op("gpsimd", lambda e: e.dma_start(b2_sb[:, :], din["moe_b2"]), incs=[(sgl, 16)])
    mld = [(sgl, sgl.v)]

    ctx.setdefault("_xn_guard", 0)
    for s in range(NSTRIP):
        xh = xn[:, :]
        guard = list(ctx.get("_xs_guard", []))
        xnw = emit_ln(ctx, src, s, xh, extra_x_waits=guard,
                      xh_guard=[(smm, ctx["_xn_guard"])])
        ln_done = xnw[0][1]

        glp = aux[0:8, 0:S]
        for k in range(NA):
            P.op("tensor", lambda e, k=k, xh=xh: e.matmul(
                glp, gate_sb[:, k * 8:(k + 1) * 8], xh[:, k * S:(k + 1) * S],
                start=(k == 0), stop=(k == NA - 1)),
                waits=(xnw + mld + [(evv, ctx.get("_moe_aux_g", 0))]),
                incs=[(smm, 1)] if k == NA - 1 else [])
        P.op("vector", lambda e: e.tensor_copy(gls[:, :], glp),
             waits=P.w(smm), incs=[(evv, 1)])
        glv = evv.v
        for j in range(4):
            P.op("tensor", lambda e, j=j: e.transpose(
                aux[0:128, S + j * 8: S + (j + 1) * 8], gls[:, j * 128:(j + 1) * 128],
                c["ident_f"][0:8, 0:8]),
                waits=[(evv, glv), (evv, ctx.get("_moe_aux2_g", 0))],
                incs=[(smm, 1)] if j == 3 else [])
        gtv = smm.v
        for j in range(4):
            eS = gw[:, j * 32 + 0: j * 32 + 8]
            tS = gw[:, j * 32 + 8: j * 32 + 16]
            g9 = gw[:, j * 32 + 16: j * 32 + 25]
            nmx = gw[:, j * 32 + 25: j * 32 + 26]
            m2c = gw[:, j * 32 + 26: j * 32 + 27]
            nrm = gw[:, j * 32 + 27: j * 32 + 28]
            tj = aux[0:128, S + j * 8: S + (j + 1) * 8]
            P.op("vector", lambda e, tj=tj, nmx=nmx: e.reduce_max(
                nmx, tj, axis=AXX, negate=True), waits=[(smm, gtv)], incs=[(evv, 1)])
            P.op("scalar", lambda e, tj=tj, eS=eS, nmx=nmx: e.activation(
                eS, tj, AF.Exp, bias=nmx), waits=P.w(evv), incs=[(evs, 1)])
            P.op("vector", lambda e, eS=eS, tS=tS: e.tensor_scalar(
                tS, eS, 1.0, None, op0=ALU.is_lt), waits=P.w(evs), incs=[(evv, 1)])
            P.op("vector", lambda e, eS=eS, tS=tS: e.tensor_mul(tS, tS, eS),
                 waits=P.w(evv), incs=[(evv, 1)])
            P.op("vector", lambda e, tS=tS, m2c=m2c: e.reduce_max(m2c, tS, axis=AXX),
                 waits=P.w(evv), incs=[(evv, 1)])
            P.op("vector", lambda e, eS=eS, tS=tS, m2c=m2c: e.tensor_scalar(
                tS, eS, m2c, None, op0=ALU.is_ge), waits=P.w(evv), incs=[(evv, 1)])
            P.op("vector", lambda e, eS=eS, tS=tS, g9=g9: e.tensor_mul(
                g9[:, 0:8], tS, eS), waits=P.w(evv), incs=[(evv, 1)])
            P.op("vector", lambda e, m2c=m2c, nrm=nrm: e.tensor_scalar(
                nrm, m2c, 1.0, None, op0=ALU.add), waits=P.w(evv), incs=[(evv, 1)])
            P.op("vector", lambda e, nrm=nrm: e.reciprocal(nrm, nrm),
                 waits=P.w(evv), incs=[(evv, 1)])
            P.op("vector", lambda e, nrm=nrm, g9=g9: e.tensor_scalar(
                g9[:, 8:9], nrm, 0.5, None, op0=ALU.mult), waits=P.w(evv), incs=[(evv, 1)])
        gdone = evv.v
        gdones = evs.v
        ctx["_moe_aux_g"] = glv
        for j in range(4):
            P.op("tensor", lambda e, j=j: e.transpose(
                aux[0:9, S + j * 128: S + (j + 1) * 128],
                gw[:, j * 32 + 16: j * 32 + 25], c["ident_f"][:, :]),
                waits=[(evv, gdone), (evs, gdones)],
                incs=[(smm, 1)] if j == 3 else [])
        P.op("vector", lambda e: e.tensor_copy(g9sf[:, :], aux[0:9, S:2 * S]),
             waits=P.w(smm), incs=[(evv, 1)])
        P.op("vector", lambda e: e.tensor_copy(g9sb[:, :], aux[0:8, S:2 * S]),
             incs=[(evv, 1)])
        ctx["_moe_aux2_g"] = evv.v
        g8v = evv.v

        w1re = wb["moe_w1"]
        w2re = wb["moe_w2"]
        for e_ in range(8):
            P.op("tensor", lambda e, e_=e_: e.matmul(
                abp[:, 0:S], esel_sb[:, e_ * 128:(e_ + 1) * 128], g9sf[:, :],
                start=True, stop=True),
                waits=[(evv, g8v), (evv, ln_done), (evv, ctx.get("_gb_guard", 0))],
                incs=[(smm, 1)])
            P.op("vector", lambda e: e.tensor_copy(gbe[:, :], abp[:, 0:S]),
                 waits=P.w(smm), incs=[(evv, 1)])
            ctx["_gb_guard"] = evv.v
            gbw = [(evv, evv.v)]

            def xb_src(k, xh=xh):
                return xh[:, k * S:(k + 1) * S]

            def b1mm(mt, mi, last, e_=e_):
                i = e_ * 2048 + mt * 128
                q, off = divmod(i, 8192)
                P.op("tensor", lambda e, q=q, off=off, mi=mi: e.matmul(
                    ctx["pp"][:, mi * S:(mi + 1) * S],
                    b1_sb[32 * q:32 * q + 1, off:off + 128],
                    c["ones_mat"][32 * q:32 * q + 1, 0:S], start=False, stop=True),
                    waits=mld, incs=[(smm, 1)] if last else [])

            def ev_w1(mt, pap, gv, e_=e_):
                P.op("vector", lambda e, mt=mt, pap=pap: e.scalar_tensor_tensor(
                    hB[:, mt * S:(mt + 1) * S], pap, 0.0, gbe[:, :],
                    op0=ALU.max, op1=ALU.mult),
                    waits=[(smm, gv)] + gbw, incs=[(evv, 1)])

            emit_gemm(ctx, w1re[e_ * D:(e_ + 1) * D, :], NA, 16, xb_src, ev_w1,
                      xnw, extra_mm=b1mm, prep_wait=pd)

            hsrc = lambda k: hB[:, k * S:(k + 1) * S]
            if e_ == 0:
                def ev_w2(mt, pap, gv):
                    P.op("vector", lambda e, mt=mt, pap=pap: e.tensor_copy(
                        acc[:, mt * S:(mt + 1) * S], pap),
                        waits=[(smm, gv)], incs=[(evv, 1)])
                emit_gemm(ctx, w2re[e_ * 2048:(e_ + 1) * 2048, :], 16, NA,
                          hsrc, ev_w2, [(evv, evv.v)], prep_wait=pd)
            else:
                def b2mm(mt, mi, last):
                    P.op("tensor", lambda e, mt=mt, mi=mi: e.matmul(
                        ctx["pp"][:, mi * S:(mi + 1) * S],
                        b2_sb[0:8, mt * 128:(mt + 1) * 128], g9sb[0:8, :],
                        start=False, stop=True),
                        waits=mld + [(evv, g8v)], incs=[(smm, 1)] if last else [])

                def ev_w2(mt, pap, gv):
                    P.op("vector", lambda e, mt=mt, pap=pap: e.tensor_add(
                        acc[:, mt * S:(mt + 1) * S], acc[:, mt * S:(mt + 1) * S],
                        pap), waits=[(smm, gv)], incs=[(evv, 1)])
                emit_gemm(ctx, w2re[e_ * 2048:(e_ + 1) * 2048, :], 16, NA,
                          hsrc, ev_w2, [(evv, evv.v)],
                          extra_mm=(b2mm if e_ == 7 else None), prep_wait=pd)

        P.op("tensor", lambda e: e.matmul(
            abp[:, S:2 * S], esel_sb[:, 8 * 128:9 * 128], g9sf[:, :],
            start=True, stop=True),
            waits=[(evv, g8v), (evv, ln_done)], incs=[(smm, 1)])
        rbv = smm.v
        xog = [(sout, ctx.get("_xo_guard", 0))]
        accv = evv.v
        for a in range(NA):
            P.op("vector", lambda e, a=a: e.tensor_mul(
                xo[:, a * S:(a + 1) * S], acc[:, a * S:(a + 1) * S], abp[:, S:2 * S]),
                waits=[(smm, rbv), (evv, accv)] + xog, incs=[(evv, 1)])
            P.op("vector", lambda e, a=a: e.tensor_add(
                xo[:, a * S:(a + 1) * S], xo[:, a * S:(a + 1) * S],
                xs[:, a * S:(a + 1) * S]), waits=P.w(evv), incs=[(evv, 1)])
        ctx["_xs_guard"] = [(evv, evv.v)]
        ddst = dst.rearrange("(a p) t -> p a t", p=128)[:, :, s * S:(s + 1) * S]
        P.op("sync", lambda e, ddst=ddst: e.dma_start(
            ddst, xo[:, :].rearrange("p (a t) -> p a t", a=NA)),
            waits=P.w(evv), incs=[(sout, 16)])
        ctx["_xo_guard"] = sout.v
        ctx["_xn_guard"] = smm.v
        ctx["_ln_rows_g"] = evv.v


def _a0_selmap(kind):
    out = []
    for pt in range(6):
        heads = [h for h in range(3 * pt, min(3 * pt + 3, 16))]
        by_tile = {}
        for h in heads:
            a = h // 2 if kind == "q" else h // 4
            soff = (h % 2) * 64 if kind == "q" else (h % 4) * 32
            doff = (h - 3 * pt) * 32
            by_tile.setdefault(a, []).append((soff, doff))
        for a, lst in sorted(by_tile.items()):
            out.append((pt, a, lst))
    return out


def emit_attn0(ctx, src, dst):
    nc, P, c, wb, din = ctx["nc"], ctx["P"], ctx["c"], ctx["wb"], ctx["din"]
    sems = ctx["sems"]
    evv, evs, smm, sout = sems["evv"], sems["evs"], sems["smm"], sems["sout"]
    xs, xo, xn, x2b = ctx["xs"], ctx["xo"], ctx["xn"], ctx["x2b"]
    aux, abp, pp = ctx["aux"], ctx["abp"], ctx["pp"]
    pd = [ctx["prep_done"]["attn0"]]
    rdy = ctx.get("_res_ready", (ctx["sres"], 0))
    es2 = ctx["stage_es"]

    def sb2(name, shape, dt=BF16):
        return es2.enter_context(nc.sbuf_tensor(name, list(shape), dt))

    psb = sb2("a0_psb", (128, NA * S))
    kvb = sb2("a0_kvb", (128, S))
    knb = sb2("a0_knb", (128, 6 * T))
    vbuf = sb2("a0_vbuf", (128, 32 * 512))
    qnb = sb2("a0_qnb", (128, 6 * S))
    attn_sb = sb2("a0_attn", (128, 6 * S))
    expA = sb2("a0_expA", (128, 2 * S))
    expB = sb2("a0_expB", (128, 2 * S))
    r16b = sb2("a0_r16b", (16, S))
    qkrow = sb2("a0_qkrow", (16, 2 * S), F32)
    zrow = sb2("a0_zrow", (1, S), F32)
    rbs = sb2("a0_rbs", (128, S))
    psv = sb2("a0_psv", (32, S))
    wkv_sb = sb2("a0_wkv_sb", (128, 8 * 128))
    wk_sb = sb2("a0_wk_sb", (128, 512))
    wv_sb = sb2("a0_wv_sb", (128, 512))
    qsel_sb = sb2("a0_qsel_sb", (128, 11 * 128))
    ksel_sb = sb2("a0_ksel_sb", (128, 8 * 128))
    qones_sb = sb2("a0_qones_sb", (128, 8 * 16))
    kones_sb = sb2("a0_kones_sb", (128, 4 * 16))
    hot_sb = sb2("a0_hot_sb", (16, 768))
    qg_sb = sb2("a0_qg_sb", (128, 1), F32)
    kg_sb = sb2("a0_kg_sb", (128, 1), F32)
    dm_sb = sb2("a0_dm_sb", (128, 4 * S))

    swl = Sem(nc, ctx["es"], "a0_ld")
    loads = [
        (wkv_sb[:, :].rearrange("p (k m) -> p k m", m=128),
         wb["a0_wkv"].rearrange("(k p) m -> p k m", p=128)),
        (wk_sb[:, :], wb["a0_wk"]),
        (wv_sb[:, :], wb["a0_wv"]),
        (qsel_sb[:, :], din["a0_qsel"]),
        (ksel_sb[:, :], din["a0_ksel"]),
        (qones_sb[:, :].rearrange("p (a h) -> p a h", h=16),
         din["a0_qones"].rearrange("(a p) h -> p a h", p=128)),
        (kones_sb[:, :].rearrange("p (a h) -> p a h", h=16),
         din["a0_kones"].rearrange("(a p) h -> p a h", p=128)),
        (hot_sb[:, :], din["a0_hot"]),
        (qg_sb[:, :], din["a0_qg"]),
        (kg_sb[:, :], din["a0_kg"]),
        (dm_sb[:, :], din["dmask"]),
    ]
    for (o, i_) in loads:
        P.op("gpsimd", lambda e, o=o, i_=i_: e.dma_start(o, i_),
             waits=[rdy] + pd, incs=[(swl, 16)])
    wl = [(swl, swl.v)]
    P.op("vector", lambda e: e.memset(attn_sb[:, :], 0.0), waits=[rdy], incs=[(evv, 1)])
    msdone = [(evv, evv.v)]

    qmap = _a0_selmap("q")
    kmap = _a0_selmap("k")
    ctx.setdefault("_xn_guard", 0)

    for s in range(NSTRIP):
        xh = xn[:, :]
        guard = list(ctx.get("_xs_guard", []))
        xnw = emit_ln(ctx, src, s, xh, extra_x_waits=guard,
                      xh_guard=[(smm, ctx["_xn_guard"])])
        ln_done = xnw[0][1]

        def ev_q(mt, pap, gv):
            P.op("scalar", lambda e, mt=mt, pap=pap: e.copy(
                psb[:, mt * S:(mt + 1) * S], pap),
                waits=[(smm, gv)], incs=[(evs, 1)])
        emit_gemm(ctx, wb["a0_wq"], NA, NA, lambda k, xh=xh: xh[:, k * S:(k + 1) * S],
                  ev_q, xnw, prep_wait=pd)
        qcv = evs.v
        for a in range(NA):
            P.op("vector", lambda e, a=a: e.tensor_mul(
                x2b[:, a * S:(a + 1) * S], psb[:, a * S:(a + 1) * S],
                psb[:, a * S:(a + 1) * S]), waits=[(evs, qcv)], incs=[(evv, 1)])
        qsq = evv.v
        for a in range(NA):
            P.op("tensor", lambda e, a=a: e.matmul(
                aux[0:16, 0:S], qones_sb[:, a * 16:(a + 1) * 16],
                x2b[:, a * S:(a + 1) * S],
                start=(a == 0), stop=(a == NA - 1)),
                waits=[(evv, qsq), (evv, ctx.get("_ln_rows_g", 0))] + wl,
                incs=[(smm, 1)] if a == NA - 1 else [])
        qrv = smm.v

        def rows_rms(inv_n, wv):
            P.op("vector", lambda e, inv_n=inv_n: e.tensor_scalar(
                qkrow[:, 0:S], aux[0:16, 0:S], inv_n, 1e-6,
                op0=ALU.mult, op1=ALU.add),
                waits=[(smm, wv)], incs=[(evv, 1)])
            P.op("vector", lambda e: e.reciprocal(qkrow[:, S:2 * S], qkrow[:, 0:S]),
                 waits=P.w(evv), incs=[(evv, 1)])
            P.op("scalar", lambda e: e.activation(qkrow[:, 0:S], qkrow[:, S:2 * S],
                                                  AF.Sqrt), waits=P.w(evv), incs=[(evs, 1)])
            P.op("vector", lambda e: e.tensor_copy(r16b[:, :], qkrow[:, 0:S]),
                 waits=P.w(evs), incs=[(evv, 1)])
            ctx["_ln_rows_g"] = evv.v
            return evv.v

        rqv = rows_rms(1.0 / 64, qrv)

        def compact(selmap, sel_sb, g_ap, dstfn, rrdy):
            qcp = aux[0:128, S:2 * S]
            byp = {}
            for (pt, a, lst) in selmap:
                byp.setdefault(pt, []).append((a, lst))
            bi = 0
            for pt in sorted(byp):
                blocks = byp[pt]
                for i, (a, lst) in enumerate(blocks):
                    P.op("tensor", lambda e, bi=bi, a=a, i=i, n=len(blocks):
                         e.matmul(qcp, sel_sb[:, bi * 128:(bi + 1) * 128],
                                  psb[:, a * S:(a + 1) * S],
                                  start=(i == 0), stop=(i == n - 1)),
                         waits=[(evv, rrdy), (evv, ctx.get("_a0_ptg", 0))],
                         incs=[(smm, 1)] if i == len(blocks) - 1 else [])
                    bi += 1
                scv = smm.v
                P.op("tensor", lambda e, pt=pt: e.matmul(
                    abp[:, 0:S], hot_sb[:, pt * 128:(pt + 1) * 128], r16b[:, :],
                    start=True, stop=True),
                    waits=[(evv, rrdy), (evv, ctx.get("_a0_rbg", 0)), (evv, ln_done)],
                    incs=[(smm, 1)])
                P.op("vector", lambda e: e.tensor_copy(rbs[:, :], abp[:, 0:S]),
                     waits=P.w(smm), incs=[(evv, 1)])
                ctx["_a0_rbg"] = evv.v
                dap = dstfn(pt)
                P.op("vector", lambda e, dap=dap: e.scalar_tensor_tensor(
                    dap, qcp, g_ap, rbs[:, :], op0=ALU.mult, op1=ALU.mult),
                    waits=[(smm, scv)] + P.w(evv), incs=[(evv, 1)])
                ctx["_a0_ptg"] = evv.v

        compact(qmap, qsel_sb, qg_sb[:, 0:1],
                lambda pt: qnb[:, pt * S:(pt + 1) * S], rqv)
        qn_done = evv.v

        for a in range(NA):
            P.op("tensor", lambda e, a=a, xh=xh: e.matmul(
                abp[:, S:2 * S], wkv_sb[:, a * 128:(a + 1) * 128],
                xh[:, a * S:(a + 1) * S], start=(a == 0), stop=(a == NA - 1)),
                waits=[(evv, ln_done)] + wl + xnw,
                incs=[(smm, 1)] if a == NA - 1 else [])
        ctx["_xn_guard"] = smm.v
        P.op("vector", lambda e: e.tensor_copy(kvb[:, :], abp[:, S:2 * S]),
             waits=P.w(smm), incs=[(evv, 1)])
        kvv = evv.v
        for mt in range(4):
            P.op("tensor", lambda e, mt=mt: e.matmul(
                pp[:, mt * S:(mt + 1) * S], wk_sb[:, mt * 128:(mt + 1) * 128],
                kvb[:, :], start=True, stop=True),
                waits=[(evv, kvv)] + list(ctx.get("_gemm_guard", [])),
                incs=[(smm, 1)] if mt == 3 else [])
        kmv = smm.v
        for mt in range(4):
            P.op("scalar", lambda e, mt=mt: e.copy(
                psb[:, mt * S:(mt + 1) * S], pp[:, mt * S:(mt + 1) * S]),
                waits=[(smm, kmv), (evv, qn_done)], incs=[(evs, 1)])
        kcv = evs.v
        for a in range(4):
            P.op("vector", lambda e, a=a: e.tensor_mul(
                x2b[:, a * S:(a + 1) * S], psb[:, a * S:(a + 1) * S],
                psb[:, a * S:(a + 1) * S]), waits=[(evs, kcv)], incs=[(evv, 1)])
        ksq = evv.v
        for a in range(4):
            P.op("tensor", lambda e, a=a: e.matmul(
                aux[0:16, 0:S], kones_sb[:, a * 16:(a + 1) * 16],
                x2b[:, a * S:(a + 1) * S], start=(a == 0), stop=(a == 3)),
                waits=[(evv, ksq)], incs=[(smm, 1)] if a == 3 else [])
        rkv = rows_rms(1.0 / 32, smm.v)
        compact(kmap, ksel_sb, kg_sb[:, 0:1],
                lambda pt: knb[:, pt * T + s * S: pt * T + (s + 1) * S], rkv)
        kn_done = evv.v
        for ts in range(4):
            P.op("tensor", lambda e, ts=ts: e.matmul(
                pp[:, ts * S:(ts + 1) * S], kvb[:, ts * 128:(ts + 1) * 128],
                wv_sb[:, :], start=True, stop=True),
                waits=[(evv, kvv), (evs, kcv)],
                incs=[(smm, 1)] if ts == 3 else [])
        vmv = smm.v
        for ts in range(4):
            vt = 4 * s + ts
            P.op("scalar", lambda e, ts=ts, vt=vt: e.copy(
                vbuf[:, vt * 512:(vt + 1) * 512],
                pp[:, ts * S:(ts + 1) * S]), waits=[(smm, vmv)], incs=[(evs, 1)])
        vcv = evs.v

        ntk = 4 * (s + 1)
        pv = pp[:, 2 * S:3 * S]
        zp = pp[:, 3 * S:4 * S]
        expg = {}
        prev_psv = vcv
        prev_rz = evv.v
        for h in range(16):
            pt, oh = h // 3, (h % 3) * 32
            mmv = {}
            for j in range(ntk + 1):
                if j < ntk:
                    bk = j % 2
                    scp = pp[:, bk * S:(bk + 1) * S]
                    P.op("tensor", lambda e, j=j, scp=scp, pt=pt, oh=oh: e.matmul(
                        scp, knb[oh:oh + 32, pt * T + j * 128: pt * T + (j + 1) * 128],
                        qnb[oh:oh + 32, pt * S:(pt + 1) * S], start=True, stop=True),
                        waits=([(evv, qn_done), (evv, kn_done), (evs, vcv)]
                               if (h == 0 and j == 0) else []) + expg.get(bk, []),
                        incs=[(smm, 1)])
                    mmv[j] = smm.v
                    diag = (j >= 4 * s)
                    P.op("scalar", lambda e, bk=bk, scp=scp: e.activation(
                        expA[:, bk * S:(bk + 1) * S], scp, AF.Exp,
                        bias=-12.0, scale=0.125),
                        waits=[(smm, mmv[j])], incs=[(evs, 1)])
                    expg[bk] = [(evs, evs.v)]
                    if diag:
                        P.op("vector", lambda e, bk=bk, jj=j - 4 * s: e.tensor_mul(
                            expB[:, bk * S:(bk + 1) * S],
                            expA[:, bk * S:(bk + 1) * S],
                            dm_sb[:, jj * S:(jj + 1) * S]),
                            waits=P.w(evs), incs=[(evv, 1)])
                        expg[bk] = expg[bk] + [(evv, evv.v)]
                if j >= 1:
                    jj = j - 1
                    bk = jj % 2
                    eb = expB if jj >= 4 * s else expA
                    st_, sp_ = (jj == 0), (jj == ntk - 1)
                    P.op("tensor", lambda e, jj=jj, bk=bk, eb=eb, h=h,
                         st_=st_, sp_=sp_: e.matmul(
                        pv[0:32, :], vbuf[:, jj * 512 + h * 32: jj * 512 + (h + 1) * 32],
                        eb[:, bk * S:(bk + 1) * S], start=st_, stop=sp_),
                        waits=expg[bk] +
                              ([(evs, prev_psv), (evv, prev_rz)] if jj == 0 else []))
                    P.op("tensor", lambda e, bk=bk, eb=eb, st_=st_, sp_=sp_: e.matmul(
                        zp[0:1, :], c["ones_col_b"][:, :],
                        eb[:, bk * S:(bk + 1) * S], start=st_, stop=sp_),
                        incs=[(smm, 1)] if sp_ else [])
            hv = smm.v
            P.op("scalar", lambda e: e.copy(psv[:, :], pv[0:32, :]),
                 waits=[(smm, hv), (evv, prev_rz)], incs=[(evs, 1)])
            prev_psv = evs.v
            P.op("vector", lambda e: e.reciprocal(zrow[:, :], zp[0:1, :]),
                 waits=[(smm, hv)], incs=[(evv, 1)])
            P.op("tensor", lambda e: e.matmul(
                aux[0:32, S:2 * S], c["ones_row_f"][0:1, 0:32], zrow[:, :],
                start=True, stop=True),
                waits=P.w(evv) + [(evv, ctx.get("_a0_zbg", 0))], incs=[(smm, 1)])
            P.op("vector", lambda e, pt=pt, oh=oh: e.tensor_mul(
                attn_sb[oh:oh + 32, pt * S:(pt + 1) * S], psv[:, :],
                aux[0:32, S:2 * S]),
                waits=P.w(smm) + [(evs, prev_psv)] + msdone, incs=[(evv, 1)])
            ctx["_a0_zbg"] = evv.v
            prev_rz = evv.v
        attn_done = evv.v
        P.op("tensor", None, waits=[(evv, attn_done)])
        ctx["_gemm_guard"] = [(evs, evs.v), (evv, evv.v)]
        xog = [(sout, ctx.get("_xo_guard", 0))]

        def ev_wo(mt, pap, gv):
            P.op("vector", lambda e, mt=mt, pap=pap: e.tensor_add(
                xo[:, mt * S:(mt + 1) * S], xs[:, mt * S:(mt + 1) * S], pap),
                waits=[(smm, gv)] + xog, incs=[(evv, 1)])
        emit_gemm(ctx, wb["a0_wo"], 6, NA,
                  lambda k: attn_sb[:, k * S:(k + 1) * S], ev_wo,
                  [(evv, attn_done)], prep_wait=pd)
        ctx["_xs_guard"] = [(evv, evv.v)]
        ddst = dst.rearrange("(a p) t -> p a t", p=128)[:, :, s * S:(s + 1) * S]
        P.op("sync", lambda e, ddst=ddst: e.dma_start(
            ddst, xo[:, :].rearrange("p (a t) -> p a t", a=NA)),
            waits=P.w(evv), incs=[(sout, 16)])
        ctx["_xo_guard"] = sout.v
        ctx["_ln_rows_g"] = evv.v


def emit_attn1(ctx, src, dst):
    nc, P, c, wb, din = ctx["nc"], ctx["P"], ctx["c"], ctx["wb"], ctx["din"]
    sems = ctx["sems"]
    evv, evs, smm, sout = sems["evv"], sems["evs"], sems["smm"], sems["sout"]
    xs, xo, xn = ctx["xs"], ctx["xo"], ctx["xn"]
    aux, pp = ctx["aux"], ctx["pp"]
    vdram = ctx["vdram"]
    pd = [ctx["prep_done"]["attn1"]]
    rdy = ctx.get("_res_ready", (ctx["sres"], 0))
    es2 = ctx["stage_es"]

    def sb2(name, shape, dt=BF16):
        return es2.enter_context(nc.sbuf_tensor(name, list(shape), dt))

    knb = sb2("a1_knb", (128, 6 * T))
    q1b = sb2("a1_q1b", (128, 6 * S))
    vsb = sb2("a1_vsb", (128, 4 * 1024))
    vh = sb2("a1_vh", (128, 2 * 32 * 64))
    sqA = sb2("a1_sqA", (128, 2 * S))
    Ab = sb2("a1_Ab", (128, 2 * S))
    a1o = sb2("a1_out", (128, NA * S))
    psv = sb2("a1_psv", (64, S))
    zrow = sb2("a1_zrow", (1, 2 * S), F32)
    hm_sb = sb2("a1_hm", (128, 5 * S))

    swl = Sem(nc, ctx["es"], "a1_ld")
    P.op("gpsimd", lambda e: e.dma_start(hm_sb[:, :], din["hmask"]),
         waits=[rdy] + pd, incs=[(swl, 16)])
    wl = [(swl, swl.v)]
    svd = Sem(nc, ctx["es"], "a1_vst")
    vhs = [Sem(nc, ctx["es"], "a1_vh0"), Sem(nc, ctx["es"], "a1_vh1")]
    vh_free = {}

    ctx.setdefault("_xn_guard", 0)
    for s in range(NSTRIP):
        xh = xn[:, :]
        guard = list(ctx.get("_xs_guard", []))
        xnw = emit_ln(ctx, src, s, xh, extra_x_waits=guard,
                      xh_guard=[(smm, ctx["_xn_guard"])])
        ln_done = xnw[0][1]

        def ev_q(mt, pap, gv):
            P.op("scalar", lambda e, mt=mt, pap=pap: e.copy(
                q1b[:, mt * S:(mt + 1) * S], pap),
                waits=[(smm, gv), (smm, ctx.get("_a1_qg", 0))], incs=[(evs, 1)])
        emit_gemm(ctx, wb["a1_wqT"], NA, 6, lambda k, xh=xh: xh[:, k * S:(k + 1) * S],
                  ev_q, xnw, prep_wait=pd)

        def ev_k(mt, pap, gv, s=s):
            P.op("scalar", lambda e, mt=mt, pap=pap, s=s: e.copy(
                knb[:, mt * T + s * S: mt * T + (s + 1) * S], pap),
                waits=[(smm, gv)], incs=[(evs, 1)])
        emit_gemm(ctx, wb["a1_wkT"], NA, 6, lambda k, xh=xh: xh[:, k * S:(k + 1) * S],
                  ev_k, xnw, prep_wait=pd)
        qk_done = evs.v

        ring = ctx["ring"]
        wvre = wb["a1_wvT"].rearrange("(kt p) m -> p kt m", p=128)
        vload = [ring.load(wvre[:, 0:4, :], 4, 1024, extra_waits=pd),
                 ring.load(wvre[:, 4:8, :], 4, 1024, extra_waits=pd)]
        vguard = list(ctx.get("_gemm_guard", []))
        for ts in range(4):
            pbank = pp[:, (ts % 2) * 2 * S:(ts % 2) * 2 * S + 2 * S]
            for nch in range(2):
                for k in range(NA):
                    ap, seq = vload[k // 4]
                    lhs = xh[:, k * S + ts * 128: k * S + (ts + 1) * 128]
                    P.op("tensor", lambda e, lhs=lhs, ap=ap, k=k, nch=nch,
                         pbank=pbank: e.matmul(
                             pbank[:, nch * S:(nch + 1) * S], lhs,
                             ap[:, (k % 4) * 1024 + nch * 512:(k % 4) * 1024 + (nch + 1) * 512],
                             start=(k == 0), stop=(k == NA - 1)),
                         waits=(ring.use_wait(seq) + (xnw + vguard +
                                [(evs, ctx.get("_a1_vg", {}).get(ts % 2, 0))]
                                if (k == 0 and nch == 0) else [])),
                         incs=[(smm, 1)] if (k == NA - 1 and nch == 1) else [])
            gv = smm.v
            P.op("scalar", lambda e, ts=ts, pbank=pbank: e.copy(
                vsb[:, ts * 1024:(ts + 1) * 1024], pbank[:, :]),
                waits=[(smm, gv), (svd, ctx.get("_a1_vsb_g", 0))], incs=[(evs, 1)])
            ctx.setdefault("_a1_vg", {})[ts % 2] = evs.v
        for (_, seq) in vload:
            ring.freed_by[seq] = (smm, smm.v)
        vdone = evs.v
        vdst = vdram[s * 512:(s + 1) * 512, :].rearrange("(ts p) c -> p ts c", p=128)
        P.op("gpsimd", lambda e, vdst=vdst: e.dma_start(
            vdst, vsb[:, :].rearrange("p (ts c) -> p ts c", c=1024)),
            waits=[(evs, vdone), (svd, svd.v)], incs=[(svd, 16)])
        svv = svd.v
        ctx["_a1_vg"] = {0: vdone, 1: vdone}
        ctx["_a1_vsb_g"] = svv

        ntk = 4 * (s + 1)
        pv = pp[:, 2 * S:3 * S]
        zp = pp[:, 3 * S:4 * S]
        sqg = {}
        prev_psv = evs.v
        prev_rz = evv.v
        for h in range(16):
            pt, oh = h // 3, (h % 3) * 32
            ot, oo = h // 2, (h % 2) * 64
            slot = h % 2
            vsrc = vdram[0:ntk * 128, h * 64:(h + 1) * 64].rearrange(
                "(j p) c -> p j c", p=128)
            P.op("gpsimd", lambda e, vsrc=vsrc, slot=slot, ntk=ntk: e.dma_start(
                vh[:, slot * 2048:slot * 2048 + ntk * 64].rearrange(
                    "p (j c) -> p j c", c=64), vsrc),
                waits=[(svd, svv)] + ([vh_free[slot]] if slot in vh_free else []),
                incs=[(vhs[slot], 16)])
            vhw = [(vhs[slot], vhs[slot].v)]
            mmv = {}
            for j in range(ntk + 1):
                if j < ntk:
                    bk = j % 2
                    scp = pp[:, bk * S:(bk + 1) * S]
                    P.op("tensor", lambda e, j=j, scp=scp, pt=pt, oh=oh: e.matmul(
                        scp, knb[oh:oh + 16, pt * T + j * 128: pt * T + (j + 1) * 128],
                        q1b[oh:oh + 16, pt * S:(pt + 1) * S], start=True, stop=True),
                        waits=([(evs, vdone)] if (h == 0 and j == 0) else []) +
                              sqg.get(bk, []),
                        incs=[(smm, 1)])
                    mmv[j] = smm.v
                    P.op("scalar", lambda e, bk=bk, scp=scp: e.activation(
                        sqA[:, bk * S:(bk + 1) * S], scp, AF.Square,
                        bias=1.0, scale=0.25),
                        waits=[(smm, mmv[j])], incs=[(evs, 1)])
                    sqg[bk] = [(evs, evs.v)]
                    hmof = (j - 4 * s) if j >= 4 * s else 4
                    P.op("vector", lambda e, bk=bk, hmof=hmof: e.scalar_tensor_tensor(
                        Ab[:, bk * S:(bk + 1) * S], sqA[:, bk * S:(bk + 1) * S],
                        1.0, hm_sb[:, hmof * S:(hmof + 1) * S],
                        op0=ALU.add, op1=ALU.mult),
                        waits=sqg[bk] + wl, incs=[(evv, 1)])
                    sqg[bk] = sqg[bk] + [(evv, evv.v)]
                if j >= 1:
                    jj = j - 1
                    bk = jj % 2
                    st_, sp_ = (jj == 0), (jj == ntk - 1)
                    P.op("tensor", lambda e, jj=jj, bk=bk, slot=slot,
                         st_=st_, sp_=sp_: e.matmul(
                        pv[0:64, :], vh[:, slot * 2048 + jj * 64: slot * 2048 + (jj + 1) * 64],
                        Ab[:, bk * S:(bk + 1) * S], start=st_, stop=sp_),
                        waits=sqg[bk] + vhw +
                              ([(evs, prev_psv), (evv, prev_rz)] if st_ else []))
                    P.op("tensor", lambda e, bk=bk, st_=st_, sp_=sp_: e.matmul(
                        zp[0:1, :], c["ones_col_b"][:, :],
                        Ab[:, bk * S:(bk + 1) * S], start=st_, stop=sp_),
                        incs=[(smm, 1)] if sp_ else [])
            hv = smm.v
            vh_free[slot] = (smm, hv)
            P.op("scalar", lambda e: e.copy(psv[:, :], pv[0:64, :]),
                 waits=[(smm, hv), (evv, prev_rz)], incs=[(evs, 1)])
            prev_psv = evs.v
            P.op("vector", lambda e: e.tensor_scalar(
                zrow[0:1, 0:S], zp[0:1, :], 1.0, 1e-12, op0=ALU.mult, op1=ALU.add),
                waits=[(smm, hv)], incs=[(evv, 1)])
            P.op("vector", lambda e: e.reciprocal(zrow[0:1, S:2 * S], zrow[0:1, 0:S]),
                 waits=P.w(evv), incs=[(evv, 1)])
            P.op("tensor", lambda e: e.matmul(
                aux[0:64, S:2 * S], c["ones_row_f"][0:1, 0:64], zrow[0:1, S:2 * S],
                start=True, stop=True),
                waits=P.w(evv) + [(evv, ctx.get("_a1_zbg", 0))], incs=[(smm, 1)])
            P.op("vector", lambda e, ot=ot, oo=oo: e.tensor_mul(
                a1o[oo:oo + 64, ot * S:(ot + 1) * S], psv[:, :],
                aux[0:64, S:2 * S]),
                waits=P.w(smm) + [(evs, prev_psv)], incs=[(evv, 1)])
            ctx["_a1_zbg"] = evv.v
            prev_rz = evv.v
        attn_done = evv.v
        P.op("tensor", None, waits=[(evv, attn_done)])
        ctx["_gemm_guard"] = [(evs, evs.v), (evv, evv.v)]
        xog = [(sout, ctx.get("_xo_guard", 0))]

        def ev_wo(mt, pap, gv):
            P.op("vector", lambda e, mt=mt, pap=pap: e.tensor_add(
                xo[:, mt * S:(mt + 1) * S], xs[:, mt * S:(mt + 1) * S], pap),
                waits=[(smm, gv)] + xog, incs=[(evv, 1)])
        emit_gemm(ctx, wb["a1_woT"], NA, NA,
                  lambda k: a1o[:, k * S:(k + 1) * S], ev_wo,
                  [(evv, attn_done)], prep_wait=pd)
        ctx["_xs_guard"] = [(evv, evv.v)]
        ctx["_a1_qg"] = smm.v
        ddst = dst.rearrange("(a p) t -> p a t", p=128)[:, :, s * S:(s + 1) * S]
        P.op("sync", lambda e, ddst=ddst: e.dma_start(
            ddst, xo[:, :].rearrange("p (a t) -> p a t", a=NA)),
            waits=P.w(evv), incs=[(sout, 16)])
        ctx["_xo_guard"] = sout.v
        ctx["_ln_rows_g"] = evv.v


def make_in_map(inputs, core, stages=ALL_STAGES):
    f = np.ascontiguousarray
    m = {}
    m["xT"] = f(np.asarray(inputs["L0_attn_x"][core]).T)
    m["cconst"] = np.tile(np.array(REG_CONSTS, dtype=np.float32), (128, 1))
    m["ones_row"] = np.ones((1, 128 + S), dtype=np.float32)
    m["identc"] = np.eye(128, dtype=np.float32)
    need = set(st[:4] if st.startswith("ffn") else st for st in stages)
    if "ffn0" in need:
        m["gate"] = f(np.asarray(inputs["L0_ffn_gate"], dtype=np.float32))
        esel = np.zeros((9, 9 * 128), dtype=np.float32)
        for e in range(9):
            esel[e, e * 128:(e + 1) * 128] = 1.0
        m["esel"] = esel
        m["moe_w1"] = f(np.asarray(inputs["L0_ffn_W1"], dtype=np.float32))
        m["moe_b1"] = f(np.asarray(inputs["L0_ffn_b1"], dtype=np.float32).reshape(1, -1))
        m["moe_w2"] = f(np.asarray(inputs["L0_ffn_W2"], dtype=np.float32))
        m["moe_b2"] = f(np.asarray(inputs["L0_ffn_b2"], dtype=np.float32))
    if "attn0" in need:
        m["a0_wq"] = f(np.asarray(inputs["L0_attn_Wq"], dtype=np.float32))
        m["a0_wkv"] = f(np.asarray(inputs["L0_attn_Wkv"], dtype=np.float32))
        m["a0_wk"] = f(np.asarray(inputs["L0_attn_Wk"], dtype=np.float32))
        m["a0_wv"] = f(np.asarray(inputs["L0_attn_Wv"], dtype=np.float32))
        wo = np.asarray(inputs["L0_attn_Wo"], dtype=np.float32)
        wop = np.zeros((768, D), dtype=np.float32)
        for h in range(16):
            pt, off = h // 3, (h % 3) * 32
            wop[pt * 128 + off: pt * 128 + off + 32, :] = wo[h * 64: h * 64 + 32, :]
        m["a0_wo"] = wop
        m["a0_qg"] = f(np.tile(np.asarray(inputs["L0_attn_q_g"], dtype=np.float32)[:32], 4)[:, None])
        m["a0_kg"] = f(np.tile(np.asarray(inputs["L0_attn_k_g"], dtype=np.float32), 4)[:, None])
        qo = np.zeros((D, 16), dtype=np.float32)
        for h in range(16):
            qo[h * 64:(h + 1) * 64, h] = 1.0
        m["a0_qones"] = qo
        ko = np.zeros((512, 16), dtype=np.float32)
        for h in range(16):
            ko[h * 32:(h + 1) * 32, h] = 1.0
        m["a0_kones"] = ko
        for kind, nb in (("q", 11), ("k", 8)):
            sel = np.zeros((128, nb * 128), dtype=np.float32)
            for bi, (pt, a, lst) in enumerate(_a0_selmap(kind)):
                for (soff, doff) in lst:
                    for r in range(32):
                        sel[soff + r, bi * 128 + doff + r] = 1.0
            m["a0_" + kind + "sel"] = sel
        hot = np.zeros((16, 768), dtype=np.float32)
        for h in range(16):
            pt, off = h // 3, (h % 3) * 32
            hot[h, pt * 128 + off: pt * 128 + off + 32] = 1.0
        m["a0_hot"] = hot
        dm = np.zeros((128, 4 * S), dtype=np.float32)
        for j in range(4):
            tk = j * 128 + np.arange(128)[:, None]
            tq = np.arange(S)[None, :]
            dm[:, j * S:(j + 1) * S] = (tk <= tq).astype(np.float32)
        m["dmask"] = dm
    if "attn1" in need:
        def pad_a1(w):
            wt = np.asarray(w, dtype=np.float32).T
            out = np.zeros((D, 768), dtype=np.float32)
            for h in range(16):
                pt, off = h // 3, (h % 3) * 32
                out[:, pt * 128 + off: pt * 128 + off + 16] = wt[:, h * 16:(h + 1) * 16]
            return out
        m["a1_wqT"] = pad_a1(inputs["L1_attn_Wq"])
        m["a1_wkT"] = pad_a1(inputs["L1_attn_Wk"])
        m["a1_wvT"] = f(np.asarray(inputs["L1_attn_Wv"], dtype=np.float32).T)
        m["a1_woT"] = f(np.asarray(inputs["L1_attn_Wo"], dtype=np.float32).T)
        hm = np.zeros((128, 5 * S), dtype=np.float32)
        for j in range(4):
            tk = j * 128 + np.arange(128)[:, None]
            tq = np.arange(S)[None, :]
            hm[:, j * S:(j + 1) * S] = 0.5 * (tk <= tq)
        hm[:, 4 * S:] = 0.5
        m["hmask"] = hm
    if "ffn1" in need:
        m["f1_w1T"] = f(np.asarray(inputs["L1_ffn_w1_w"], dtype=np.float32).T)
        m["f1_w2T"] = f(np.asarray(inputs["L1_ffn_w2_w"], dtype=np.float32).T)
        m["f1_h1T"] = f(np.asarray(inputs["L1_ffn_h1_w"], dtype=np.float32).T)
        m["f1_h2T"] = f(np.asarray(inputs["L1_ffn_h2_w"], dtype=np.float32).T)
        m["f1_w3T"] = f(np.asarray(inputs["L1_ffn_w3_w"], dtype=np.float32).T)
        m["f1_s1"] = f(np.asarray(inputs["L1_ffn_w1_s"], dtype=np.float32).reshape(32, 128).T)
        m["f1_s2"] = f(np.asarray(inputs["L1_ffn_w2_s"], dtype=np.float32).reshape(32, 128).T)
        m["f1_sh1"] = f(np.asarray(inputs["L1_ffn_h1_s"], dtype=np.float32).reshape(32, 128).T)
        m["f1_sh2"] = f(np.asarray(inputs["L1_ffn_h2_s"], dtype=np.float32).reshape(32, 128).T)
        m["f1_s3"] = f(np.asarray(inputs["L1_ffn_w3_s"], dtype=np.float32).reshape(8, 128).T)
    return m


def kernel(**inputs):
    nc = build_nc(ALL_STAGES)
    in_maps = [make_in_map(inputs, cid) for cid in range(8)]
    res = run_bass_kernel_spmd(nc, in_maps, core_ids=list(range(8)))
    out = np.stack([np.asarray(r["outT"]).T for r in res.results], axis=0)
    return out.astype(np.float32)


# version 02
# speedup vs baseline: 1.0561x; 1.0561x over previous
import numpy as np
import concourse.bass as bass
import concourse.mybir as mybir
from concourse.bass_utils import run_bass_kernel_spmd
from contextlib import ExitStack

F32 = mybir.dt.float32
BF16 = mybir.dt.bfloat16
AF = mybir.ActivationFunctionType
ALU = mybir.AluOpType
AXX = mybir.AxisListType.X

T = 4096
D = 1024
S = 512
NSTRIP = T // S
NA = D // 128

ALL_STAGES = ("ffn0a", "attn0", "ffn0b", "ffn1a", "attn1", "ffn1b")

REG_CONSTS = [-12.0, 0.125, 1.0, 0.25, 0.5]


class Sem:
    __slots__ = ("h", "v", "name")

    def __init__(self, nc, es, name):
        self.h = es.enter_context(nc.semaphore(name))
        self.v = 0
        self.name = name


class Sched:

    def __init__(self):
        self.ops = []

    def op(self, eng, fn=None, waits=(), incs=()):
        rincs = []
        for (sem, k) in incs:
            sem.v += k
            rincs.append((sem, k))
        self.ops.append((eng, fn, [(s, int(v)) for (s, v) in waits], rincs))

    def w(self, *sems):
        return [(s, s.v) for s in sems]

    def emit(self, eng_name, eng):
        seen = {}
        for e, fn, waits, incs in self.ops:
            if e != eng_name:
                continue
            for (sem, val) in waits:
                if val > 0 and seen.get(sem.name, -1) < val:
                    eng.wait_ge(sem.h, val)
                    seen[sem.name] = val
            inst = fn(eng) if fn is not None else None
            if incs:
                assert inst is not None and len(incs) == 1, \
                    "each sem inc must ride a real instruction"
                inst.then_inc(incs[0][0].h, incs[0][1])

    def emit_all(self, block):
        for name in ("sync", "tensor", "scalar", "vector", "gpsimd"):
            getattr(block, name)(lambda eng, n=name: self.emit(n, eng))


class WRing:

    def __init__(self, nc, es, P, name, nslots, slot_cols):
        self.sb = es.enter_context(nc.sbuf_tensor(name, [128, nslots * slot_cols], BF16))
        self.nslots = nslots
        self.cols = slot_cols
        self.P = P
        self.sw = [Sem(nc, es, f"{name}_w{i}") for i in range(nslots)]
        self.freed_by = {}
        self.seq = 0

    def load(self, dram_ap, kcnt, mcols, extra_waits=()):
        ncols = kcnt * mcols
        assert ncols <= self.cols
        seq = self.seq
        self.seq += 1
        slot = seq % self.nslots
        ap = self.sb[:, slot * self.cols: slot * self.cols + ncols]
        out = ap.rearrange("p (kt m) -> p kt m", m=mcols)
        waits = list(extra_waits)
        if seq >= self.nslots:
            waits.append(self.freed_by.pop(seq - self.nslots))
        self.P.op("sync", lambda e, out=out, d=dram_ap: e.dma_start(out, d),
                  waits=waits, incs=[(self.sw[slot], 16)])
        return ap, seq

    def use_wait(self, seq):
        return [(self.sw[seq % self.nslots], 16 * (seq // self.nslots + 1))]


def build_nc(stages=ALL_STAGES):
    nc = bass.Bass("TRN2", target_bir_lowering=False, debug=False)
    es = ExitStack()
    P = Sched()

    din = {}

    def dri(name, shape, dt=F32):
        din[name] = nc.dram_tensor(name, list(shape), dt, kind="ExternalInput").ap()
        return din[name]

    xT = dri("xT", (D, T))
    dri("cconst", (128, len(REG_CONSTS)))
    dri("ones_row", (1, 128 + S))
    dri("identc", (128, 128))

    need = set()
    for st in stages:
        need.add(st[:4] if st.startswith("ffn") else st)
    if "ffn0" in need:
        dri("gate", (D, 8))
        dri("esel", (9, 9 * 128))
        dri("moe_w1", (8, D, 2048))
        dri("moe_b1", (1, 8 * 2048))
        dri("moe_w2", (8, 2048, D))
        dri("moe_b2", (8, D))
    if "attn0" in need:
        dri("a0_wq", (D, D))
        dri("a0_wkv", (D, 128))
        dri("a0_wk", (128, 512))
        dri("a0_wv", (128, 512))
        dri("a0_wo", (768, D))
        dri("a0_qg", (128, 1))
        dri("a0_kg", (128, 1))
        dri("a0_qones", (D, 16))
        dri("a0_kones", (512, 16))
        dri("a0_qsel", (128, 11 * 128))
        dri("a0_ksel", (128, 8 * 128))
        dri("a0_hot", (16, 768))
        dri("dmask", (128, 4 * S))
    if "attn1" in need:
        dri("a1_wqT", (D, 768))
        dri("a1_wkT", (D, 768))
        dri("a1_wvT", (D, D))
        dri("a1_woT", (D, D))
        dri("hmask", (128, 5 * S))
    if "ffn1" in need:
        dri("f1_w1T", (D, 4096))
        dri("f1_w2T", (D, 4096))
        dri("f1_h1T", (4096, 4096))
        dri("f1_h2T", (4096, 4096))
        dri("f1_w3T", (4096, D))
        dri("f1_s1", (128, 32))
        dri("f1_s2", (128, 32))
        dri("f1_sh1", (128, 32))
        dri("f1_sh2", (128, 32))
        dri("f1_s3", (128, 8))

    xres = nc.dram_tensor("xres", [D, T], F32, kind="Internal").ap()
    outT = nc.dram_tensor("outT", [D, T], F32, kind="ExternalOutput").ap()

    wb = {}

    def drw(name, shape):
        wb[name] = nc.dram_tensor(name + "_b", list(shape), BF16, kind="Internal").ap()
        return wb[name]

    if "ffn0" in need:
        drw("gate", (D, 8))
        drw("moe_w1", (8 * D, 2048))
        drw("moe_w2", (8 * 2048, D))
    if "attn0" in need:
        for n, sh in [("a0_wq", (D, D)), ("a0_wkv", (D, 128)), ("a0_wk", (128, 512)),
                      ("a0_wv", (128, 512)), ("a0_wo", (768, D))]:
            drw(n, sh)
    if "attn1" in need:
        for n, sh in [("a1_wqT", (D, 768)), ("a1_wkT", (D, 768)),
                      ("a1_wvT", (D, D)), ("a1_woT", (D, D))]:
            drw(n, sh)
        ctx_vdram = nc.dram_tensor("vdram", [T, D], BF16, kind="Internal").ap()
    if "ffn1" in need:
        for n, sh in [("f1_w1T", (D, 4096)), ("f1_w2T", (D, 4096)),
                      ("f1_h1T", (4096, 4096)), ("f1_h2T", (4096, 4096)),
                      ("f1_w3T", (4096, D))]:
            drw(n, sh)

    def sb(name, shape, dt=F32):
        return es.enter_context(nc.sbuf_tensor(name, list(shape), dt))

    def psum(name, shape, dt=F32):
        return es.enter_context(nc.psum_tensor(name, list(shape), dt))

    c = {}
    c["cc"] = sb("cc", (128, len(REG_CONSTS)))
    for i, v in enumerate(REG_CONSTS):
        nc.const_aps.register(F32, v, c["cc"][:, i:i + 1])
    c["ones_col_f"] = sb("ones_col_f", (128, 1))
    c["ones_col_b"] = sb("ones_col_b", (128, 1), BF16)
    c["ones_row_f"] = sb("ones_row_f", (1, 128))
    c["ones_row_b"] = sb("ones_row_b", (1, S), BF16)
    c["ident_f"] = sb("ident_f", (128, 128))
    c["ones_mat"] = sb("ones_mat", (128, S), BF16)

    sboot = Sem(nc, es, "boot")
    sboot2 = Sem(nc, es, "boot2")

    P.op("gpsimd", lambda e: e.dma_start(c["cc"][:, :], din["cconst"]), incs=[(sboot, 16)])
    orow = din["ones_row"]
    P.op("gpsimd", lambda e: e.dma_start(c["ones_col_f"][:, :],
                                         orow[0:1, 0:128].rearrange("a p -> p a")), incs=[(sboot, 16)])
    P.op("gpsimd", lambda e: e.dma_start(c["ones_col_b"][:, :],
                                         orow[0:1, 0:128].rearrange("a p -> p a")), incs=[(sboot, 16)])
    P.op("gpsimd", lambda e: e.dma_start(c["ones_row_f"][:, :], orow[0:1, 0:128]), incs=[(sboot, 16)])
    P.op("gpsimd", lambda e: e.dma_start(c["ones_row_b"][:, :], orow[0:1, 128:128 + S]), incs=[(sboot, 16)])
    P.op("gpsimd", lambda e: e.dma_start(c["ident_f"][:, :], din["identc"]), incs=[(sboot, 16)])
    P.op("gpsimd", lambda e: e.memset(c["ones_mat"][:, :], 1.0), incs=[(sboot2, 1)])
    boot_n = sboot.v
    boot2_n = sboot2.v

    sres = Sem(nc, es, "resinit")
    P.op("gpsimd", lambda e: e.dma_start(xres, xT), incs=[(sres, 16)])

    prep_done = {}
    order = []
    for st in stages:
        key = st[:4] if st.startswith("ffn") else st
        if key not in order:
            order.append(key)
    group = {"ffn0": ["gate", "moe_w1", "moe_w2"],
             "attn0": ["a0_wq", "a0_wkv", "a0_wk", "a0_wv", "a0_wo"],
             "attn1": ["a1_wqT", "a1_wkT", "a1_wvT", "a1_woT"],
             "ffn1": ["f1_w1T", "f1_w2T", "f1_h1T", "f1_h2T", "f1_w3T"]}
    for key in order:
        sp = Sem(nc, es, "prep_" + key)
        for n in group[key]:
            src = din[n]
            if len(src.shape) == 3:
                src = src.rearrange("e k m -> (e k) m")
            P.op("gpsimd", lambda e, n=n, src=src: e.dma_start(wb[n], src),
                 incs=[(sp, 16)])
        prep_done[key] = (sp, sp.v)

    for e in ("tensor", "scalar", "vector", "sync"):
        P.op(e, None, waits=[(sboot, boot_n), (sboot2, boot2_n)])

    xs = sb("xs", (128, NA * S))
    x2b = sb("x2b", (128, NA * S), BF16)
    xn = sb("xn", (128, NA * S), BF16)
    xo = sb("xo", (128, NA * S))
    t1 = xo
    rws = sb("rws", (1, 4 * S))
    paux = psum("paux", (128, 2 * S))
    abp = psum("abp", (128, 2 * S))
    pp = psum("pp", (128, 4 * S))
    stpA = paux[0:1, 0:S]
    stpB = paux[0:1, S:2 * S]

    ring = WRing(nc, es, P, "wring", 2, 4096)

    sems = {n: Sem(nc, es, n) for n in ("sx", "evv", "evs", "smm", "sout")}

    ctx = dict(nc=nc, P=P, c=c, din=din, wb=wb, xres=xres, outT=outT,
               vdram=(ctx_vdram if "attn1" in need else None),
               xs=xs, x2b=x2b, xn=xn, t1=t1, xo=xo, rws=rws,
               stpA=stpA, stpB=stpB, abp=abp, pp=pp, aux=paux, ring=ring,
               sems=sems, sres=sres, prep_done=prep_done, es=es)

    with es:
        block = es.enter_context(nc.Block())
        P.emit_all(block)
        for si, st in enumerate(stages):
            is_last = (si == len(stages) - 1)
            src = xT if si == 0 else xres
            dst = outT if is_last else xres
            P = Sched()
            ctx["P"] = P
            ring.P = P
            with ExitStack() as stage_es:
                ctx["stage_es"] = stage_es
                if st in ("ffn1a", "ffn1b"):
                    emit_ffn1(ctx, src, dst)
                elif st in ("ffn0a", "ffn0b"):
                    emit_ffn0(ctx, src, dst)
                elif st == "attn0":
                    emit_attn0(ctx, src, dst)
                elif st == "attn1":
                    emit_attn1(ctx, src, dst)
                else:
                    raise ValueError(st)
                ctx["_res_ready"] = (sems["sout"], sems["sout"].v)
                P.emit_all(block)

        P = Sched()
        P.op("sync", None, waits=P.w(sems["sout"]))
        P.emit_all(block)
    return nc


def emit_ln(ctx, srcdram, s, xh, extra_x_waits=(), xh_guard=()):
    P, c = ctx["P"], ctx["c"]
    xs, x2b, t1, rws = ctx["xs"], ctx["x2b"], ctx["t1"], ctx["rws"]
    stpA, stpB, abp = ctx["stpA"], ctx["stpB"], ctx["abp"]
    sems = ctx["sems"]
    sx, evv, evs, smm = sems["sx"], sems["evv"], sems["evs"], sems["smm"]
    sres = ctx["sres"]

    src = srcdram.rearrange("(a p) t -> p a t", p=128)[:, :, s * S:(s + 1) * S]
    P.op("gpsimd", lambda e: e.dma_start(
        xs[:, :].rearrange("p (a t) -> p a t", a=NA), src),
        waits=[(sres, 16), ctx.get("_res_ready", (sres, 0))] + list(extra_x_waits),
        incs=[(sx, 16)])
    xw = [(sx, sx.v)]

    for a in range(NA):
        P.op("scalar", lambda e, a=a: e.activation(
            x2b[:, a * S:(a + 1) * S], xs[:, a * S:(a + 1) * S], AF.Square),
            waits=xw + [(smm, ctx.get("_ln_x2b_g", 0))], incs=[(evs, 1)])
    sqv = evs.v
    rowg = [(evv, ctx.get("_ln_rows_g", 0))]
    for a in range(NA):
        P.op("tensor", lambda e, a=a: e.matmul(
            stpA[0:1, :], c["ones_col_f"][:, :], xs[:, a * S:(a + 1) * S],
            start=(a == 0), stop=(a == NA - 1)),
            waits=xw + rowg, incs=[(smm, 1)] if a == NA - 1 else [])
    for a in range(NA):
        P.op("tensor", lambda e, a=a: e.matmul(
            stpB[0:1, :], c["ones_col_b"][:, :], x2b[:, a * S:(a + 1) * S],
            start=(a == 0), stop=(a == NA - 1)),
            waits=[(evs, sqv)] + rowg, incs=[(smm, 1)] if a == NA - 1 else [])
    mst = smm.v
    ctx["_ln_x2b_g"] = mst

    m = rws[0:1, 0:S]; arow = rws[0:1, S:2 * S]
    brow = rws[0:1, 2 * S:3 * S]; trow = rws[0:1, 3 * S:4 * S]
    P.op("vector", lambda e: e.tensor_scalar(m, stpA[0:1, :], 1.0 / D, None,
                                             op0=ALU.mult),
         waits=[(smm, mst)], incs=[(evv, 1)])
    P.op("vector", lambda e: e.tensor_scalar(trow, stpB[0:1, :], 1.0 / D, 1e-5,
                                             op0=ALU.mult, op1=ALU.add),
         incs=[(evv, 1)])
    ctx["_ln_rows_g"] = evv.v
    P.op("vector", lambda e: e.scalar_tensor_tensor(
        brow, m, -1.0, m, op0=ALU.mult, op1=ALU.mult),
        waits=P.w(evv), incs=[(evv, 1)])
    P.op("vector", lambda e: e.tensor_add(trow, trow, brow),
         waits=P.w(evv), incs=[(evv, 1)])
    P.op("vector", lambda e: e.reciprocal(brow, trow),
         waits=P.w(evv), incs=[(evv, 1)])
    P.op("scalar", lambda e: e.activation(arow, brow, AF.Sqrt),
         waits=P.w(evv), incs=[(evs, 1)])
    P.op("vector", lambda e: e.scalar_tensor_tensor(
        brow, m, -1.0, arow, op0=ALU.mult, op1=ALU.mult),
        waits=P.w(evs), incs=[(evv, 1)])
    evrow = evv.v

    P.op("tensor", lambda e: e.matmul(abp[:, 0:S], c["ones_row_f"][:, :], arow,
                                      start=True, stop=True),
         waits=[(evv, evrow)])
    P.op("tensor", lambda e: e.matmul(abp[:, S:2 * S], c["ones_row_f"][:, :], brow,
                                      start=True, stop=True), incs=[(smm, 1)])
    mbc = smm.v
    xog = [(ctx["sems"]["sout"], ctx.get("_xo_guard", 0))]
    for a in range(NA):
        P.op("vector", lambda e, a=a: e.tensor_mul(
            t1[:, a * S:(a + 1) * S], xs[:, a * S:(a + 1) * S], abp[:, 0:S]),
            waits=[(smm, mbc)] + list(xh_guard) + xog, incs=[(evv, 1)])
        P.op("vector", lambda e, a=a: e.tensor_add(
            xh[:, a * S:(a + 1) * S], t1[:, a * S:(a + 1) * S], abp[:, S:2 * S]),
            waits=P.w(evv), incs=[(evv, 1)])
    ln_done = evv.v
    return [(evv, ln_done)]


def emit_gemm(ctx, wdram, nk, nm, src_fn, evac, src_waits, extra_mm=None,
              prep_wait=()):
    P = ctx["P"]
    ring, pp = ctx["ring"], ctx["pp"]
    sems = ctx["sems"]
    smm, evv, evs = sems["smm"], sems["evv"], sems["evs"]
    wre = wdram.rearrange("(kt p) m -> p kt m", p=128)
    kb = 8
    pg = {0: list(ctx.get("_gemm_guard", [])), 1: list(ctx.get("_gemm_guard", []))}
    blk = 0
    for mb in range(0, nm, 2):
        par = blk % 2
        blk += 1
        mcnt = min(2, nm - mb)
        mcols = mcnt * 128
        batches = []
        for k0 in range(0, nk, kb):
            kcnt = min(kb, nk - k0)
            batches.append((k0, kcnt))

        def do_load(i):
            k0, kcnt = batches[i]
            dap = wre[:, k0:k0 + kcnt, mb * 128: mb * 128 + mcols]
            return ring.load(dap, kcnt, mcols, extra_waits=prep_wait)

        loaded = {0: do_load(0)}
        if len(batches) > 1:
            loaded[1] = do_load(1)
        guard = pg[par]
        first = True
        for i, (k0, kcnt) in enumerate(batches):
            ap, seq = loaded.pop(i)
            for ki in range(kcnt):
                k = k0 + ki
                for mi in range(mcnt):
                    lhs = ap[:, ki * mcols + mi * 128: ki * mcols + (mi + 1) * 128]
                    last_of_slot = (ki == kcnt - 1 and mi == mcnt - 1)
                    stop = (k == nk - 1 and extra_mm is None)
                    bank = 2 * par + mi
                    P.op("tensor", lambda e, lhs=lhs, k=k, bank=bank, stop=stop:
                         e.matmul(pp[:, bank * S:(bank + 1) * S], lhs, src_fn(k),
                                  start=(k == 0), stop=stop),
                         waits=(ring.use_wait(seq) +
                                (src_waits + guard if first else [])),
                         incs=[(smm, 1)] if last_of_slot else [])
                    first = False
            ring.freed_by[seq] = (smm, smm.v)
            if i + 2 < len(batches):
                loaded[i + 2] = do_load(i + 2)
        if extra_mm is not None:
            for mi in range(mcnt):
                bank = 2 * par + mi
                extra_mm(mb + mi, pp[:, bank * S:(bank + 1) * S],
                         last=(mi == mcnt - 1))
        gv = smm.v
        for mi in range(mcnt):
            bank = 2 * par + mi
            evac(mb + mi, pp[:, bank * S:(bank + 1) * S], gv)
        pg[par] = [(evv, evv.v), (evs, evs.v)]
    ctx["_gemm_guard"] = [(evv, evv.v), (evs, evs.v)]
    return


def emit_ffn1(ctx, src, dst):
    P, wb, din = ctx["P"], ctx["wb"], ctx["din"]
    sems = ctx["sems"]
    evv, evs, smm, sout = sems["evv"], sems["evs"], sems["smm"], sems["sout"]
    xs, xo, xn = ctx["xs"], ctx["xo"], ctx["xn"]
    pd = [ctx["prep_done"]["ffn1"]]
    tag = str(smm.v)
    hA = ctx["stage_es"].enter_context(
        ctx["nc"].sbuf_tensor("hA" + tag, [128, 32 * S], BF16))
    hB = ctx["stage_es"].enter_context(
        ctx["nc"].sbuf_tensor("hB" + tag, [128, 32 * S], BF16))

    rdy = ctx.get("_res_ready", (ctx["sres"], 0))
    sc = ctx["stage_es"].enter_context(
        ctx["nc"].sbuf_tensor("f1sc" + tag, [128, 32 * 4 + 8 + 8], F32))
    sld = Sem(ctx["nc"], ctx["es"], "f1scl" + tag)
    for i, n in enumerate(("f1_s1", "f1_s2", "f1_sh1", "f1_sh2")):
        P.op("gpsimd", lambda e, i=i, n=n: e.dma_start(sc[:, i * 32:(i + 1) * 32], din[n]),
             waits=[rdy], incs=[(sld, 16)])
    P.op("gpsimd", lambda e: e.dma_start(sc[:, 128:136], din["f1_s3"]), incs=[(sld, 16)])
    P.op("vector", lambda e: e.tensor_scalar(sc[:, 136:144], sc[:, 128:136],
                                             0.5, None, op0=ALU.mult),
         waits=P.w(sld), incs=[(evv, 1)])
    screato = [(evv, evv.v)]
    s1 = lambda mt: sc[:, 0 * 32 + mt: 0 * 32 + mt + 1]
    s2 = lambda mt: sc[:, 1 * 32 + mt: 1 * 32 + mt + 1]
    sh1 = lambda mt: sc[:, 2 * 32 + mt: 2 * 32 + mt + 1]
    sh2 = lambda mt: sc[:, 3 * 32 + mt: 3 * 32 + mt + 1]
    s3h = lambda mt: sc[:, 136 + mt: 136 + mt + 1]

    ctx.setdefault("_xn_guard", 0)
    for s in range(NSTRIP):
        xh = xn[:, :]
        guard = list(ctx.get("_xs_guard", []))
        xnw = emit_ln(ctx, src, s, xh, extra_x_waits=guard,
                      xh_guard=[(smm, ctx["_xn_guard"])])
        xsrc = lambda k, xh=xh: xh[:, k * S:(k + 1) * S]

        def ev_w1(mt, pap, gv):
            P.op("scalar", lambda e, mt=mt, pap=pap: e.activation(
                hA[:, mt * S:(mt + 1) * S], pap, AF.Silu, scale=s1(mt)),
                waits=[(smm, gv)] + screato, incs=[(evs, 1)])
        emit_gemm(ctx, wb["f1_w1T"], NA, 32, xsrc, ev_w1, xnw, prep_wait=pd)
        w1_done = evs.v

        def ev_w2(mt, pap, gv):
            P.op("vector", lambda e, mt=mt, pap=pap: e.scalar_tensor_tensor(
                hB[:, mt * S:(mt + 1) * S], pap, s2(mt), hA[:, mt * S:(mt + 1) * S],
                op0=ALU.mult, op1=ALU.mult),
                waits=[(smm, gv), (evs, w1_done)], incs=[(evv, 1)])
        emit_gemm(ctx, wb["f1_w2T"], NA, 32, xsrc, ev_w2, xnw, prep_wait=pd)
        ctx["_xn_guard"] = smm.v
        hsrc = lambda k: hB[:, k * S:(k + 1) * S]
        hB_done = [(evv, evv.v)]

        def ev_h1(mt, pap, gv):
            P.op("scalar", lambda e, mt=mt, pap=pap: e.activation(
                hA[:, mt * S:(mt + 1) * S], pap, AF.Gelu, scale=sh1(mt)),
                waits=[(smm, gv)], incs=[(evs, 1)])
        emit_gemm(ctx, wb["f1_h1T"], 32, 32, hsrc, ev_h1, hB_done, prep_wait=pd)
        hA_done = [(evs, evs.v)]
        hsrcA = lambda k: hA[:, k * S:(k + 1) * S]

        def ev_h2(mt, pap, gv):
            P.op("scalar", lambda e, mt=mt, pap=pap: e.activation(
                hB[:, mt * S:(mt + 1) * S], pap, AF.Gelu, scale=sh2(mt)),
                waits=[(smm, gv)], incs=[(evs, 1)])
        emit_gemm(ctx, wb["f1_h2T"], 32, 32, hsrcA, ev_h2, hA_done, prep_wait=pd)
        hB2_done = [(evs, evs.v)]

        xog = [(sout, ctx.get("_xo_guard", 0))]

        def ev_w3(mt, pap, gv):
            P.op("vector", lambda e, mt=mt, pap=pap: e.scalar_tensor_tensor(
                xo[:, mt * S:(mt + 1) * S], pap, s3h(mt), xs[:, mt * S:(mt + 1) * S],
                op0=ALU.mult, op1=ALU.add),
                waits=[(smm, gv)] + xog, incs=[(evv, 1)])
        emit_gemm(ctx, wb["f1_w3T"], 32, NA, hsrc, ev_w3, hB2_done, prep_wait=pd)
        ctx["_xs_guard"] = [(evv, evv.v)]
        ddst = dst.rearrange("(a p) t -> p a t", p=128)[:, :, s * S:(s + 1) * S]
        P.op("sync", lambda e, ddst=ddst: e.dma_start(
            ddst, xo[:, :].rearrange("p (a t) -> p a t", a=NA)),
            waits=P.w(evv), incs=[(sout, 16)])
        ctx["_xo_guard"] = sout.v


def emit_ffn0(ctx, src, dst):
    nc, P, c, wb, din = ctx["nc"], ctx["P"], ctx["c"], ctx["wb"], ctx["din"]
    sems = ctx["sems"]
    evv, evs, smm, sout = sems["evv"], sems["evs"], sems["smm"], sems["sout"]
    xs, xo, xn = ctx["xs"], ctx["xo"], ctx["xn"]
    abp, aux = ctx["abp"], ctx["aux"]
    pd = [ctx["prep_done"]["ffn0"]]
    es2 = ctx["stage_es"]
    tag = str(smm.v)
    hB = es2.enter_context(nc.sbuf_tensor("moeh" + tag, [128, 16 * S], BF16))
    acc = es2.enter_context(nc.sbuf_tensor("moeacc" + tag, [128, NA * S], F32))
    rdy = ctx.get("_res_ready", (ctx["sres"], 0))
    gate_sb = es2.enter_context(nc.sbuf_tensor("gate_sb" + tag, [128, 64], BF16))
    b1_sb = es2.enter_context(nc.sbuf_tensor("b1_sb" + tag, [33, 8192], BF16))
    b2_sb = es2.enter_context(nc.sbuf_tensor("b2_sb" + tag, [8, D], BF16))
    gls = es2.enter_context(nc.sbuf_tensor("gls" + tag, [8, S], F32))
    gw = es2.enter_context(nc.sbuf_tensor("gw" + tag, [128, 4 * 32], F32))
    g9sf = es2.enter_context(nc.sbuf_tensor("g9sf" + tag, [9, S], F32))
    g9sb = es2.enter_context(nc.sbuf_tensor("g9sb" + tag, [8, S], BF16))
    gbe = es2.enter_context(nc.sbuf_tensor("gbe" + tag, [128, S], BF16))
    esel_sb = es2.enter_context(nc.sbuf_tensor("esel_sb" + tag, [9, 9 * 128], F32))
    sgl = Sem(nc, ctx["es"], "moe_ld" + tag)
    P.op("gpsimd", lambda e: e.dma_start(esel_sb[:, :], din["esel"]),
         waits=[rdy], incs=[(sgl, 16)])
    P.op("gpsimd", lambda e: e.dma_start(
        gate_sb[:, :].rearrange("p (k e) -> p k e", e=8),
        wb["gate"].rearrange("(k p) e -> p k e", p=128)),
        waits=[pd[0]], incs=[(sgl, 16)])
    for q in range(2):
        P.op("gpsimd", lambda e, q=q: e.dma_start(
            b1_sb[32 * q:32 * q + 1, :], din["moe_b1"][0:1, q * 8192:(q + 1) * 8192]),
            incs=[(sgl, 16)])
    P.op("gpsimd", lambda e: e.dma_start(b2_sb[:, :], din["moe_b2"]), incs=[(sgl, 16)])
    mld = [(sgl, sgl.v)]

    ctx.setdefault("_xn_guard", 0)
    for s in range(NSTRIP):
        xh = xn[:, :]
        guard = list(ctx.get("_xs_guard", []))
        xnw = emit_ln(ctx, src, s, xh, extra_x_waits=guard,
                      xh_guard=[(smm, ctx["_xn_guard"])])
        ln_done = xnw[0][1]

        glp = aux[0:8, 0:S]
        for k in range(NA):
            P.op("tensor", lambda e, k=k, xh=xh: e.matmul(
                glp, gate_sb[:, k * 8:(k + 1) * 8], xh[:, k * S:(k + 1) * S],
                start=(k == 0), stop=(k == NA - 1)),
                waits=(xnw + mld + [(evv, ctx.get("_moe_aux_g", 0))]),
                incs=[(smm, 1)] if k == NA - 1 else [])
        P.op("vector", lambda e: e.tensor_copy(gls[:, :], glp),
             waits=P.w(smm), incs=[(evv, 1)])
        glv = evv.v
        for j in range(4):
            P.op("tensor", lambda e, j=j: e.transpose(
                aux[0:128, S + j * 8: S + (j + 1) * 8], gls[:, j * 128:(j + 1) * 128],
                c["ident_f"][0:8, 0:8]),
                waits=[(evv, glv), (evv, ctx.get("_moe_aux2_g", 0))],
                incs=[(smm, 1)] if j == 3 else [])
        gtv = smm.v
        for j in range(4):
            eS = gw[:, j * 32 + 0: j * 32 + 8]
            tS = gw[:, j * 32 + 8: j * 32 + 16]
            g9 = gw[:, j * 32 + 16: j * 32 + 25]
            nmx = gw[:, j * 32 + 25: j * 32 + 26]
            m2c = gw[:, j * 32 + 26: j * 32 + 27]
            nrm = gw[:, j * 32 + 27: j * 32 + 28]
            tj = aux[0:128, S + j * 8: S + (j + 1) * 8]
            P.op("vector", lambda e, tj=tj, nmx=nmx: e.reduce_max(
                nmx, tj, axis=AXX, negate=True), waits=[(smm, gtv)], incs=[(evv, 1)])
            P.op("scalar", lambda e, tj=tj, eS=eS, nmx=nmx: e.activation(
                eS, tj, AF.Exp, bias=nmx), waits=P.w(evv), incs=[(evs, 1)])
            P.op("vector", lambda e, eS=eS, tS=tS: e.tensor_scalar(
                tS, eS, 1.0, None, op0=ALU.is_lt), waits=P.w(evs), incs=[(evv, 1)])
            P.op("vector", lambda e, eS=eS, tS=tS: e.tensor_mul(tS, tS, eS),
                 waits=P.w(evv), incs=[(evv, 1)])
            P.op("vector", lambda e, tS=tS, m2c=m2c: e.reduce_max(m2c, tS, axis=AXX),
                 waits=P.w(evv), incs=[(evv, 1)])
            P.op("vector", lambda e, eS=eS, tS=tS, m2c=m2c: e.tensor_scalar(
                tS, eS, m2c, None, op0=ALU.is_ge), waits=P.w(evv), incs=[(evv, 1)])
            P.op("vector", lambda e, eS=eS, tS=tS, g9=g9: e.tensor_mul(
                g9[:, 0:8], tS, eS), waits=P.w(evv), incs=[(evv, 1)])
            P.op("vector", lambda e, m2c=m2c, nrm=nrm: e.tensor_scalar(
                nrm, m2c, 1.0, None, op0=ALU.add), waits=P.w(evv), incs=[(evv, 1)])
            P.op("vector", lambda e, nrm=nrm: e.reciprocal(nrm, nrm),
                 waits=P.w(evv), incs=[(evv, 1)])
            P.op("vector", lambda e, nrm=nrm, g9=g9: e.tensor_scalar(
                g9[:, 8:9], nrm, 0.5, None, op0=ALU.mult), waits=P.w(evv), incs=[(evv, 1)])
        gdone = evv.v
        gdones = evs.v
        ctx["_moe_aux_g"] = glv
        for j in range(4):
            P.op("tensor", lambda e, j=j: e.transpose(
                aux[0:9, S + j * 128: S + (j + 1) * 128],
                gw[:, j * 32 + 16: j * 32 + 25], c["ident_f"][:, :]),
                waits=[(evv, gdone), (evs, gdones)],
                incs=[(smm, 1)] if j == 3 else [])
        P.op("vector", lambda e: e.tensor_copy(g9sf[:, :], aux[0:9, S:2 * S]),
             waits=P.w(smm), incs=[(evv, 1)])
        P.op("vector", lambda e: e.tensor_copy(g9sb[:, :], aux[0:8, S:2 * S]),
             incs=[(evv, 1)])
        ctx["_moe_aux2_g"] = evv.v
        g8v = evv.v

        w1re = wb["moe_w1"]
        w2re = wb["moe_w2"]
        for e_ in range(8):
            P.op("tensor", lambda e, e_=e_: e.matmul(
                abp[:, 0:S], esel_sb[:, e_ * 128:(e_ + 1) * 128], g9sf[:, :],
                start=True, stop=True),
                waits=[(evv, g8v), (evv, ln_done), (evv, ctx.get("_gb_guard", 0))],
                incs=[(smm, 1)])
            P.op("vector", lambda e: e.tensor_copy(gbe[:, :], abp[:, 0:S]),
                 waits=P.w(smm), incs=[(evv, 1)])
            ctx["_gb_guard"] = evv.v
            gbw = [(evv, evv.v)]

            def xb_src(k, xh=xh):
                return xh[:, k * S:(k + 1) * S]

            def b1mm(mt, pap, last, e_=e_):
                i = e_ * 2048 + mt * 128
                q, off = divmod(i, 8192)
                P.op("tensor", lambda e, q=q, off=off, pap=pap: e.matmul(
                    pap, b1_sb[32 * q:32 * q + 1, off:off + 128],
                    c["ones_mat"][32 * q:32 * q + 1, 0:S], start=False, stop=True),
                    waits=mld, incs=[(smm, 1)] if last else [])

            def ev_w1(mt, pap, gv, e_=e_):
                P.op("vector", lambda e, mt=mt, pap=pap: e.scalar_tensor_tensor(
                    hB[:, mt * S:(mt + 1) * S], pap, 0.0, gbe[:, :],
                    op0=ALU.max, op1=ALU.mult),
                    waits=[(smm, gv)] + gbw, incs=[(evv, 1)])

            emit_gemm(ctx, w1re[e_ * D:(e_ + 1) * D, :], NA, 16, xb_src, ev_w1,
                      xnw, extra_mm=b1mm, prep_wait=pd)

            hsrc = lambda k: hB[:, k * S:(k + 1) * S]
            if e_ == 0:
                def ev_w2(mt, pap, gv):
                    P.op("vector", lambda e, mt=mt, pap=pap: e.tensor_copy(
                        acc[:, mt * S:(mt + 1) * S], pap),
                        waits=[(smm, gv)], incs=[(evv, 1)])
                emit_gemm(ctx, w2re[e_ * 2048:(e_ + 1) * 2048, :], 16, NA,
                          hsrc, ev_w2, [(evv, evv.v)], prep_wait=pd)
            else:
                def b2mm(mt, pap, last):
                    P.op("tensor", lambda e, mt=mt, pap=pap: e.matmul(
                        pap, b2_sb[0:8, mt * 128:(mt + 1) * 128], g9sb[0:8, :],
                        start=False, stop=True),
                        waits=mld + [(evv, g8v)], incs=[(smm, 1)] if last else [])

                def ev_w2(mt, pap, gv):
                    P.op("vector", lambda e, mt=mt, pap=pap: e.tensor_add(
                        acc[:, mt * S:(mt + 1) * S], acc[:, mt * S:(mt + 1) * S],
                        pap), waits=[(smm, gv)], incs=[(evv, 1)])
                emit_gemm(ctx, w2re[e_ * 2048:(e_ + 1) * 2048, :], 16, NA,
                          hsrc, ev_w2, [(evv, evv.v)],
                          extra_mm=(b2mm if e_ == 7 else None), prep_wait=pd)

        P.op("tensor", lambda e: e.matmul(
            abp[:, S:2 * S], esel_sb[:, 8 * 128:9 * 128], g9sf[:, :],
            start=True, stop=True),
            waits=[(evv, g8v), (evv, ln_done)], incs=[(smm, 1)])
        rbv = smm.v
        xog = [(sout, ctx.get("_xo_guard", 0))]
        accv = evv.v
        for a in range(NA):
            P.op("vector", lambda e, a=a: e.tensor_mul(
                xo[:, a * S:(a + 1) * S], acc[:, a * S:(a + 1) * S], abp[:, S:2 * S]),
                waits=[(smm, rbv), (evv, accv)] + xog, incs=[(evv, 1)])
            P.op("vector", lambda e, a=a: e.tensor_add(
                xo[:, a * S:(a + 1) * S], xo[:, a * S:(a + 1) * S],
                xs[:, a * S:(a + 1) * S]), waits=P.w(evv), incs=[(evv, 1)])
        ctx["_xs_guard"] = [(evv, evv.v)]
        ddst = dst.rearrange("(a p) t -> p a t", p=128)[:, :, s * S:(s + 1) * S]
        P.op("sync", lambda e, ddst=ddst: e.dma_start(
            ddst, xo[:, :].rearrange("p (a t) -> p a t", a=NA)),
            waits=P.w(evv), incs=[(sout, 16)])
        ctx["_xo_guard"] = sout.v
        ctx["_xn_guard"] = smm.v
        ctx["_ln_rows_g"] = evv.v


def _a0_selmap(kind):
    out = []
    for pt in range(6):
        heads = [h for h in range(3 * pt, min(3 * pt + 3, 16))]
        by_tile = {}
        for h in heads:
            a = h // 2 if kind == "q" else h // 4
            soff = (h % 2) * 64 if kind == "q" else (h % 4) * 32
            doff = (h - 3 * pt) * 32
            by_tile.setdefault(a, []).append((soff, doff))
        for a, lst in sorted(by_tile.items()):
            out.append((pt, a, lst))
    return out


def emit_attn0(ctx, src, dst):
    nc, P, c, wb, din = ctx["nc"], ctx["P"], ctx["c"], ctx["wb"], ctx["din"]
    sems = ctx["sems"]
    evv, evs, smm, sout = sems["evv"], sems["evs"], sems["smm"], sems["sout"]
    xs, xo, xn, x2b = ctx["xs"], ctx["xo"], ctx["xn"], ctx["x2b"]
    aux, abp, pp = ctx["aux"], ctx["abp"], ctx["pp"]
    pd = [ctx["prep_done"]["attn0"]]
    rdy = ctx.get("_res_ready", (ctx["sres"], 0))
    es2 = ctx["stage_es"]

    def sb2(name, shape, dt=BF16):
        return es2.enter_context(nc.sbuf_tensor(name, list(shape), dt))

    psb = sb2("a0_psb", (128, NA * S))
    kvb = sb2("a0_kvb", (128, S))
    knb = sb2("a0_knb", (128, 6 * T))
    vbuf = sb2("a0_vbuf", (128, 32 * 512))
    qnb = sb2("a0_qnb", (128, 6 * S))
    attn_sb = sb2("a0_attn", (128, 6 * S))
    expA = sb2("a0_expA", (128, 2 * S))
    expB = sb2("a0_expB", (128, 2 * S))
    r16b = sb2("a0_r16b", (16, S))
    qkrow = sb2("a0_qkrow", (16, 2 * S), F32)
    zrow = sb2("a0_zrow", (1, S), F32)
    rbs = sb2("a0_rbs", (128, S))
    psv = sb2("a0_psv", (32, S))
    wkv_sb = sb2("a0_wkv_sb", (128, 8 * 128))
    wk_sb = sb2("a0_wk_sb", (128, 512))
    wv_sb = sb2("a0_wv_sb", (128, 512))
    qsel_sb = sb2("a0_qsel_sb", (128, 11 * 128))
    ksel_sb = sb2("a0_ksel_sb", (128, 8 * 128))
    qones_sb = sb2("a0_qones_sb", (128, 8 * 16))
    kones_sb = sb2("a0_kones_sb", (128, 4 * 16))
    hot_sb = sb2("a0_hot_sb", (16, 768))
    qg_sb = sb2("a0_qg_sb", (128, 1), F32)
    kg_sb = sb2("a0_kg_sb", (128, 1), F32)
    dm_sb = sb2("a0_dm_sb", (128, 4 * S))

    swl = Sem(nc, ctx["es"], "a0_ld")
    loads = [
        (wkv_sb[:, :].rearrange("p (k m) -> p k m", m=128),
         wb["a0_wkv"].rearrange("(k p) m -> p k m", p=128)),
        (wk_sb[:, :], wb["a0_wk"]),
        (wv_sb[:, :], wb["a0_wv"]),
        (qsel_sb[:, :], din["a0_qsel"]),
        (ksel_sb[:, :], din["a0_ksel"]),
        (qones_sb[:, :].rearrange("p (a h) -> p a h", h=16),
         din["a0_qones"].rearrange("(a p) h -> p a h", p=128)),
        (kones_sb[:, :].rearrange("p (a h) -> p a h", h=16),
         din["a0_kones"].rearrange("(a p) h -> p a h", p=128)),
        (hot_sb[:, :], din["a0_hot"]),
        (qg_sb[:, :], din["a0_qg"]),
        (kg_sb[:, :], din["a0_kg"]),
        (dm_sb[:, :], din["dmask"]),
    ]
    for (o, i_) in loads:
        P.op("gpsimd", lambda e, o=o, i_=i_: e.dma_start(o, i_),
             waits=[rdy] + pd, incs=[(swl, 16)])
    wl = [(swl, swl.v)]
    P.op("vector", lambda e: e.memset(attn_sb[:, :], 0.0), waits=[rdy], incs=[(evv, 1)])
    msdone = [(evv, evv.v)]

    qmap = _a0_selmap("q")
    kmap = _a0_selmap("k")
    ctx.setdefault("_xn_guard", 0)

    for s in range(NSTRIP):
        xh = xn[:, :]
        guard = list(ctx.get("_xs_guard", []))
        xnw = emit_ln(ctx, src, s, xh, extra_x_waits=guard,
                      xh_guard=[(smm, ctx["_xn_guard"])])
        ln_done = xnw[0][1]

        def ev_q(mt, pap, gv):
            P.op("scalar", lambda e, mt=mt, pap=pap: e.copy(
                psb[:, mt * S:(mt + 1) * S], pap),
                waits=[(smm, gv)], incs=[(evs, 1)])
        emit_gemm(ctx, wb["a0_wq"], NA, NA, lambda k, xh=xh: xh[:, k * S:(k + 1) * S],
                  ev_q, xnw, prep_wait=pd)
        qcv = evs.v
        for a in range(NA):
            P.op("vector", lambda e, a=a: e.tensor_mul(
                x2b[:, a * S:(a + 1) * S], psb[:, a * S:(a + 1) * S],
                psb[:, a * S:(a + 1) * S]), waits=[(evs, qcv)], incs=[(evv, 1)])
        qsq = evv.v
        for a in range(NA):
            P.op("tensor", lambda e, a=a: e.matmul(
                aux[0:16, 0:S], qones_sb[:, a * 16:(a + 1) * 16],
                x2b[:, a * S:(a + 1) * S],
                start=(a == 0), stop=(a == NA - 1)),
                waits=[(evv, qsq), (evv, ctx.get("_ln_rows_g", 0))] + wl,
                incs=[(smm, 1)] if a == NA - 1 else [])
        qrv = smm.v

        def rows_rms(inv_n, wv):
            P.op("vector", lambda e, inv_n=inv_n: e.tensor_scalar(
                qkrow[:, 0:S], aux[0:16, 0:S], inv_n, 1e-6,
                op0=ALU.mult, op1=ALU.add),
                waits=[(smm, wv)], incs=[(evv, 1)])
            P.op("vector", lambda e: e.reciprocal(qkrow[:, S:2 * S], qkrow[:, 0:S]),
                 waits=P.w(evv), incs=[(evv, 1)])
            P.op("scalar", lambda e: e.activation(qkrow[:, 0:S], qkrow[:, S:2 * S],
                                                  AF.Sqrt), waits=P.w(evv), incs=[(evs, 1)])
            P.op("vector", lambda e: e.tensor_copy(r16b[:, :], qkrow[:, 0:S]),
                 waits=P.w(evs), incs=[(evv, 1)])
            ctx["_ln_rows_g"] = evv.v
            return evv.v

        rqv = rows_rms(1.0 / 64, qrv)

        def compact(selmap, sel_sb, g_ap, dstfn, rrdy):
            qcp = aux[0:128, S:2 * S]
            byp = {}
            for (pt, a, lst) in selmap:
                byp.setdefault(pt, []).append((a, lst))
            bi = 0
            for pt in sorted(byp):
                blocks = byp[pt]
                for i, (a, lst) in enumerate(blocks):
                    P.op("tensor", lambda e, bi=bi, a=a, i=i, n=len(blocks):
                         e.matmul(qcp, sel_sb[:, bi * 128:(bi + 1) * 128],
                                  psb[:, a * S:(a + 1) * S],
                                  start=(i == 0), stop=(i == n - 1)),
                         waits=[(evv, rrdy), (evv, ctx.get("_a0_ptg", 0))],
                         incs=[(smm, 1)] if i == len(blocks) - 1 else [])
                    bi += 1
                scv = smm.v
                P.op("tensor", lambda e, pt=pt: e.matmul(
                    abp[:, 0:S], hot_sb[:, pt * 128:(pt + 1) * 128], r16b[:, :],
                    start=True, stop=True),
                    waits=[(evv, rrdy), (evv, ctx.get("_a0_rbg", 0)), (evv, ln_done)],
                    incs=[(smm, 1)])
                P.op("vector", lambda e: e.tensor_copy(rbs[:, :], abp[:, 0:S]),
                     waits=P.w(smm), incs=[(evv, 1)])
                ctx["_a0_rbg"] = evv.v
                dap = dstfn(pt)
                P.op("vector", lambda e, dap=dap: e.scalar_tensor_tensor(
                    dap, qcp, g_ap, rbs[:, :], op0=ALU.mult, op1=ALU.mult),
                    waits=[(smm, scv)] + P.w(evv), incs=[(evv, 1)])
                ctx["_a0_ptg"] = evv.v

        compact(qmap, qsel_sb, qg_sb[:, 0:1],
                lambda pt: qnb[:, pt * S:(pt + 1) * S], rqv)
        qn_done = evv.v

        for a in range(NA):
            P.op("tensor", lambda e, a=a, xh=xh: e.matmul(
                abp[:, S:2 * S], wkv_sb[:, a * 128:(a + 1) * 128],
                xh[:, a * S:(a + 1) * S], start=(a == 0), stop=(a == NA - 1)),
                waits=[(evv, ln_done)] + wl + xnw,
                incs=[(smm, 1)] if a == NA - 1 else [])
        ctx["_xn_guard"] = smm.v
        P.op("vector", lambda e: e.tensor_copy(kvb[:, :], abp[:, S:2 * S]),
             waits=P.w(smm), incs=[(evv, 1)])
        kvv = evv.v
        for mt in range(4):
            P.op("tensor", lambda e, mt=mt: e.matmul(
                pp[:, mt * S:(mt + 1) * S], wk_sb[:, mt * 128:(mt + 1) * 128],
                kvb[:, :], start=True, stop=True),
                waits=[(evv, kvv)] + list(ctx.get("_gemm_guard", [])),
                incs=[(smm, 1)] if mt == 3 else [])
        kmv = smm.v
        for mt in range(4):
            P.op("scalar", lambda e, mt=mt: e.copy(
                psb[:, mt * S:(mt + 1) * S], pp[:, mt * S:(mt + 1) * S]),
                waits=[(smm, kmv), (evv, qn_done)], incs=[(evs, 1)])
        kcv = evs.v
        for a in range(4):
            P.op("vector", lambda e, a=a: e.tensor_mul(
                x2b[:, a * S:(a + 1) * S], psb[:, a * S:(a + 1) * S],
                psb[:, a * S:(a + 1) * S]), waits=[(evs, kcv)], incs=[(evv, 1)])
        ksq = evv.v
        for a in range(4):
            P.op("tensor", lambda e, a=a: e.matmul(
                aux[0:16, 0:S], kones_sb[:, a * 16:(a + 1) * 16],
                x2b[:, a * S:(a + 1) * S], start=(a == 0), stop=(a == 3)),
                waits=[(evv, ksq)], incs=[(smm, 1)] if a == 3 else [])
        rkv = rows_rms(1.0 / 32, smm.v)
        compact(kmap, ksel_sb, kg_sb[:, 0:1],
                lambda pt: knb[:, pt * T + s * S: pt * T + (s + 1) * S], rkv)
        kn_done = evv.v
        for ts in range(4):
            P.op("tensor", lambda e, ts=ts: e.matmul(
                pp[:, ts * S:(ts + 1) * S], kvb[:, ts * 128:(ts + 1) * 128],
                wv_sb[:, :], start=True, stop=True),
                waits=[(evv, kvv), (evs, kcv)],
                incs=[(smm, 1)] if ts == 3 else [])
        vmv = smm.v
        for ts in range(4):
            vt = 4 * s + ts
            P.op("scalar", lambda e, ts=ts, vt=vt: e.copy(
                vbuf[:, vt * 512:(vt + 1) * 512],
                pp[:, ts * S:(ts + 1) * S]), waits=[(smm, vmv)], incs=[(evs, 1)])
        vcv = evs.v

        ntk = 4 * (s + 1)
        pv = pp[:, 2 * S:3 * S]
        zp = pp[:, 3 * S:4 * S]
        expg = {}
        prev_psv = vcv
        prev_rz = evv.v
        for h in range(16):
            pt, oh = h // 3, (h % 3) * 32
            mmv = {}
            for j in range(ntk + 1):
                if j < ntk:
                    bk = j % 2
                    scp = pp[:, bk * S:(bk + 1) * S]
                    P.op("tensor", lambda e, j=j, scp=scp, pt=pt, oh=oh: e.matmul(
                        scp, knb[oh:oh + 32, pt * T + j * 128: pt * T + (j + 1) * 128],
                        qnb[oh:oh + 32, pt * S:(pt + 1) * S], start=True, stop=True),
                        waits=([(evv, qn_done), (evv, kn_done), (evs, vcv)]
                               if (h == 0 and j == 0) else []) + expg.get(bk, []),
                        incs=[(smm, 1)])
                    mmv[j] = smm.v
                    diag = (j >= 4 * s)
                    P.op("scalar", lambda e, bk=bk, scp=scp: e.activation(
                        expA[:, bk * S:(bk + 1) * S], scp, AF.Exp,
                        bias=-12.0, scale=0.125),
                        waits=[(smm, mmv[j])], incs=[(evs, 1)])
                    expg[bk] = [(evs, evs.v)]
                    if diag:
                        P.op("vector", lambda e, bk=bk, jj=j - 4 * s: e.tensor_mul(
                            expB[:, bk * S:(bk + 1) * S],
                            expA[:, bk * S:(bk + 1) * S],
                            dm_sb[:, jj * S:(jj + 1) * S]),
                            waits=P.w(evs), incs=[(evv, 1)])
                        expg[bk] = expg[bk] + [(evv, evv.v)]
                if j >= 1:
                    jj = j - 1
                    bk = jj % 2
                    eb = expB if jj >= 4 * s else expA
                    st_, sp_ = (jj == 0), (jj == ntk - 1)
                    P.op("tensor", lambda e, jj=jj, bk=bk, eb=eb, h=h,
                         st_=st_, sp_=sp_: e.matmul(
                        pv[0:32, :], vbuf[:, jj * 512 + h * 32: jj * 512 + (h + 1) * 32],
                        eb[:, bk * S:(bk + 1) * S], start=st_, stop=sp_),
                        waits=expg[bk] +
                              ([(evs, prev_psv), (evv, prev_rz)] if jj == 0 else []))
                    P.op("tensor", lambda e, bk=bk, eb=eb, st_=st_, sp_=sp_: e.matmul(
                        zp[0:1, :], c["ones_col_b"][:, :],
                        eb[:, bk * S:(bk + 1) * S], start=st_, stop=sp_),
                        incs=[(smm, 1)] if sp_ else [])
            hv = smm.v
            P.op("scalar", lambda e: e.copy(psv[:, :], pv[0:32, :]),
                 waits=[(smm, hv), (evv, prev_rz)], incs=[(evs, 1)])
            prev_psv = evs.v
            P.op("vector", lambda e: e.reciprocal(zrow[:, :], zp[0:1, :]),
                 waits=[(smm, hv)], incs=[(evv, 1)])
            P.op("tensor", lambda e: e.matmul(
                aux[0:32, S:2 * S], c["ones_row_f"][0:1, 0:32], zrow[:, :],
                start=True, stop=True),
                waits=P.w(evv) + [(evv, ctx.get("_a0_zbg", 0))], incs=[(smm, 1)])
            P.op("vector", lambda e, pt=pt, oh=oh: e.tensor_mul(
                attn_sb[oh:oh + 32, pt * S:(pt + 1) * S], psv[:, :],
                aux[0:32, S:2 * S]),
                waits=P.w(smm) + [(evs, prev_psv)] + msdone, incs=[(evv, 1)])
            ctx["_a0_zbg"] = evv.v
            prev_rz = evv.v
        attn_done = evv.v
        P.op("tensor", None, waits=[(evv, attn_done)])
        ctx["_gemm_guard"] = [(evs, evs.v), (evv, evv.v)]
        xog = [(sout, ctx.get("_xo_guard", 0))]

        def ev_wo(mt, pap, gv):
            P.op("vector", lambda e, mt=mt, pap=pap: e.tensor_add(
                xo[:, mt * S:(mt + 1) * S], xs[:, mt * S:(mt + 1) * S], pap),
                waits=[(smm, gv)] + xog, incs=[(evv, 1)])
        emit_gemm(ctx, wb["a0_wo"], 6, NA,
                  lambda k: attn_sb[:, k * S:(k + 1) * S], ev_wo,
                  [(evv, attn_done)], prep_wait=pd)
        ctx["_xs_guard"] = [(evv, evv.v)]
        ddst = dst.rearrange("(a p) t -> p a t", p=128)[:, :, s * S:(s + 1) * S]
        P.op("sync", lambda e, ddst=ddst: e.dma_start(
            ddst, xo[:, :].rearrange("p (a t) -> p a t", a=NA)),
            waits=P.w(evv), incs=[(sout, 16)])
        ctx["_xo_guard"] = sout.v
        ctx["_ln_rows_g"] = evv.v


def emit_attn1(ctx, src, dst):
    nc, P, c, wb, din = ctx["nc"], ctx["P"], ctx["c"], ctx["wb"], ctx["din"]
    sems = ctx["sems"]
    evv, evs, smm, sout = sems["evv"], sems["evs"], sems["smm"], sems["sout"]
    xs, xo, xn = ctx["xs"], ctx["xo"], ctx["xn"]
    aux, pp = ctx["aux"], ctx["pp"]
    vdram = ctx["vdram"]
    pd = [ctx["prep_done"]["attn1"]]
    rdy = ctx.get("_res_ready", (ctx["sres"], 0))
    es2 = ctx["stage_es"]

    def sb2(name, shape, dt=BF16):
        return es2.enter_context(nc.sbuf_tensor(name, list(shape), dt))

    knb = sb2("a1_knb", (128, 6 * T))
    q1b = sb2("a1_q1b", (128, 6 * S))
    vsb = sb2("a1_vsb", (128, 4 * 1024))
    vh = sb2("a1_vh", (128, 2 * 32 * 64))
    sqA = sb2("a1_sqA", (128, 2 * S))
    Ab = sb2("a1_Ab", (128, 2 * S))
    a1o = sb2("a1_out", (128, NA * S))
    psv = sb2("a1_psv", (64, S))
    zrow = sb2("a1_zrow", (1, 2 * S), F32)
    hm_sb = sb2("a1_hm", (128, 5 * S))

    swl = Sem(nc, ctx["es"], "a1_ld")
    P.op("gpsimd", lambda e: e.dma_start(hm_sb[:, :], din["hmask"]),
         waits=[rdy] + pd, incs=[(swl, 16)])
    wl = [(swl, swl.v)]
    svd = Sem(nc, ctx["es"], "a1_vst")
    vhs = [Sem(nc, ctx["es"], "a1_vh0"), Sem(nc, ctx["es"], "a1_vh1")]
    vh_free = {}

    ctx.setdefault("_xn_guard", 0)
    for s in range(NSTRIP):
        xh = xn[:, :]
        guard = list(ctx.get("_xs_guard", []))
        xnw = emit_ln(ctx, src, s, xh, extra_x_waits=guard,
                      xh_guard=[(smm, ctx["_xn_guard"])])
        ln_done = xnw[0][1]

        def ev_q(mt, pap, gv):
            P.op("scalar", lambda e, mt=mt, pap=pap: e.copy(
                q1b[:, mt * S:(mt + 1) * S], pap),
                waits=[(smm, gv), (smm, ctx.get("_a1_qg", 0))], incs=[(evs, 1)])
        emit_gemm(ctx, wb["a1_wqT"], NA, 6, lambda k, xh=xh: xh[:, k * S:(k + 1) * S],
                  ev_q, xnw, prep_wait=pd)

        def ev_k(mt, pap, gv, s=s):
            P.op("scalar", lambda e, mt=mt, pap=pap, s=s: e.copy(
                knb[:, mt * T + s * S: mt * T + (s + 1) * S], pap),
                waits=[(smm, gv)], incs=[(evs, 1)])
        emit_gemm(ctx, wb["a1_wkT"], NA, 6, lambda k, xh=xh: xh[:, k * S:(k + 1) * S],
                  ev_k, xnw, prep_wait=pd)
        qk_done = evs.v

        ring = ctx["ring"]
        wvre = wb["a1_wvT"].rearrange("(kt p) m -> p kt m", p=128)
        vload = [ring.load(wvre[:, 0:4, :], 4, 1024, extra_waits=pd),
                 ring.load(wvre[:, 4:8, :], 4, 1024, extra_waits=pd)]
        vguard = list(ctx.get("_gemm_guard", []))
        for ts in range(4):
            pbank = pp[:, (ts % 2) * 2 * S:(ts % 2) * 2 * S + 2 * S]
            for nch in range(2):
                for k in range(NA):
                    ap, seq = vload[k // 4]
                    lhs = xh[:, k * S + ts * 128: k * S + (ts + 1) * 128]
                    P.op("tensor", lambda e, lhs=lhs, ap=ap, k=k, nch=nch,
                         pbank=pbank: e.matmul(
                             pbank[:, nch * S:(nch + 1) * S], lhs,
                             ap[:, (k % 4) * 1024 + nch * 512:(k % 4) * 1024 + (nch + 1) * 512],
                             start=(k == 0), stop=(k == NA - 1)),
                         waits=(ring.use_wait(seq) + (xnw + vguard +
                                [(evs, ctx.get("_a1_vg", {}).get(ts % 2, 0))]
                                if (k == 0 and nch == 0) else [])),
                         incs=[(smm, 1)] if (k == NA - 1 and nch == 1) else [])
            gv = smm.v
            P.op("scalar", lambda e, ts=ts, pbank=pbank: e.copy(
                vsb[:, ts * 1024:(ts + 1) * 1024], pbank[:, :]),
                waits=[(smm, gv), (svd, ctx.get("_a1_vsb_g", 0))], incs=[(evs, 1)])
            ctx.setdefault("_a1_vg", {})[ts % 2] = evs.v
        for (_, seq) in vload:
            ring.freed_by[seq] = (smm, smm.v)
        vdone = evs.v
        vdst = vdram[s * 512:(s + 1) * 512, :].rearrange("(ts p) c -> p ts c", p=128)
        P.op("gpsimd", lambda e, vdst=vdst: e.dma_start(
            vdst, vsb[:, :].rearrange("p (ts c) -> p ts c", c=1024)),
            waits=[(evs, vdone), (svd, svd.v)], incs=[(svd, 16)])
        svv = svd.v
        ctx["_a1_vg"] = {0: vdone, 1: vdone}
        ctx["_a1_vsb_g"] = svv

        ntk = 4 * (s + 1)
        pv = pp[:, 2 * S:3 * S]
        zp = pp[:, 3 * S:4 * S]
        sqg = {}
        prev_psv = evs.v
        prev_rz = evv.v
        for h in range(16):
            pt, oh = h // 3, (h % 3) * 32
            ot, oo = h // 2, (h % 2) * 64
            slot = h % 2
            vsrc = vdram[0:ntk * 128, h * 64:(h + 1) * 64].rearrange(
                "(j p) c -> p j c", p=128)
            P.op("gpsimd", lambda e, vsrc=vsrc, slot=slot, ntk=ntk: e.dma_start(
                vh[:, slot * 2048:slot * 2048 + ntk * 64].rearrange(
                    "p (j c) -> p j c", c=64), vsrc),
                waits=[(svd, svv)] + ([vh_free[slot]] if slot in vh_free else []),
                incs=[(vhs[slot], 16)])
            vhw = [(vhs[slot], vhs[slot].v)]
            mmv = {}
            for j in range(ntk + 1):
                if j < ntk:
                    bk = j % 2
                    scp = pp[:, bk * S:(bk + 1) * S]
                    P.op("tensor", lambda e, j=j, scp=scp, pt=pt, oh=oh: e.matmul(
                        scp, knb[oh:oh + 16, pt * T + j * 128: pt * T + (j + 1) * 128],
                        q1b[oh:oh + 16, pt * S:(pt + 1) * S], start=True, stop=True),
                        waits=([(evs, vdone)] if (h == 0 and j == 0) else []) +
                              sqg.get(bk, []),
                        incs=[(smm, 1)])
                    mmv[j] = smm.v
                    P.op("scalar", lambda e, bk=bk, scp=scp: e.activation(
                        sqA[:, bk * S:(bk + 1) * S], scp, AF.Square,
                        bias=1.0, scale=0.25),
                        waits=[(smm, mmv[j])], incs=[(evs, 1)])
                    sqg[bk] = [(evs, evs.v)]
                    hmof = (j - 4 * s) if j >= 4 * s else 4
                    P.op("vector", lambda e, bk=bk, hmof=hmof: e.scalar_tensor_tensor(
                        Ab[:, bk * S:(bk + 1) * S], sqA[:, bk * S:(bk + 1) * S],
                        1.0, hm_sb[:, hmof * S:(hmof + 1) * S],
                        op0=ALU.add, op1=ALU.mult),
                        waits=sqg[bk] + wl, incs=[(evv, 1)])
                    sqg[bk] = sqg[bk] + [(evv, evv.v)]
                if j >= 1:
                    jj = j - 1
                    bk = jj % 2
                    st_, sp_ = (jj == 0), (jj == ntk - 1)
                    P.op("tensor", lambda e, jj=jj, bk=bk, slot=slot,
                         st_=st_, sp_=sp_: e.matmul(
                        pv[0:64, :], vh[:, slot * 2048 + jj * 64: slot * 2048 + (jj + 1) * 64],
                        Ab[:, bk * S:(bk + 1) * S], start=st_, stop=sp_),
                        waits=sqg[bk] + vhw +
                              ([(evs, prev_psv), (evv, prev_rz)] if st_ else []))
                    P.op("tensor", lambda e, bk=bk, st_=st_, sp_=sp_: e.matmul(
                        zp[0:1, :], c["ones_col_b"][:, :],
                        Ab[:, bk * S:(bk + 1) * S], start=st_, stop=sp_),
                        incs=[(smm, 1)] if sp_ else [])
            hv = smm.v
            vh_free[slot] = (smm, hv)
            P.op("scalar", lambda e: e.copy(psv[:, :], pv[0:64, :]),
                 waits=[(smm, hv), (evv, prev_rz)], incs=[(evs, 1)])
            prev_psv = evs.v
            P.op("vector", lambda e: e.tensor_scalar(
                zrow[0:1, 0:S], zp[0:1, :], 1.0, 1e-12, op0=ALU.mult, op1=ALU.add),
                waits=[(smm, hv)], incs=[(evv, 1)])
            P.op("vector", lambda e: e.reciprocal(zrow[0:1, S:2 * S], zrow[0:1, 0:S]),
                 waits=P.w(evv), incs=[(evv, 1)])
            P.op("tensor", lambda e: e.matmul(
                aux[0:64, S:2 * S], c["ones_row_f"][0:1, 0:64], zrow[0:1, S:2 * S],
                start=True, stop=True),
                waits=P.w(evv) + [(evv, ctx.get("_a1_zbg", 0))], incs=[(smm, 1)])
            P.op("vector", lambda e, ot=ot, oo=oo: e.tensor_mul(
                a1o[oo:oo + 64, ot * S:(ot + 1) * S], psv[:, :],
                aux[0:64, S:2 * S]),
                waits=P.w(smm) + [(evs, prev_psv)], incs=[(evv, 1)])
            ctx["_a1_zbg"] = evv.v
            prev_rz = evv.v
        attn_done = evv.v
        P.op("tensor", None, waits=[(evv, attn_done)])
        ctx["_gemm_guard"] = [(evs, evs.v), (evv, evv.v)]
        xog = [(sout, ctx.get("_xo_guard", 0))]

        def ev_wo(mt, pap, gv):
            P.op("vector", lambda e, mt=mt, pap=pap: e.tensor_add(
                xo[:, mt * S:(mt + 1) * S], xs[:, mt * S:(mt + 1) * S], pap),
                waits=[(smm, gv)] + xog, incs=[(evv, 1)])
        emit_gemm(ctx, wb["a1_woT"], NA, NA,
                  lambda k: a1o[:, k * S:(k + 1) * S], ev_wo,
                  [(evv, attn_done)], prep_wait=pd)
        ctx["_xs_guard"] = [(evv, evv.v)]
        ctx["_a1_qg"] = smm.v
        ddst = dst.rearrange("(a p) t -> p a t", p=128)[:, :, s * S:(s + 1) * S]
        P.op("sync", lambda e, ddst=ddst: e.dma_start(
            ddst, xo[:, :].rearrange("p (a t) -> p a t", a=NA)),
            waits=P.w(evv), incs=[(sout, 16)])
        ctx["_xo_guard"] = sout.v
        ctx["_ln_rows_g"] = evv.v


def make_in_map(inputs, core, stages=ALL_STAGES):
    f = np.ascontiguousarray
    m = {}
    m["xT"] = f(np.asarray(inputs["L0_attn_x"][core]).T)
    m["cconst"] = np.tile(np.array(REG_CONSTS, dtype=np.float32), (128, 1))
    m["ones_row"] = np.ones((1, 128 + S), dtype=np.float32)
    m["identc"] = np.eye(128, dtype=np.float32)
    need = set(st[:4] if st.startswith("ffn") else st for st in stages)
    if "ffn0" in need:
        m["gate"] = f(np.asarray(inputs["L0_ffn_gate"], dtype=np.float32))
        esel = np.zeros((9, 9 * 128), dtype=np.float32)
        for e in range(9):
            esel[e, e * 128:(e + 1) * 128] = 1.0
        m["esel"] = esel
        m["moe_w1"] = f(np.asarray(inputs["L0_ffn_W1"], dtype=np.float32))
        m["moe_b1"] = f(np.asarray(inputs["L0_ffn_b1"], dtype=np.float32).reshape(1, -1))
        m["moe_w2"] = f(np.asarray(inputs["L0_ffn_W2"], dtype=np.float32))
        m["moe_b2"] = f(np.asarray(inputs["L0_ffn_b2"], dtype=np.float32))
    if "attn0" in need:
        m["a0_wq"] = f(np.asarray(inputs["L0_attn_Wq"], dtype=np.float32))
        m["a0_wkv"] = f(np.asarray(inputs["L0_attn_Wkv"], dtype=np.float32))
        m["a0_wk"] = f(np.asarray(inputs["L0_attn_Wk"], dtype=np.float32))
        m["a0_wv"] = f(np.asarray(inputs["L0_attn_Wv"], dtype=np.float32))
        wo = np.asarray(inputs["L0_attn_Wo"], dtype=np.float32)
        wop = np.zeros((768, D), dtype=np.float32)
        for h in range(16):
            pt, off = h // 3, (h % 3) * 32
            wop[pt * 128 + off: pt * 128 + off + 32, :] = wo[h * 64: h * 64 + 32, :]
        m["a0_wo"] = wop
        m["a0_qg"] = f(np.tile(np.asarray(inputs["L0_attn_q_g"], dtype=np.float32)[:32], 4)[:, None])
        m["a0_kg"] = f(np.tile(np.asarray(inputs["L0_attn_k_g"], dtype=np.float32), 4)[:, None])
        qo = np.zeros((D, 16), dtype=np.float32)
        for h in range(16):
            qo[h * 64:(h + 1) * 64, h] = 1.0
        m["a0_qones"] = qo
        ko = np.zeros((512, 16), dtype=np.float32)
        for h in range(16):
            ko[h * 32:(h + 1) * 32, h] = 1.0
        m["a0_kones"] = ko
        for kind, nb in (("q", 11), ("k", 8)):
            sel = np.zeros((128, nb * 128), dtype=np.float32)
            for bi, (pt, a, lst) in enumerate(_a0_selmap(kind)):
                for (soff, doff) in lst:
                    for r in range(32):
                        sel[soff + r, bi * 128 + doff + r] = 1.0
            m["a0_" + kind + "sel"] = sel
        hot = np.zeros((16, 768), dtype=np.float32)
        for h in range(16):
            pt, off = h // 3, (h % 3) * 32
            hot[h, pt * 128 + off: pt * 128 + off + 32] = 1.0
        m["a0_hot"] = hot
        dm = np.zeros((128, 4 * S), dtype=np.float32)
        for j in range(4):
            tk = j * 128 + np.arange(128)[:, None]
            tq = np.arange(S)[None, :]
            dm[:, j * S:(j + 1) * S] = (tk <= tq).astype(np.float32)
        m["dmask"] = dm
    if "attn1" in need:
        def pad_a1(w):
            wt = np.asarray(w, dtype=np.float32).T
            out = np.zeros((D, 768), dtype=np.float32)
            for h in range(16):
                pt, off = h // 3, (h % 3) * 32
                out[:, pt * 128 + off: pt * 128 + off + 16] = wt[:, h * 16:(h + 1) * 16]
            return out
        m["a1_wqT"] = pad_a1(inputs["L1_attn_Wq"])
        m["a1_wkT"] = pad_a1(inputs["L1_attn_Wk"])
        m["a1_wvT"] = f(np.asarray(inputs["L1_attn_Wv"], dtype=np.float32).T)
        m["a1_woT"] = f(np.asarray(inputs["L1_attn_Wo"], dtype=np.float32).T)
        hm = np.zeros((128, 5 * S), dtype=np.float32)
        for j in range(4):
            tk = j * 128 + np.arange(128)[:, None]
            tq = np.arange(S)[None, :]
            hm[:, j * S:(j + 1) * S] = 0.5 * (tk <= tq)
        hm[:, 4 * S:] = 0.5
        m["hmask"] = hm
    if "ffn1" in need:
        m["f1_w1T"] = f(np.asarray(inputs["L1_ffn_w1_w"], dtype=np.float32).T)
        m["f1_w2T"] = f(np.asarray(inputs["L1_ffn_w2_w"], dtype=np.float32).T)
        m["f1_h1T"] = f(np.asarray(inputs["L1_ffn_h1_w"], dtype=np.float32).T)
        m["f1_h2T"] = f(np.asarray(inputs["L1_ffn_h2_w"], dtype=np.float32).T)
        m["f1_w3T"] = f(np.asarray(inputs["L1_ffn_w3_w"], dtype=np.float32).T)
        m["f1_s1"] = f(np.asarray(inputs["L1_ffn_w1_s"], dtype=np.float32).reshape(32, 128).T)
        m["f1_s2"] = f(np.asarray(inputs["L1_ffn_w2_s"], dtype=np.float32).reshape(32, 128).T)
        m["f1_sh1"] = f(np.asarray(inputs["L1_ffn_h1_s"], dtype=np.float32).reshape(32, 128).T)
        m["f1_sh2"] = f(np.asarray(inputs["L1_ffn_h2_s"], dtype=np.float32).reshape(32, 128).T)
        m["f1_s3"] = f(np.asarray(inputs["L1_ffn_w3_s"], dtype=np.float32).reshape(8, 128).T)
    return m


def kernel(**inputs):
    nc = build_nc(ALL_STAGES)
    in_maps = [make_in_map(inputs, cid) for cid in range(8)]
    res = run_bass_kernel_spmd(nc, in_maps, core_ids=list(range(8)))
    out = np.stack([np.asarray(r["outT"]).T for r in res.results], axis=0)
    return out.astype(np.float32)


# version 03
# speedup vs baseline: 1.0721x; 1.0151x over previous
import numpy as np
import concourse.bass as bass
import concourse.mybir as mybir
from concourse.bass_utils import run_bass_kernel_spmd
from contextlib import ExitStack

F32 = mybir.dt.float32
BF16 = mybir.dt.bfloat16
AF = mybir.ActivationFunctionType
ALU = mybir.AluOpType
AXX = mybir.AxisListType.X

T = 4096
D = 1024
S = 512
NSTRIP = T // S
NA = D // 128

ALL_STAGES = ("ffn0a", "attn0", "ffn0b", "ffn1a", "attn1", "ffn1b")

REG_CONSTS = [-12.0, 0.125, 1.0, 0.25, 0.5]


class Sem:
    __slots__ = ("h", "v", "name")

    def __init__(self, nc, es, name):
        self.h = es.enter_context(nc.semaphore(name))
        self.v = 0
        self.name = name


class Sched:

    def __init__(self):
        self.ops = []

    def op(self, eng, fn=None, waits=(), incs=()):
        rincs = []
        for (sem, k) in incs:
            sem.v += k
            rincs.append((sem, k))
        self.ops.append((eng, fn, [(s, int(v)) for (s, v) in waits], rincs))

    def w(self, *sems):
        return [(s, s.v) for s in sems]

    def emit(self, eng_name, eng):
        seen = {}
        for e, fn, waits, incs in self.ops:
            if e != eng_name:
                continue
            for (sem, val) in waits:
                if val > 0 and seen.get(sem.name, -1) < val:
                    eng.wait_ge(sem.h, val)
                    seen[sem.name] = val
            inst = fn(eng) if fn is not None else None
            if incs:
                assert inst is not None and len(incs) == 1, \
                    "each sem inc must ride a real instruction"
                inst.then_inc(incs[0][0].h, incs[0][1])

    def emit_all(self, block):
        for name in ("sync", "tensor", "scalar", "vector", "gpsimd"):
            getattr(block, name)(lambda eng, n=name: self.emit(n, eng))


class WRing:

    def __init__(self, nc, es, P, name, nslots, slot_cols):
        self.sb = es.enter_context(nc.sbuf_tensor(name, [128, nslots * slot_cols], BF16))
        self.nslots = nslots
        self.cols = slot_cols
        self.P = P
        self.sw = [Sem(nc, es, f"{name}_w{i}") for i in range(nslots)]
        self.freed_by = {}
        self.seq = 0

    def load(self, dram_ap, kcnt, mcols, extra_waits=()):
        ncols = kcnt * mcols
        assert ncols <= self.cols
        seq = self.seq
        self.seq += 1
        slot = seq % self.nslots
        ap = self.sb[:, slot * self.cols: slot * self.cols + ncols]
        out = ap.rearrange("p (kt m) -> p kt m", m=mcols)
        waits = list(extra_waits)
        if seq >= self.nslots:
            waits.append(self.freed_by.pop(seq - self.nslots))
        self.P.op("sync", lambda e, out=out, d=dram_ap: e.dma_start(out, d),
                  waits=waits, incs=[(self.sw[slot], 16)])
        return ap, seq

    def use_wait(self, seq):
        return [(self.sw[seq % self.nslots], 16 * (seq // self.nslots + 1))]


def build_nc(stages=ALL_STAGES):
    nc = bass.Bass("TRN2", target_bir_lowering=False, debug=False)
    es = ExitStack()
    P = Sched()

    din = {}

    def dri(name, shape, dt=F32):
        din[name] = nc.dram_tensor(name, list(shape), dt, kind="ExternalInput").ap()
        return din[name]

    xT = dri("xT", (D, T))
    dri("cconst", (128, len(REG_CONSTS)))
    dri("ones_row", (1, 128 + S))
    dri("identc", (128, 128))

    need = set()
    for st in stages:
        need.add(st[:4] if st.startswith("ffn") else st)
    if "ffn0" in need:
        dri("gate", (D, 8))
        dri("esel", (9, 9 * 128))
        dri("moe_w1", (8, D, 2048))
        dri("moe_b1", (128, 8 * 16))
        dri("moe_w2", (8, 2048, D))
        dri("moe_b2", (8, D))
    if "attn0" in need:
        dri("a0_wq", (D, D))
        dri("a0_wkv", (D, 128))
        dri("a0_wk", (128, 512))
        dri("a0_wv", (128, 512))
        dri("a0_wo", (768, D))
        dri("a0_qg", (128, 1))
        dri("a0_kg", (128, 1))
        dri("a0_qones", (D, 16))
        dri("a0_kones", (512, 16))
        dri("a0_qsel", (128, 11 * 128))
        dri("a0_ksel", (128, 8 * 128))
        dri("a0_hot", (16, 768))
        dri("dmask", (128, 4 * S))
    if "attn1" in need:
        dri("a1_wqT", (D, 768))
        dri("a1_wkT", (D, 768))
        dri("a1_wvT", (D, D))
        dri("a1_woT", (D, D))
        dri("hmask", (128, 5 * S))
    if "ffn1" in need:
        dri("f1_w1T", (D, 4096))
        dri("f1_w2T", (D, 4096))
        dri("f1_h1T", (4096, 4096))
        dri("f1_h2T", (4096, 4096))
        dri("f1_w3T", (4096, D))
        dri("f1_s1", (128, 32))
        dri("f1_s2", (128, 32))
        dri("f1_sh1", (128, 32))
        dri("f1_sh2", (128, 32))
        dri("f1_s3", (128, 8))

    xres = nc.dram_tensor("xres", [D, T], F32, kind="Internal").ap()
    outT = nc.dram_tensor("outT", [D, T], F32, kind="ExternalOutput").ap()

    wb = {}

    def drw(name, shape):
        wb[name] = nc.dram_tensor(name + "_b", list(shape), BF16, kind="Internal").ap()
        return wb[name]

    if "ffn0" in need:
        drw("gate", (D, 8))
        drw("moe_w1", (8 * D, 2048))
        drw("moe_w2", (8 * 2048, D))
    if "attn0" in need:
        for n, sh in [("a0_wq", (D, D)), ("a0_wkv", (D, 128)), ("a0_wk", (128, 512)),
                      ("a0_wv", (128, 512)), ("a0_wo", (768, D))]:
            drw(n, sh)
    if "attn1" in need:
        for n, sh in [("a1_wqT", (D, 768)), ("a1_wkT", (D, 768)),
                      ("a1_wvT", (D, D)), ("a1_woT", (D, D))]:
            drw(n, sh)
        ctx_vdram = nc.dram_tensor("vdram", [T, 16 * 65], BF16, kind="Internal").ap()
    if "ffn1" in need:
        for n, sh in [("f1_w1T", (D, 4096)), ("f1_w2T", (D, 4096)),
                      ("f1_h1T", (4096, 4096)), ("f1_h2T", (4096, 4096)),
                      ("f1_w3T", (4096, D))]:
            drw(n, sh)

    def sb(name, shape, dt=F32):
        return es.enter_context(nc.sbuf_tensor(name, list(shape), dt))

    def psum(name, shape, dt=F32):
        return es.enter_context(nc.psum_tensor(name, list(shape), dt))

    c = {}
    c["cc"] = sb("cc", (128, len(REG_CONSTS)))
    for i, v in enumerate(REG_CONSTS):
        nc.const_aps.register(F32, v, c["cc"][:, i:i + 1])
    c["ones_col_f"] = sb("ones_col_f", (128, 1))
    c["ones_col_b"] = sb("ones_col_b", (128, 1), BF16)
    c["ones_row_f"] = sb("ones_row_f", (1, 128))
    c["ones_row_b"] = sb("ones_row_b", (1, S), BF16)
    c["ident_f"] = sb("ident_f", (128, 128))
    c["ones_mat"] = sb("ones_mat", (128, S), BF16)

    sboot = Sem(nc, es, "boot")
    sboot2 = Sem(nc, es, "boot2")

    P.op("gpsimd", lambda e: e.dma_start(c["cc"][:, :], din["cconst"]), incs=[(sboot, 16)])
    orow = din["ones_row"]
    P.op("gpsimd", lambda e: e.dma_start(c["ones_col_f"][:, :],
                                         orow[0:1, 0:128].rearrange("a p -> p a")), incs=[(sboot, 16)])
    P.op("gpsimd", lambda e: e.dma_start(c["ones_col_b"][:, :],
                                         orow[0:1, 0:128].rearrange("a p -> p a")), incs=[(sboot, 16)])
    P.op("gpsimd", lambda e: e.dma_start(c["ones_row_f"][:, :], orow[0:1, 0:128]), incs=[(sboot, 16)])
    P.op("gpsimd", lambda e: e.dma_start(c["ones_row_b"][:, :], orow[0:1, 128:128 + S]), incs=[(sboot, 16)])
    P.op("gpsimd", lambda e: e.dma_start(c["ident_f"][:, :], din["identc"]), incs=[(sboot, 16)])
    P.op("gpsimd", lambda e: e.memset(c["ones_mat"][:, :], 1.0), incs=[(sboot2, 1)])
    boot_n = sboot.v
    boot2_n = sboot2.v

    sres = Sem(nc, es, "resinit")
    P.op("gpsimd", lambda e: e.dma_start(xres, xT), incs=[(sres, 16)])

    prep_done = {}
    order = []
    for st in stages:
        key = st[:4] if st.startswith("ffn") else st
        if key not in order:
            order.append(key)
    group = {"ffn0": ["gate", "moe_w1", "moe_w2"],
             "attn0": ["a0_wq", "a0_wkv", "a0_wk", "a0_wv", "a0_wo"],
             "attn1": ["a1_wqT", "a1_wkT", "a1_wvT", "a1_woT"],
             "ffn1": ["f1_w1T", "f1_w2T", "f1_h1T", "f1_h2T", "f1_w3T"]}
    for key in order:
        sp = Sem(nc, es, "prep_" + key)
        for n in group[key]:
            src = din[n]
            if len(src.shape) == 3:
                src = src.rearrange("e k m -> (e k) m")
            P.op("gpsimd", lambda e, n=n, src=src: e.dma_start(wb[n], src),
                 incs=[(sp, 16)])
        prep_done[key] = (sp, sp.v)

    for e in ("tensor", "scalar", "vector", "sync"):
        P.op(e, None, waits=[(sboot, boot_n), (sboot2, boot2_n)])

    xs = sb("xs", (128, NA * S))
    x2b = sb("x2b", (128, NA * S), BF16)
    xn = sb("xn", (128, NA * S), BF16)
    xo = sb("xo", (128, NA * S))
    t1 = xo
    rws = sb("rws", (1, 4 * S))
    paux = psum("paux", (128, 2 * S))
    abp = psum("abp", (128, 2 * S))
    pp = psum("pp", (128, 4 * S))
    stpA = paux[0:1, 0:S]
    stpB = paux[0:1, S:2 * S]

    ring = WRing(nc, es, P, "wring", 2, 4096)

    sems = {n: Sem(nc, es, n) for n in ("sx", "evv", "evs", "smm", "sout")}

    ctx = dict(nc=nc, P=P, c=c, din=din, wb=wb, xres=xres, outT=outT,
               vdram=(ctx_vdram if "attn1" in need else None),
               xs=xs, x2b=x2b, xn=xn, t1=t1, xo=xo, rws=rws,
               stpA=stpA, stpB=stpB, abp=abp, pp=pp, aux=paux, ring=ring,
               sems=sems, sres=sres, prep_done=prep_done, es=es)

    with es:
        block = es.enter_context(nc.Block())
        P.emit_all(block)
        for si, st in enumerate(stages):
            is_last = (si == len(stages) - 1)
            src = xT if si == 0 else xres
            dst = outT if is_last else xres
            P = Sched()
            ctx["P"] = P
            ring.P = P
            with ExitStack() as stage_es:
                ctx["stage_es"] = stage_es
                if st in ("ffn1a", "ffn1b"):
                    emit_ffn1(ctx, src, dst)
                elif st in ("ffn0a", "ffn0b"):
                    emit_ffn0(ctx, src, dst)
                elif st == "attn0":
                    emit_attn0(ctx, src, dst)
                elif st == "attn1":
                    emit_attn1(ctx, src, dst)
                else:
                    raise ValueError(st)
                ctx["_res_ready"] = (sems["sout"], sems["sout"].v)
                P.emit_all(block)

        P = Sched()
        P.op("sync", None, waits=P.w(sems["sout"]))
        P.emit_all(block)
    return nc


def emit_ln(ctx, srcdram, s, xh, extra_x_waits=(), xh_guard=()):
    P, c = ctx["P"], ctx["c"]
    xs, x2b, t1, rws = ctx["xs"], ctx["x2b"], ctx["t1"], ctx["rws"]
    stpA, stpB, abp = ctx["stpA"], ctx["stpB"], ctx["abp"]
    sems = ctx["sems"]
    sx, evv, evs, smm = sems["sx"], sems["evv"], sems["evs"], sems["smm"]
    sres = ctx["sres"]

    src = srcdram.rearrange("(a p) t -> p a t", p=128)[:, :, s * S:(s + 1) * S]
    P.op("gpsimd", lambda e: e.dma_start(
        xs[:, :].rearrange("p (a t) -> p a t", a=NA), src),
        waits=[(sres, 16), ctx.get("_res_ready", (sres, 0))] + list(extra_x_waits),
        incs=[(sx, 16)])
    xw = [(sx, sx.v)]

    for a in range(NA):
        P.op("scalar", lambda e, a=a: e.activation(
            x2b[:, a * S:(a + 1) * S], xs[:, a * S:(a + 1) * S], AF.Square),
            waits=xw + [(smm, ctx.get("_ln_x2b_g", 0))], incs=[(evs, 1)])
    sqv = evs.v
    rowg = [(evv, ctx.get("_ln_rows_g", 0))]
    for a in range(NA):
        P.op("tensor", lambda e, a=a: e.matmul(
            stpA[0:1, :], c["ones_col_f"][:, :], xs[:, a * S:(a + 1) * S],
            start=(a == 0), stop=(a == NA - 1)),
            waits=xw + rowg, incs=[(smm, 1)] if a == NA - 1 else [])
    for a in range(NA):
        P.op("tensor", lambda e, a=a: e.matmul(
            stpB[0:1, :], c["ones_col_b"][:, :], x2b[:, a * S:(a + 1) * S],
            start=(a == 0), stop=(a == NA - 1)),
            waits=[(evs, sqv)] + rowg, incs=[(smm, 1)] if a == NA - 1 else [])
    mst = smm.v
    ctx["_ln_x2b_g"] = mst

    m = rws[0:1, 0:S]; arow = rws[0:1, S:2 * S]
    brow = rws[0:1, 2 * S:3 * S]; trow = rws[0:1, 3 * S:4 * S]
    P.op("vector", lambda e: e.tensor_scalar(m, stpA[0:1, :], 1.0 / D, None,
                                             op0=ALU.mult),
         waits=[(smm, mst)], incs=[(evv, 1)])
    P.op("vector", lambda e: e.tensor_scalar(trow, stpB[0:1, :], 1.0 / D, 1e-5,
                                             op0=ALU.mult, op1=ALU.add),
         incs=[(evv, 1)])
    ctx["_ln_rows_g"] = evv.v
    P.op("vector", lambda e: e.scalar_tensor_tensor(
        brow, m, -1.0, m, op0=ALU.mult, op1=ALU.mult),
        waits=P.w(evv), incs=[(evv, 1)])
    P.op("vector", lambda e: e.tensor_add(trow, trow, brow),
         waits=P.w(evv), incs=[(evv, 1)])
    P.op("vector", lambda e: e.reciprocal(brow, trow),
         waits=P.w(evv), incs=[(evv, 1)])
    P.op("scalar", lambda e: e.activation(arow, brow, AF.Sqrt),
         waits=P.w(evv), incs=[(evs, 1)])
    P.op("vector", lambda e: e.scalar_tensor_tensor(
        brow, m, -1.0, arow, op0=ALU.mult, op1=ALU.mult),
        waits=P.w(evs), incs=[(evv, 1)])
    evrow = evv.v

    P.op("tensor", lambda e: e.matmul(abp[:, 0:S], c["ones_row_f"][:, :], arow,
                                      start=True, stop=True),
         waits=[(evv, evrow)])
    P.op("tensor", lambda e: e.matmul(abp[:, S:2 * S], c["ones_row_f"][:, :], brow,
                                      start=True, stop=True), incs=[(smm, 1)])
    mbc = smm.v
    xog = [(ctx["sems"]["sout"], ctx.get("_xo_guard", 0))]
    for a in range(NA):
        P.op("vector", lambda e, a=a: e.tensor_mul(
            t1[:, a * S:(a + 1) * S], xs[:, a * S:(a + 1) * S], abp[:, 0:S]),
            waits=[(smm, mbc)] + list(xh_guard) + xog, incs=[(evv, 1)])
        P.op("vector", lambda e, a=a: e.tensor_add(
            xh[:, a * S:(a + 1) * S], t1[:, a * S:(a + 1) * S], abp[:, S:2 * S]),
            waits=P.w(evv), incs=[(evv, 1)])
    ln_done = evv.v
    return [(evv, ln_done)]


def emit_gemm(ctx, wdram, nk, nm, src_fn, evac, src_waits, extra_mm=None,
              prep_wait=()):
    P = ctx["P"]
    ring, pp = ctx["ring"], ctx["pp"]
    sems = ctx["sems"]
    smm, evv, evs = sems["smm"], sems["evv"], sems["evs"]
    wre = wdram.rearrange("(kt p) m -> p kt m", p=128)
    kb = 8
    pg = {0: list(ctx.get("_gemm_guard", [])), 1: list(ctx.get("_gemm_guard", []))}
    blk = 0
    for mb in range(0, nm, 2):
        par = blk % 2
        blk += 1
        mcnt = min(2, nm - mb)
        mcols = mcnt * 128
        batches = []
        for k0 in range(0, nk, kb):
            kcnt = min(kb, nk - k0)
            batches.append((k0, kcnt))

        def do_load(i):
            k0, kcnt = batches[i]
            dap = wre[:, k0:k0 + kcnt, mb * 128: mb * 128 + mcols]
            return ring.load(dap, kcnt, mcols, extra_waits=prep_wait)

        loaded = {0: do_load(0)}
        if len(batches) > 1:
            loaded[1] = do_load(1)
        guard = pg[par]
        first = True
        for i, (k0, kcnt) in enumerate(batches):
            ap, seq = loaded.pop(i)
            for ki in range(kcnt):
                k = k0 + ki
                for mi in range(mcnt):
                    lhs = ap[:, ki * mcols + mi * 128: ki * mcols + (mi + 1) * 128]
                    last_of_slot = (ki == kcnt - 1 and mi == mcnt - 1)
                    stop = (k == nk - 1 and extra_mm is None)
                    bank = 2 * par + mi
                    P.op("tensor", lambda e, lhs=lhs, k=k, bank=bank, stop=stop:
                         e.matmul(pp[:, bank * S:(bank + 1) * S], lhs, src_fn(k),
                                  start=(k == 0), stop=stop),
                         waits=(ring.use_wait(seq) +
                                (src_waits + guard if first else [])),
                         incs=[(smm, 1)] if last_of_slot else [])
                    first = False
            ring.freed_by[seq] = (smm, smm.v)
            if i + 2 < len(batches):
                loaded[i + 2] = do_load(i + 2)
        if extra_mm is not None:
            for mi in range(mcnt):
                bank = 2 * par + mi
                extra_mm(mb + mi, pp[:, bank * S:(bank + 1) * S],
                         last=(mi == mcnt - 1))
        gv = smm.v
        for mi in range(mcnt):
            bank = 2 * par + mi
            evac(mb + mi, pp[:, bank * S:(bank + 1) * S], gv)
        pg[par] = [(evv, evv.v), (evs, evs.v)]
    ctx["_gemm_guard"] = [(evv, evv.v), (evs, evs.v)]
    return


def emit_ffn1(ctx, src, dst):
    P, wb, din = ctx["P"], ctx["wb"], ctx["din"]
    sems = ctx["sems"]
    evv, evs, smm, sout = sems["evv"], sems["evs"], sems["smm"], sems["sout"]
    xs, xo, xn = ctx["xs"], ctx["xo"], ctx["xn"]
    pd = [ctx["prep_done"]["ffn1"]]
    tag = str(smm.v)
    hA = ctx["stage_es"].enter_context(
        ctx["nc"].sbuf_tensor("hA" + tag, [128, 32 * S], BF16))
    hB = ctx["stage_es"].enter_context(
        ctx["nc"].sbuf_tensor("hB" + tag, [128, 32 * S], BF16))

    rdy = ctx.get("_res_ready", (ctx["sres"], 0))
    sc = ctx["stage_es"].enter_context(
        ctx["nc"].sbuf_tensor("f1sc" + tag, [128, 32 * 4 + 8 + 8], F32))
    sld = Sem(ctx["nc"], ctx["es"], "f1scl" + tag)
    for i, n in enumerate(("f1_s1", "f1_s2", "f1_sh1", "f1_sh2")):
        P.op("gpsimd", lambda e, i=i, n=n: e.dma_start(sc[:, i * 32:(i + 1) * 32], din[n]),
             waits=[rdy], incs=[(sld, 16)])
    P.op("gpsimd", lambda e: e.dma_start(sc[:, 128:136], din["f1_s3"]), incs=[(sld, 16)])
    P.op("vector", lambda e: e.tensor_scalar(sc[:, 136:144], sc[:, 128:136],
                                             0.5, None, op0=ALU.mult),
         waits=P.w(sld), incs=[(evv, 1)])
    screato = [(evv, evv.v)]
    s1 = lambda mt: sc[:, 0 * 32 + mt: 0 * 32 + mt + 1]
    s2 = lambda mt: sc[:, 1 * 32 + mt: 1 * 32 + mt + 1]
    sh1 = lambda mt: sc[:, 2 * 32 + mt: 2 * 32 + mt + 1]
    sh2 = lambda mt: sc[:, 3 * 32 + mt: 3 * 32 + mt + 1]
    s3h = lambda mt: sc[:, 136 + mt: 136 + mt + 1]

    ctx.setdefault("_xn_guard", 0)
    for s in range(NSTRIP):
        xh = xn[:, :]
        guard = list(ctx.get("_xs_guard", []))
        xnw = emit_ln(ctx, src, s, xh, extra_x_waits=guard,
                      xh_guard=[(smm, ctx["_xn_guard"])])
        xsrc = lambda k, xh=xh: xh[:, k * S:(k + 1) * S]

        def ev_w1(mt, pap, gv):
            P.op("scalar", lambda e, mt=mt, pap=pap: e.activation(
                hA[:, mt * S:(mt + 1) * S], pap, AF.Silu, scale=s1(mt)),
                waits=[(smm, gv)] + screato, incs=[(evs, 1)])
        emit_gemm(ctx, wb["f1_w1T"], NA, 32, xsrc, ev_w1, xnw, prep_wait=pd)
        w1_done = evs.v

        def ev_w2(mt, pap, gv):
            P.op("vector", lambda e, mt=mt, pap=pap: e.scalar_tensor_tensor(
                hB[:, mt * S:(mt + 1) * S], pap, s2(mt), hA[:, mt * S:(mt + 1) * S],
                op0=ALU.mult, op1=ALU.mult),
                waits=[(smm, gv), (evs, w1_done)], incs=[(evv, 1)])
        emit_gemm(ctx, wb["f1_w2T"], NA, 32, xsrc, ev_w2, xnw, prep_wait=pd)
        ctx["_xn_guard"] = smm.v
        hsrc = lambda k: hB[:, k * S:(k + 1) * S]
        hB_done = [(evv, evv.v)]

        def ev_h1(mt, pap, gv):
            P.op("scalar", lambda e, mt=mt, pap=pap: e.activation(
                hA[:, mt * S:(mt + 1) * S], pap, AF.Gelu, scale=sh1(mt)),
                waits=[(smm, gv)], incs=[(evs, 1)])
        emit_gemm(ctx, wb["f1_h1T"], 32, 32, hsrc, ev_h1, hB_done, prep_wait=pd)
        hA_done = [(evs, evs.v)]
        hsrcA = lambda k: hA[:, k * S:(k + 1) * S]

        def ev_h2(mt, pap, gv):
            P.op("scalar", lambda e, mt=mt, pap=pap: e.activation(
                hB[:, mt * S:(mt + 1) * S], pap, AF.Gelu, scale=sh2(mt)),
                waits=[(smm, gv)], incs=[(evs, 1)])
        emit_gemm(ctx, wb["f1_h2T"], 32, 32, hsrcA, ev_h2, hA_done, prep_wait=pd)
        hB2_done = [(evs, evs.v)]

        xog = [(sout, ctx.get("_xo_guard", 0))]

        def ev_w3(mt, pap, gv):
            P.op("vector", lambda e, mt=mt, pap=pap: e.scalar_tensor_tensor(
                xo[:, mt * S:(mt + 1) * S], pap, s3h(mt), xs[:, mt * S:(mt + 1) * S],
                op0=ALU.mult, op1=ALU.add),
                waits=[(smm, gv)] + xog, incs=[(evv, 1)])
        emit_gemm(ctx, wb["f1_w3T"], 32, NA, hsrc, ev_w3, hB2_done, prep_wait=pd)
        ctx["_xs_guard"] = [(evv, evv.v)]
        ddst = dst.rearrange("(a p) t -> p a t", p=128)[:, :, s * S:(s + 1) * S]
        P.op("sync", lambda e, ddst=ddst: e.dma_start(
            ddst, xo[:, :].rearrange("p (a t) -> p a t", a=NA)),
            waits=P.w(evv), incs=[(sout, 16)])
        ctx["_xo_guard"] = sout.v


def emit_ffn0(ctx, src, dst):
    nc, P, c, wb, din = ctx["nc"], ctx["P"], ctx["c"], ctx["wb"], ctx["din"]
    sems = ctx["sems"]
    evv, evs, smm, sout = sems["evv"], sems["evs"], sems["smm"], sems["sout"]
    xs, xo, xn = ctx["xs"], ctx["xo"], ctx["xn"]
    abp, aux = ctx["abp"], ctx["aux"]
    pd = [ctx["prep_done"]["ffn0"]]
    es2 = ctx["stage_es"]
    tag = str(smm.v)
    hB = es2.enter_context(nc.sbuf_tensor("moeh" + tag, [128, 16 * S], BF16))
    acc = es2.enter_context(nc.sbuf_tensor("moeacc" + tag, [128, NA * S], F32))
    rdy = ctx.get("_res_ready", (ctx["sres"], 0))
    gate_sb = es2.enter_context(nc.sbuf_tensor("gate_sb" + tag, [128, 64], BF16))
    b1_sb = es2.enter_context(nc.sbuf_tensor("b1_sb" + tag, [128, 128], F32))
    b2_sb = es2.enter_context(nc.sbuf_tensor("b2_sb" + tag, [8, D], BF16))
    gls = es2.enter_context(nc.sbuf_tensor("gls" + tag, [8, S], F32))
    gw = es2.enter_context(nc.sbuf_tensor("gw" + tag, [128, 4 * 32], F32))
    g9sf = es2.enter_context(nc.sbuf_tensor("g9sf" + tag, [9, S], F32))
    g9sb = es2.enter_context(nc.sbuf_tensor("g9sb" + tag, [8, S], BF16))
    gbe = es2.enter_context(nc.sbuf_tensor("gbe" + tag, [128, S], BF16))
    esel_sb = es2.enter_context(nc.sbuf_tensor("esel_sb" + tag, [9, 9 * 128], F32))
    sgl = Sem(nc, ctx["es"], "moe_ld" + tag)
    P.op("gpsimd", lambda e: e.dma_start(esel_sb[:, :], din["esel"]),
         waits=[rdy], incs=[(sgl, 16)])
    P.op("gpsimd", lambda e: e.dma_start(
        gate_sb[:, :].rearrange("p (k e) -> p k e", e=8),
        wb["gate"].rearrange("(k p) e -> p k e", p=128)),
        waits=[pd[0]], incs=[(sgl, 16)])
    P.op("gpsimd", lambda e: e.dma_start(b1_sb[:, :], din["moe_b1"]), incs=[(sgl, 16)])
    P.op("gpsimd", lambda e: e.dma_start(b2_sb[:, :], din["moe_b2"]), incs=[(sgl, 16)])
    mld = [(sgl, sgl.v)]

    ctx.setdefault("_xn_guard", 0)
    for s in range(NSTRIP):
        xh = xn[:, :]
        guard = list(ctx.get("_xs_guard", []))
        xnw = emit_ln(ctx, src, s, xh, extra_x_waits=guard,
                      xh_guard=[(smm, ctx["_xn_guard"])])
        ln_done = xnw[0][1]

        glp = aux[0:8, 0:S]
        for k in range(NA):
            P.op("tensor", lambda e, k=k, xh=xh: e.matmul(
                glp, gate_sb[:, k * 8:(k + 1) * 8], xh[:, k * S:(k + 1) * S],
                start=(k == 0), stop=(k == NA - 1)),
                waits=(xnw + mld + [(evv, ctx.get("_moe_aux_g", 0))]),
                incs=[(smm, 1)] if k == NA - 1 else [])
        P.op("vector", lambda e: e.tensor_copy(gls[:, :], glp),
             waits=P.w(smm), incs=[(evv, 1)])
        glv = evv.v
        for j in range(4):
            P.op("tensor", lambda e, j=j: e.transpose(
                aux[0:128, S + j * 8: S + (j + 1) * 8], gls[:, j * 128:(j + 1) * 128],
                c["ident_f"][0:8, 0:8]),
                waits=[(evv, glv), (evv, ctx.get("_moe_aux2_g", 0))],
                incs=[(smm, 1)] if j == 3 else [])
        gtv = smm.v
        for j in range(4):
            eS = gw[:, j * 32 + 0: j * 32 + 8]
            tS = gw[:, j * 32 + 8: j * 32 + 16]
            g9 = gw[:, j * 32 + 16: j * 32 + 25]
            nmx = gw[:, j * 32 + 25: j * 32 + 26]
            m2c = gw[:, j * 32 + 26: j * 32 + 27]
            nrm = gw[:, j * 32 + 27: j * 32 + 28]
            tj = aux[0:128, S + j * 8: S + (j + 1) * 8]
            P.op("vector", lambda e, tj=tj, nmx=nmx: e.reduce_max(
                nmx, tj, axis=AXX, negate=True), waits=[(smm, gtv)], incs=[(evv, 1)])
            P.op("scalar", lambda e, tj=tj, eS=eS, nmx=nmx: e.activation(
                eS, tj, AF.Exp, bias=nmx), waits=P.w(evv), incs=[(evs, 1)])
            P.op("vector", lambda e, eS=eS, tS=tS: e.tensor_scalar(
                tS, eS, 1.0, None, op0=ALU.is_lt), waits=P.w(evs), incs=[(evv, 1)])
            P.op("vector", lambda e, eS=eS, tS=tS: e.tensor_mul(tS, tS, eS),
                 waits=P.w(evv), incs=[(evv, 1)])
            P.op("vector", lambda e, tS=tS, m2c=m2c: e.reduce_max(m2c, tS, axis=AXX),
                 waits=P.w(evv), incs=[(evv, 1)])
            P.op("vector", lambda e, eS=eS, tS=tS, m2c=m2c: e.tensor_scalar(
                tS, eS, m2c, None, op0=ALU.is_ge), waits=P.w(evv), incs=[(evv, 1)])
            P.op("vector", lambda e, eS=eS, tS=tS, g9=g9: e.tensor_mul(
                g9[:, 0:8], tS, eS), waits=P.w(evv), incs=[(evv, 1)])
            P.op("vector", lambda e, m2c=m2c, nrm=nrm: e.tensor_scalar(
                nrm, m2c, 1.0, None, op0=ALU.add), waits=P.w(evv), incs=[(evv, 1)])
            P.op("vector", lambda e, nrm=nrm: e.reciprocal(nrm, nrm),
                 waits=P.w(evv), incs=[(evv, 1)])
            P.op("vector", lambda e, nrm=nrm, g9=g9: e.tensor_scalar(
                g9[:, 8:9], nrm, 0.5, None, op0=ALU.mult), waits=P.w(evv), incs=[(evv, 1)])
        gdone = evv.v
        gdones = evs.v
        ctx["_moe_aux_g"] = glv
        for j in range(4):
            P.op("tensor", lambda e, j=j: e.transpose(
                aux[0:9, S + j * 128: S + (j + 1) * 128],
                gw[:, j * 32 + 16: j * 32 + 25], c["ident_f"][:, :]),
                waits=[(evv, gdone), (evs, gdones)],
                incs=[(smm, 1)] if j == 3 else [])
        P.op("vector", lambda e: e.tensor_copy(g9sf[:, :], aux[0:9, S:2 * S]),
             waits=P.w(smm), incs=[(evv, 1)])
        P.op("vector", lambda e: e.tensor_copy(g9sb[:, :], aux[0:8, S:2 * S]),
             incs=[(evv, 1)])
        ctx["_moe_aux2_g"] = evv.v
        g8v = evv.v

        w1re = wb["moe_w1"]
        w2re = wb["moe_w2"]
        for e_ in range(8):
            P.op("tensor", lambda e, e_=e_: e.matmul(
                abp[:, 0:S], esel_sb[:, e_ * 128:(e_ + 1) * 128], g9sf[:, :],
                start=True, stop=True),
                waits=[(evv, g8v), (evv, ln_done), (evv, ctx.get("_gb_guard", 0))],
                incs=[(smm, 1)])
            P.op("vector", lambda e: e.tensor_copy(gbe[:, :], abp[:, 0:S]),
                 waits=P.w(smm), incs=[(evv, 1)])
            ctx["_gb_guard"] = evv.v
            gbw = [(evv, evv.v)]

            def xb_src(k, xh=xh):
                return xh[:, k * S:(k + 1) * S]

            def ev_w1(mt, pap, gv, e_=e_):
                P.op("scalar", lambda e, mt=mt, pap=pap, e_=e_: e.activation(
                    hB[:, mt * S:(mt + 1) * S], pap, AF.Relu,
                    bias=b1_sb[:, e_ * 16 + mt: e_ * 16 + mt + 1]),
                    waits=[(smm, gv)] + mld, incs=[(evs, 1)])
                P.op("vector", lambda e, mt=mt: e.tensor_mul(
                    hB[:, mt * S:(mt + 1) * S], hB[:, mt * S:(mt + 1) * S],
                    gbe[:, :]),
                    waits=P.w(evs) + gbw, incs=[(evv, 1)])

            emit_gemm(ctx, w1re[e_ * D:(e_ + 1) * D, :], NA, 16, xb_src, ev_w1,
                      xnw, prep_wait=pd)

            hsrc = lambda k: hB[:, k * S:(k + 1) * S]
            if e_ == 0:
                def ev_w2(mt, pap, gv):
                    P.op("vector", lambda e, mt=mt, pap=pap: e.tensor_copy(
                        acc[:, mt * S:(mt + 1) * S], pap),
                        waits=[(smm, gv)], incs=[(evv, 1)])
                emit_gemm(ctx, w2re[e_ * 2048:(e_ + 1) * 2048, :], 16, NA,
                          hsrc, ev_w2, [(evv, evv.v)], prep_wait=pd)
            else:
                def b2mm(mt, pap, last):
                    P.op("tensor", lambda e, mt=mt, pap=pap: e.matmul(
                        pap, b2_sb[0:8, mt * 128:(mt + 1) * 128], g9sb[0:8, :],
                        start=False, stop=True),
                        waits=mld + [(evv, g8v)], incs=[(smm, 1)] if last else [])

                def ev_w2(mt, pap, gv):
                    P.op("vector", lambda e, mt=mt, pap=pap: e.tensor_add(
                        acc[:, mt * S:(mt + 1) * S], acc[:, mt * S:(mt + 1) * S],
                        pap), waits=[(smm, gv)], incs=[(evv, 1)])
                emit_gemm(ctx, w2re[e_ * 2048:(e_ + 1) * 2048, :], 16, NA,
                          hsrc, ev_w2, [(evv, evv.v)],
                          extra_mm=(b2mm if e_ == 7 else None), prep_wait=pd)

        P.op("tensor", lambda e: e.matmul(
            abp[:, S:2 * S], esel_sb[:, 8 * 128:9 * 128], g9sf[:, :],
            start=True, stop=True),
            waits=[(evv, g8v), (evv, ln_done)], incs=[(smm, 1)])
        rbv = smm.v
        xog = [(sout, ctx.get("_xo_guard", 0))]
        accv = evv.v
        for a in range(NA):
            P.op("vector", lambda e, a=a: e.tensor_mul(
                xo[:, a * S:(a + 1) * S], acc[:, a * S:(a + 1) * S], abp[:, S:2 * S]),
                waits=[(smm, rbv), (evv, accv)] + xog, incs=[(evv, 1)])
            P.op("vector", lambda e, a=a: e.tensor_add(
                xo[:, a * S:(a + 1) * S], xo[:, a * S:(a + 1) * S],
                xs[:, a * S:(a + 1) * S]), waits=P.w(evv), incs=[(evv, 1)])
        ctx["_xs_guard"] = [(evv, evv.v)]
        ddst = dst.rearrange("(a p) t -> p a t", p=128)[:, :, s * S:(s + 1) * S]
        P.op("sync", lambda e, ddst=ddst: e.dma_start(
            ddst, xo[:, :].rearrange("p (a t) -> p a t", a=NA)),
            waits=P.w(evv), incs=[(sout, 16)])
        ctx["_xo_guard"] = sout.v
        ctx["_xn_guard"] = smm.v
        ctx["_ln_rows_g"] = evv.v


def _a0_selmap(kind):
    out = []
    for pt in range(6):
        heads = [h for h in range(3 * pt, min(3 * pt + 3, 16))]
        by_tile = {}
        for h in heads:
            a = h // 2 if kind == "q" else h // 4
            soff = (h % 2) * 64 if kind == "q" else (h % 4) * 32
            doff = (h - 3 * pt) * 32
            by_tile.setdefault(a, []).append((soff, doff))
        for a, lst in sorted(by_tile.items()):
            out.append((pt, a, lst))
    return out


def emit_attn0(ctx, src, dst):
    nc, P, c, wb, din = ctx["nc"], ctx["P"], ctx["c"], ctx["wb"], ctx["din"]
    sems = ctx["sems"]
    evv, evs, smm, sout = sems["evv"], sems["evs"], sems["smm"], sems["sout"]
    xs, xo, xn, x2b = ctx["xs"], ctx["xo"], ctx["xn"], ctx["x2b"]
    aux, abp, pp = ctx["aux"], ctx["abp"], ctx["pp"]
    pd = [ctx["prep_done"]["attn0"]]
    rdy = ctx.get("_res_ready", (ctx["sres"], 0))
    es2 = ctx["stage_es"]

    def sb2(name, shape, dt=BF16):
        return es2.enter_context(nc.sbuf_tensor(name, list(shape), dt))

    psb = sb2("a0_psb", (128, NA * S))
    kvb = sb2("a0_kvb", (128, S))
    knb = sb2("a0_knb", (128, 6 * T))
    vbuf = sb2("a0_vbuf", (128, 32 * 528))
    qnb = sb2("a0_qnb", (128, 6 * S))
    attn_sb = sb2("a0_attn", (128, 6 * S))
    expA = sb2("a0_expA", (128, 2 * S))
    expB = sb2("a0_expB", (128, 2 * S))
    r16b = sb2("a0_r16b", (16, S))
    qkrow = sb2("a0_qkrow", (16, 2 * S), F32)
    zrow = sb2("a0_zrow", (1, S), F32)
    rbs = sb2("a0_rbs", (128, S))
    psv = sb2("a0_psv", (32, S))
    wkv_sb = sb2("a0_wkv_sb", (128, 8 * 128))
    wk_sb = sb2("a0_wk_sb", (128, 512))
    wv_sb = sb2("a0_wv_sb", (128, 512))
    qsel_sb = sb2("a0_qsel_sb", (128, 11 * 128))
    ksel_sb = sb2("a0_ksel_sb", (128, 8 * 128))
    qones_sb = sb2("a0_qones_sb", (128, 8 * 16))
    kones_sb = sb2("a0_kones_sb", (128, 4 * 16))
    hot_sb = sb2("a0_hot_sb", (16, 768))
    qg_sb = sb2("a0_qg_sb", (128, 1), F32)
    kg_sb = sb2("a0_kg_sb", (128, 1), F32)
    dm_sb = sb2("a0_dm_sb", (128, 4 * S))

    swl = Sem(nc, ctx["es"], "a0_ld")
    loads = [
        (wkv_sb[:, :].rearrange("p (k m) -> p k m", m=128),
         wb["a0_wkv"].rearrange("(k p) m -> p k m", p=128)),
        (wk_sb[:, :], wb["a0_wk"]),
        (wv_sb[:, :], wb["a0_wv"]),
        (qsel_sb[:, :], din["a0_qsel"]),
        (ksel_sb[:, :], din["a0_ksel"]),
        (qones_sb[:, :].rearrange("p (a h) -> p a h", h=16),
         din["a0_qones"].rearrange("(a p) h -> p a h", p=128)),
        (kones_sb[:, :].rearrange("p (a h) -> p a h", h=16),
         din["a0_kones"].rearrange("(a p) h -> p a h", p=128)),
        (hot_sb[:, :], din["a0_hot"]),
        (qg_sb[:, :], din["a0_qg"]),
        (kg_sb[:, :], din["a0_kg"]),
        (dm_sb[:, :], din["dmask"]),
    ]
    for (o, i_) in loads:
        P.op("gpsimd", lambda e, o=o, i_=i_: e.dma_start(o, i_),
             waits=[rdy] + pd, incs=[(swl, 16)])
    wl = [(swl, swl.v)]
    P.op("vector", lambda e: e.memset(attn_sb[:, :], 0.0), waits=[rdy], incs=[(evv, 1)])
    P.op("vector", lambda e: e.memset(
        vbuf[:, :].rearrange("p (j h c) -> p j h c", h=16, c=33)[:, :, :, 32:33], 1.0),
        waits=[rdy], incs=[(evv, 1)])
    msdone = [(evv, evv.v)]

    qmap = _a0_selmap("q")
    kmap = _a0_selmap("k")
    ctx.setdefault("_xn_guard", 0)

    for s in range(NSTRIP):
        xh = xn[:, :]
        guard = list(ctx.get("_xs_guard", []))
        xnw = emit_ln(ctx, src, s, xh, extra_x_waits=guard,
                      xh_guard=[(smm, ctx["_xn_guard"])])
        ln_done = xnw[0][1]

        def ev_q(mt, pap, gv):
            P.op("scalar", lambda e, mt=mt, pap=pap: e.copy(
                psb[:, mt * S:(mt + 1) * S], pap),
                waits=[(smm, gv)], incs=[(evs, 1)])
        emit_gemm(ctx, wb["a0_wq"], NA, NA, lambda k, xh=xh: xh[:, k * S:(k + 1) * S],
                  ev_q, xnw, prep_wait=pd)
        qcv = evs.v
        for a in range(NA):
            P.op("vector", lambda e, a=a: e.tensor_mul(
                x2b[:, a * S:(a + 1) * S], psb[:, a * S:(a + 1) * S],
                psb[:, a * S:(a + 1) * S]), waits=[(evs, qcv)], incs=[(evv, 1)])
        qsq = evv.v
        for a in range(NA):
            P.op("tensor", lambda e, a=a: e.matmul(
                aux[0:16, 0:S], qones_sb[:, a * 16:(a + 1) * 16],
                x2b[:, a * S:(a + 1) * S],
                start=(a == 0), stop=(a == NA - 1)),
                waits=[(evv, qsq), (evv, ctx.get("_ln_rows_g", 0))] + wl,
                incs=[(smm, 1)] if a == NA - 1 else [])
        qrv = smm.v

        def rows_rms(inv_n, wv):
            P.op("vector", lambda e, inv_n=inv_n: e.tensor_scalar(
                qkrow[:, 0:S], aux[0:16, 0:S], inv_n, 1e-6,
                op0=ALU.mult, op1=ALU.add),
                waits=[(smm, wv)], incs=[(evv, 1)])
            P.op("vector", lambda e: e.reciprocal(qkrow[:, S:2 * S], qkrow[:, 0:S]),
                 waits=P.w(evv), incs=[(evv, 1)])
            P.op("scalar", lambda e: e.activation(qkrow[:, 0:S], qkrow[:, S:2 * S],
                                                  AF.Sqrt), waits=P.w(evv), incs=[(evs, 1)])
            P.op("vector", lambda e: e.tensor_copy(r16b[:, :], qkrow[:, 0:S]),
                 waits=P.w(evs), incs=[(evv, 1)])
            ctx["_ln_rows_g"] = evv.v
            return evv.v

        rqv = rows_rms(1.0 / 64, qrv)

        def compact(selmap, sel_sb, g_ap, dstfn, rrdy):
            qcp = aux[0:128, S:2 * S]
            byp = {}
            for (pt, a, lst) in selmap:
                byp.setdefault(pt, []).append((a, lst))
            bi = 0
            for pt in sorted(byp):
                blocks = byp[pt]
                for i, (a, lst) in enumerate(blocks):
                    P.op("tensor", lambda e, bi=bi, a=a, i=i, n=len(blocks):
                         e.matmul(qcp, sel_sb[:, bi * 128:(bi + 1) * 128],
                                  psb[:, a * S:(a + 1) * S],
                                  start=(i == 0), stop=(i == n - 1)),
                         waits=[(evv, rrdy), (evv, ctx.get("_a0_ptg", 0))],
                         incs=[(smm, 1)] if i == len(blocks) - 1 else [])
                    bi += 1
                scv = smm.v
                P.op("tensor", lambda e, pt=pt: e.matmul(
                    abp[:, 0:S], hot_sb[:, pt * 128:(pt + 1) * 128], r16b[:, :],
                    start=True, stop=True),
                    waits=[(evv, rrdy), (evv, ctx.get("_a0_rbg", 0)), (evv, ln_done)],
                    incs=[(smm, 1)])
                P.op("vector", lambda e: e.tensor_copy(rbs[:, :], abp[:, 0:S]),
                     waits=P.w(smm), incs=[(evv, 1)])
                ctx["_a0_rbg"] = evv.v
                dap = dstfn(pt)
                P.op("vector", lambda e, dap=dap: e.scalar_tensor_tensor(
                    dap, qcp, g_ap, rbs[:, :], op0=ALU.mult, op1=ALU.mult),
                    waits=[(smm, scv)] + P.w(evv), incs=[(evv, 1)])
                ctx["_a0_ptg"] = evv.v

        compact(qmap, qsel_sb, qg_sb[:, 0:1],
                lambda pt: qnb[:, pt * S:(pt + 1) * S], rqv)
        qn_done = evv.v

        for a in range(NA):
            P.op("tensor", lambda e, a=a, xh=xh: e.matmul(
                abp[:, S:2 * S], wkv_sb[:, a * 128:(a + 1) * 128],
                xh[:, a * S:(a + 1) * S], start=(a == 0), stop=(a == NA - 1)),
                waits=[(evv, ln_done)] + wl + xnw,
                incs=[(smm, 1)] if a == NA - 1 else [])
        ctx["_xn_guard"] = smm.v
        P.op("vector", lambda e: e.tensor_copy(kvb[:, :], abp[:, S:2 * S]),
             waits=P.w(smm), incs=[(evv, 1)])
        kvv = evv.v
        for mt in range(4):
            P.op("tensor", lambda e, mt=mt: e.matmul(
                pp[:, mt * S:(mt + 1) * S], wk_sb[:, mt * 128:(mt + 1) * 128],
                kvb[:, :], start=True, stop=True),
                waits=[(evv, kvv)] + list(ctx.get("_gemm_guard", [])),
                incs=[(smm, 1)] if mt == 3 else [])
        kmv = smm.v
        for mt in range(4):
            P.op("scalar", lambda e, mt=mt: e.copy(
                psb[:, mt * S:(mt + 1) * S], pp[:, mt * S:(mt + 1) * S]),
                waits=[(smm, kmv), (evv, qn_done)], incs=[(evs, 1)])
        kcv = evs.v
        for a in range(4):
            P.op("vector", lambda e, a=a: e.tensor_mul(
                x2b[:, a * S:(a + 1) * S], psb[:, a * S:(a + 1) * S],
                psb[:, a * S:(a + 1) * S]), waits=[(evs, kcv)], incs=[(evv, 1)])
        ksq = evv.v
        for a in range(4):
            P.op("tensor", lambda e, a=a: e.matmul(
                aux[0:16, 0:S], kones_sb[:, a * 16:(a + 1) * 16],
                x2b[:, a * S:(a + 1) * S], start=(a == 0), stop=(a == 3)),
                waits=[(evv, ksq)], incs=[(smm, 1)] if a == 3 else [])
        rkv = rows_rms(1.0 / 32, smm.v)
        compact(kmap, ksel_sb, kg_sb[:, 0:1],
                lambda pt: knb[:, pt * T + s * S: pt * T + (s + 1) * S], rkv)
        kn_done = evv.v
        for ts in range(4):
            P.op("tensor", lambda e, ts=ts: e.matmul(
                pp[:, ts * S:(ts + 1) * S], kvb[:, ts * 128:(ts + 1) * 128],
                wv_sb[:, :], start=True, stop=True),
                waits=[(evv, kvv), (evs, kcv)],
                incs=[(smm, 1)] if ts == 3 else [])
        vmv = smm.v
        for ts in range(4):
            vt = 4 * s + ts
            P.op("scalar", lambda e, ts=ts, vt=vt: e.copy(
                vbuf[:, vt * 528:(vt + 1) * 528].rearrange(
                    "p (h c) -> p h c", c=33)[:, :, 0:32],
                pp[:, ts * S:(ts + 1) * S].rearrange("p (h c) -> p h c", c=32)),
                waits=[(smm, vmv)] + msdone, incs=[(evs, 1)])
        vcv = evs.v

        ntk = 4 * (s + 1)
        pv = pp[:, 2 * S:3 * S]
        expg = {}
        prev_psv = vcv
        prev_rz = evv.v
        for h in range(16):
            pt, oh = h // 3, (h % 3) * 32
            mmv = {}
            for j in range(ntk + 1):
                if j < ntk:
                    bk = j % 2
                    scp = pp[:, bk * S:(bk + 1) * S]
                    P.op("tensor", lambda e, j=j, scp=scp, pt=pt, oh=oh: e.matmul(
                        scp, knb[oh:oh + 32, pt * T + j * 128: pt * T + (j + 1) * 128],
                        qnb[oh:oh + 32, pt * S:(pt + 1) * S], start=True, stop=True),
                        waits=([(evv, qn_done), (evv, kn_done), (evs, vcv)]
                               if (h == 0 and j == 0) else []) + expg.get(bk, []),
                        incs=[(smm, 1)])
                    mmv[j] = smm.v
                    diag = (j >= 4 * s)
                    P.op("scalar", lambda e, bk=bk, scp=scp: e.activation(
                        expA[:, bk * S:(bk + 1) * S], scp, AF.Exp,
                        bias=-12.0, scale=0.125),
                        waits=[(smm, mmv[j])], incs=[(evs, 1)])
                    expg[bk] = [(evs, evs.v)]
                    if diag:
                        P.op("vector", lambda e, bk=bk, jj=j - 4 * s: e.tensor_mul(
                            expB[:, bk * S:(bk + 1) * S],
                            expA[:, bk * S:(bk + 1) * S],
                            dm_sb[:, jj * S:(jj + 1) * S]),
                            waits=P.w(evs), incs=[(evv, 1)])
                        expg[bk] = expg[bk] + [(evv, evv.v)]
                if j >= 1:
                    jj = j - 1
                    bk = jj % 2
                    eb = expB if jj >= 4 * s else expA
                    st_, sp_ = (jj == 0), (jj == ntk - 1)
                    P.op("tensor", lambda e, jj=jj, bk=bk, eb=eb, h=h,
                         st_=st_, sp_=sp_: e.matmul(
                        pv[0:33, :], vbuf[:, jj * 528 + h * 33: jj * 528 + (h + 1) * 33],
                        eb[:, bk * S:(bk + 1) * S], start=st_, stop=sp_),
                        waits=expg[bk] +
                              ([(evs, prev_psv), (evv, prev_rz)] if jj == 0 else []),
                        incs=[(smm, 1)] if sp_ else [])
            hv = smm.v
            P.op("scalar", lambda e: e.copy(psv[:, :], pv[0:32, :]),
                 waits=[(smm, hv), (evv, prev_rz)], incs=[(evs, 1)])
            prev_psv = evs.v
            P.op("vector", lambda e: e.reciprocal(zrow[:, :], pv[32:33, :]),
                 waits=[(smm, hv)], incs=[(evv, 1)])
            P.op("tensor", lambda e: e.matmul(
                aux[0:32, S:2 * S], c["ones_row_f"][0:1, 0:32], zrow[:, :],
                start=True, stop=True),
                waits=P.w(evv) + [(evv, ctx.get("_a0_zbg", 0))], incs=[(smm, 1)])
            P.op("vector", lambda e, pt=pt, oh=oh: e.tensor_mul(
                attn_sb[oh:oh + 32, pt * S:(pt + 1) * S], psv[:, :],
                aux[0:32, S:2 * S]),
                waits=P.w(smm) + [(evs, prev_psv)] + msdone, incs=[(evv, 1)])
            ctx["_a0_zbg"] = evv.v
            prev_rz = evv.v
        attn_done = evv.v
        P.op("tensor", None, waits=[(evv, attn_done)])
        ctx["_gemm_guard"] = [(evs, evs.v), (evv, evv.v)]
        xog = [(sout, ctx.get("_xo_guard", 0))]

        def ev_wo(mt, pap, gv):
            P.op("vector", lambda e, mt=mt, pap=pap: e.tensor_add(
                xo[:, mt * S:(mt + 1) * S], xs[:, mt * S:(mt + 1) * S], pap),
                waits=[(smm, gv)] + xog, incs=[(evv, 1)])
        emit_gemm(ctx, wb["a0_wo"], 6, NA,
                  lambda k: attn_sb[:, k * S:(k + 1) * S], ev_wo,
                  [(evv, attn_done)], prep_wait=pd)
        ctx["_xs_guard"] = [(evv, evv.v)]
        ddst = dst.rearrange("(a p) t -> p a t", p=128)[:, :, s * S:(s + 1) * S]
        P.op("sync", lambda e, ddst=ddst: e.dma_start(
            ddst, xo[:, :].rearrange("p (a t) -> p a t", a=NA)),
            waits=P.w(evv), incs=[(sout, 16)])
        ctx["_xo_guard"] = sout.v
        ctx["_ln_rows_g"] = evv.v


def emit_attn1(ctx, src, dst):
    nc, P, c, wb, din = ctx["nc"], ctx["P"], ctx["c"], ctx["wb"], ctx["din"]
    sems = ctx["sems"]
    evv, evs, smm, sout = sems["evv"], sems["evs"], sems["smm"], sems["sout"]
    xs, xo, xn = ctx["xs"], ctx["xo"], ctx["xn"]
    aux, pp = ctx["aux"], ctx["pp"]
    vdram = ctx["vdram"]
    pd = [ctx["prep_done"]["attn1"]]
    rdy = ctx.get("_res_ready", (ctx["sres"], 0))
    es2 = ctx["stage_es"]

    def sb2(name, shape, dt=BF16):
        return es2.enter_context(nc.sbuf_tensor(name, list(shape), dt))

    knb = sb2("a1_knb", (128, 6 * T))
    q1b = sb2("a1_q1b", (128, 6 * S))
    vsb = sb2("a1_vsb", (128, 4 * 1024))
    vh = sb2("a1_vh", (128, 2 * 32 * 65))
    sqA = sb2("a1_sqA", (128, 2 * S))
    Ab = sb2("a1_Ab", (128, 2 * S))
    a1o = sb2("a1_out", (128, NA * S))
    psv = sb2("a1_psv", (64, S))
    zrow = sb2("a1_zrow", (1, 2 * S), F32)
    hm_sb = sb2("a1_hm", (128, 5 * S))

    swl = Sem(nc, ctx["es"], "a1_ld")
    P.op("gpsimd", lambda e: e.dma_start(hm_sb[:, :], din["hmask"]),
         waits=[rdy] + pd, incs=[(swl, 16)])
    wl = [(swl, swl.v)]
    svd = Sem(nc, ctx["es"], "a1_vst")
    vhs = [Sem(nc, ctx["es"], "a1_vh0"), Sem(nc, ctx["es"], "a1_vh1")]
    vh_free = {}
    vflat = vdram.rearrange("t c -> (t c)")
    nch = (T * 16 * 65) // 65536
    for i in range(nch):
        P.op("gpsimd", lambda e, i=i: e.dma_start(
            vflat[i * 65536:(i + 1) * 65536].rearrange("(p f) -> p f", p=128),
            c["ones_mat"][:, :]),
            waits=[rdy], incs=[(svd, 16)])

    ctx.setdefault("_xn_guard", 0)
    for s in range(NSTRIP):
        xh = xn[:, :]
        guard = list(ctx.get("_xs_guard", []))
        xnw = emit_ln(ctx, src, s, xh, extra_x_waits=guard,
                      xh_guard=[(smm, ctx["_xn_guard"])])
        ln_done = xnw[0][1]

        def ev_q(mt, pap, gv):
            P.op("scalar", lambda e, mt=mt, pap=pap: e.copy(
                q1b[:, mt * S:(mt + 1) * S], pap),
                waits=[(smm, gv), (smm, ctx.get("_a1_qg", 0))], incs=[(evs, 1)])
        emit_gemm(ctx, wb["a1_wqT"], NA, 6, lambda k, xh=xh: xh[:, k * S:(k + 1) * S],
                  ev_q, xnw, prep_wait=pd)

        def ev_k(mt, pap, gv, s=s):
            P.op("scalar", lambda e, mt=mt, pap=pap, s=s: e.copy(
                knb[:, mt * T + s * S: mt * T + (s + 1) * S], pap),
                waits=[(smm, gv)], incs=[(evs, 1)])
        emit_gemm(ctx, wb["a1_wkT"], NA, 6, lambda k, xh=xh: xh[:, k * S:(k + 1) * S],
                  ev_k, xnw, prep_wait=pd)
        qk_done = evs.v

        ring = ctx["ring"]
        wvre = wb["a1_wvT"].rearrange("(kt p) m -> p kt m", p=128)
        vload = [ring.load(wvre[:, 0:4, :], 4, 1024, extra_waits=pd),
                 ring.load(wvre[:, 4:8, :], 4, 1024, extra_waits=pd)]
        vguard = list(ctx.get("_gemm_guard", []))
        for ts in range(4):
            pbank = pp[:, (ts % 2) * 2 * S:(ts % 2) * 2 * S + 2 * S]
            for nch in range(2):
                for k in range(NA):
                    ap, seq = vload[k // 4]
                    lhs = xh[:, k * S + ts * 128: k * S + (ts + 1) * 128]
                    P.op("tensor", lambda e, lhs=lhs, ap=ap, k=k, nch=nch,
                         pbank=pbank: e.matmul(
                             pbank[:, nch * S:(nch + 1) * S], lhs,
                             ap[:, (k % 4) * 1024 + nch * 512:(k % 4) * 1024 + (nch + 1) * 512],
                             start=(k == 0), stop=(k == NA - 1)),
                         waits=(ring.use_wait(seq) + (xnw + vguard +
                                [(evs, ctx.get("_a1_vg", {}).get(ts % 2, 0))]
                                if (k == 0 and nch == 0) else [])),
                         incs=[(smm, 1)] if (k == NA - 1 and nch == 1) else [])
            gv = smm.v
            P.op("scalar", lambda e, ts=ts, pbank=pbank: e.copy(
                vsb[:, ts * 1024:(ts + 1) * 1024], pbank[:, :]),
                waits=[(smm, gv), (svd, ctx.get("_a1_vsb_g", 0))], incs=[(evs, 1)])
            ctx.setdefault("_a1_vg", {})[ts % 2] = evs.v
        for (_, seq) in vload:
            ring.freed_by[seq] = (smm, smm.v)
        vdone = evs.v
        for ts in range(4):
            vdst = vdram[s * 512 + ts * 128: s * 512 + (ts + 1) * 128, :].rearrange(
                "p (h c) -> p h c", c=65)[:, :, 0:64]
            P.op("gpsimd", lambda e, vdst=vdst, ts=ts: e.dma_start(
                vdst, vsb[:, ts * 1024:(ts + 1) * 1024].rearrange(
                    "p (h c) -> p h c", c=64)),
                waits=[(evs, vdone), (svd, svd.v)], incs=[(svd, 16)])
        svv = svd.v
        ctx["_a1_vg"] = {0: vdone, 1: vdone}
        ctx["_a1_vsb_g"] = svv

        ntk = 4 * (s + 1)
        pv = pp[:, 2 * S:3 * S]
        sqg = {}
        prev_psv = evs.v
        prev_rz = evv.v
        for h in range(16):
            pt, oh = h // 3, (h % 3) * 32
            ot, oo = h // 2, (h % 2) * 64
            slot = h % 2
            vsrc = vdram[0:ntk * 128, h * 65:(h + 1) * 65].rearrange(
                "(j p) c -> p j c", p=128)
            P.op("gpsimd", lambda e, vsrc=vsrc, slot=slot, ntk=ntk: e.dma_start(
                vh[:, slot * 2080:slot * 2080 + ntk * 65].rearrange(
                    "p (j c) -> p j c", c=65), vsrc),
                waits=[(svd, svv)] + ([vh_free[slot]] if slot in vh_free else []),
                incs=[(vhs[slot], 16)])
            vhw = [(vhs[slot], vhs[slot].v)]
            mmv = {}
            for j in range(ntk + 1):
                if j < ntk:
                    bk = j % 2
                    scp = pp[:, bk * S:(bk + 1) * S]
                    P.op("tensor", lambda e, j=j, scp=scp, pt=pt, oh=oh: e.matmul(
                        scp, knb[oh:oh + 16, pt * T + j * 128: pt * T + (j + 1) * 128],
                        q1b[oh:oh + 16, pt * S:(pt + 1) * S], start=True, stop=True),
                        waits=([(evs, vdone)] if (h == 0 and j == 0) else []) +
                              sqg.get(bk, []),
                        incs=[(smm, 1)])
                    mmv[j] = smm.v
                    P.op("scalar", lambda e, bk=bk, scp=scp: e.activation(
                        sqA[:, bk * S:(bk + 1) * S], scp, AF.Square,
                        bias=1.0, scale=0.25),
                        waits=[(smm, mmv[j])], incs=[(evs, 1)])
                    sqg[bk] = [(evs, evs.v)]
                    hmof = (j - 4 * s) if j >= 4 * s else 4
                    P.op("vector", lambda e, bk=bk, hmof=hmof: e.scalar_tensor_tensor(
                        Ab[:, bk * S:(bk + 1) * S], sqA[:, bk * S:(bk + 1) * S],
                        1.0, hm_sb[:, hmof * S:(hmof + 1) * S],
                        op0=ALU.add, op1=ALU.mult),
                        waits=sqg[bk] + wl, incs=[(evv, 1)])
                    sqg[bk] = sqg[bk] + [(evv, evv.v)]
                if j >= 1:
                    jj = j - 1
                    bk = jj % 2
                    st_, sp_ = (jj == 0), (jj == ntk - 1)
                    P.op("tensor", lambda e, jj=jj, bk=bk, slot=slot,
                         st_=st_, sp_=sp_: e.matmul(
                        pv[0:65, :], vh[:, slot * 2080 + jj * 65: slot * 2080 + (jj + 1) * 65],
                        Ab[:, bk * S:(bk + 1) * S], start=st_, stop=sp_),
                        waits=sqg[bk] + vhw +
                              ([(evs, prev_psv), (evv, prev_rz)] if st_ else []),
                        incs=[(smm, 1)] if sp_ else [])
            hv = smm.v
            vh_free[slot] = (smm, hv)
            P.op("scalar", lambda e: e.copy(psv[:, :], pv[0:64, :]),
                 waits=[(smm, hv), (evv, prev_rz)], incs=[(evs, 1)])
            prev_psv = evs.v
            P.op("vector", lambda e: e.tensor_scalar(
                zrow[0:1, 0:S], pv[64:65, :], 1.0, 1e-12, op0=ALU.mult, op1=ALU.add),
                waits=[(smm, hv)], incs=[(evv, 1)])
            P.op("vector", lambda e: e.reciprocal(zrow[0:1, S:2 * S], zrow[0:1, 0:S]),
                 waits=P.w(evv), incs=[(evv, 1)])
            P.op("tensor", lambda e: e.matmul(
                aux[0:64, S:2 * S], c["ones_row_f"][0:1, 0:64], zrow[0:1, S:2 * S],
                start=True, stop=True),
                waits=P.w(evv) + [(evv, ctx.get("_a1_zbg", 0))], incs=[(smm, 1)])
            P.op("vector", lambda e, ot=ot, oo=oo: e.tensor_mul(
                a1o[oo:oo + 64, ot * S:(ot + 1) * S], psv[:, :],
                aux[0:64, S:2 * S]),
                waits=P.w(smm) + [(evs, prev_psv)], incs=[(evv, 1)])
            ctx["_a1_zbg"] = evv.v
            prev_rz = evv.v
        attn_done = evv.v
        P.op("tensor", None, waits=[(evv, attn_done)])
        ctx["_gemm_guard"] = [(evs, evs.v), (evv, evv.v)]
        xog = [(sout, ctx.get("_xo_guard", 0))]

        def ev_wo(mt, pap, gv):
            P.op("vector", lambda e, mt=mt, pap=pap: e.tensor_add(
                xo[:, mt * S:(mt + 1) * S], xs[:, mt * S:(mt + 1) * S], pap),
                waits=[(smm, gv)] + xog, incs=[(evv, 1)])
        emit_gemm(ctx, wb["a1_woT"], NA, NA,
                  lambda k: a1o[:, k * S:(k + 1) * S], ev_wo,
                  [(evv, attn_done)], prep_wait=pd)
        ctx["_xs_guard"] = [(evv, evv.v)]
        ctx["_a1_qg"] = smm.v
        ddst = dst.rearrange("(a p) t -> p a t", p=128)[:, :, s * S:(s + 1) * S]
        P.op("sync", lambda e, ddst=ddst: e.dma_start(
            ddst, xo[:, :].rearrange("p (a t) -> p a t", a=NA)),
            waits=P.w(evv), incs=[(sout, 16)])
        ctx["_xo_guard"] = sout.v
        ctx["_ln_rows_g"] = evv.v


def make_in_map(inputs, core, stages=ALL_STAGES):
    f = np.ascontiguousarray
    m = {}
    m["xT"] = f(np.asarray(inputs["L0_attn_x"][core]).T)
    m["cconst"] = np.tile(np.array(REG_CONSTS, dtype=np.float32), (128, 1))
    m["ones_row"] = np.ones((1, 128 + S), dtype=np.float32)
    m["identc"] = np.eye(128, dtype=np.float32)
    need = set(st[:4] if st.startswith("ffn") else st for st in stages)
    if "ffn0" in need:
        m["gate"] = f(np.asarray(inputs["L0_ffn_gate"], dtype=np.float32))
        esel = np.zeros((9, 9 * 128), dtype=np.float32)
        for e in range(9):
            esel[e, e * 128:(e + 1) * 128] = 1.0
        m["esel"] = esel
        m["moe_w1"] = f(np.asarray(inputs["L0_ffn_W1"], dtype=np.float32))
        m["moe_b1"] = f(np.asarray(inputs["L0_ffn_b1"], dtype=np.float32)
                        .reshape(8, 16, 128).transpose(2, 0, 1).reshape(128, 128))
        m["moe_w2"] = f(np.asarray(inputs["L0_ffn_W2"], dtype=np.float32))
        m["moe_b2"] = f(np.asarray(inputs["L0_ffn_b2"], dtype=np.float32))
    if "attn0" in need:
        m["a0_wq"] = f(np.asarray(inputs["L0_attn_Wq"], dtype=np.float32))
        m["a0_wkv"] = f(np.asarray(inputs["L0_attn_Wkv"], dtype=np.float32))
        m["a0_wk"] = f(np.asarray(inputs["L0_attn_Wk"], dtype=np.float32))
        m["a0_wv"] = f(np.asarray(inputs["L0_attn_Wv"], dtype=np.float32))
        wo = np.asarray(inputs["L0_attn_Wo"], dtype=np.float32)
        wop = np.zeros((768, D), dtype=np.float32)
        for h in range(16):
            pt, off = h // 3, (h % 3) * 32
            wop[pt * 128 + off: pt * 128 + off + 32, :] = wo[h * 64: h * 64 + 32, :]
        m["a0_wo"] = wop
        m["a0_qg"] = f(np.tile(np.asarray(inputs["L0_attn_q_g"], dtype=np.float32)[:32], 4)[:, None])
        m["a0_kg"] = f(np.tile(np.asarray(inputs["L0_attn_k_g"], dtype=np.float32), 4)[:, None])
        qo = np.zeros((D, 16), dtype=np.float32)
        for h in range(16):
            qo[h * 64:(h + 1) * 64, h] = 1.0
        m["a0_qones"] = qo
        ko = np.zeros((512, 16), dtype=np.float32)
        for h in range(16):
            ko[h * 32:(h + 1) * 32, h] = 1.0
        m["a0_kones"] = ko
        for kind, nb in (("q", 11), ("k", 8)):
            sel = np.zeros((128, nb * 128), dtype=np.float32)
            for bi, (pt, a, lst) in enumerate(_a0_selmap(kind)):
                for (soff, doff) in lst:
                    for r in range(32):
                        sel[soff + r, bi * 128 + doff + r] = 1.0
            m["a0_" + kind + "sel"] = sel
        hot = np.zeros((16, 768), dtype=np.float32)
        for h in range(16):
            pt, off = h // 3, (h % 3) * 32
            hot[h, pt * 128 + off: pt * 128 + off + 32] = 1.0
        m["a0_hot"] = hot
        dm = np.zeros((128, 4 * S), dtype=np.float32)
        for j in range(4):
            tk = j * 128 + np.arange(128)[:, None]
            tq = np.arange(S)[None, :]
            dm[:, j * S:(j + 1) * S] = (tk <= tq).astype(np.float32)
        m["dmask"] = dm
    if "attn1" in need:
        def pad_a1(w):
            wt = np.asarray(w, dtype=np.float32).T
            out = np.zeros((D, 768), dtype=np.float32)
            for h in range(16):
                pt, off = h // 3, (h % 3) * 32
                out[:, pt * 128 + off: pt * 128 + off + 16] = wt[:, h * 16:(h + 1) * 16]
            return out
        m["a1_wqT"] = pad_a1(inputs["L1_attn_Wq"])
        m["a1_wkT"] = pad_a1(inputs["L1_attn_Wk"])
        m["a1_wvT"] = f(np.asarray(inputs["L1_attn_Wv"], dtype=np.float32).T)
        m["a1_woT"] = f(np.asarray(inputs["L1_attn_Wo"], dtype=np.float32).T)
        hm = np.zeros((128, 5 * S), dtype=np.float32)
        for j in range(4):
            tk = j * 128 + np.arange(128)[:, None]
            tq = np.arange(S)[None, :]
            hm[:, j * S:(j + 1) * S] = 0.5 * (tk <= tq)
        hm[:, 4 * S:] = 0.5
        m["hmask"] = hm
    if "ffn1" in need:
        m["f1_w1T"] = f(np.asarray(inputs["L1_ffn_w1_w"], dtype=np.float32).T)
        m["f1_w2T"] = f(np.asarray(inputs["L1_ffn_w2_w"], dtype=np.float32).T)
        m["f1_h1T"] = f(np.asarray(inputs["L1_ffn_h1_w"], dtype=np.float32).T)
        m["f1_h2T"] = f(np.asarray(inputs["L1_ffn_h2_w"], dtype=np.float32).T)
        m["f1_w3T"] = f(np.asarray(inputs["L1_ffn_w3_w"], dtype=np.float32).T)
        m["f1_s1"] = f(np.asarray(inputs["L1_ffn_w1_s"], dtype=np.float32).reshape(32, 128).T)
        m["f1_s2"] = f(np.asarray(inputs["L1_ffn_w2_s"], dtype=np.float32).reshape(32, 128).T)
        m["f1_sh1"] = f(np.asarray(inputs["L1_ffn_h1_s"], dtype=np.float32).reshape(32, 128).T)
        m["f1_sh2"] = f(np.asarray(inputs["L1_ffn_h2_s"], dtype=np.float32).reshape(32, 128).T)
        m["f1_s3"] = f(np.asarray(inputs["L1_ffn_w3_s"], dtype=np.float32).reshape(8, 128).T)
    return m


def kernel(**inputs):
    nc = build_nc(ALL_STAGES)
    in_maps = [make_in_map(inputs, cid) for cid in range(8)]
    res = run_bass_kernel_spmd(nc, in_maps, core_ids=list(range(8)))
    out = np.stack([np.asarray(r["outT"]).T for r in res.results], axis=0)
    return out.astype(np.float32)


# version 07 (final)
# speedup vs baseline: 1.2598x; 1.1752x over previous
import numpy as np
import concourse.bass as bass
import concourse.mybir as mybir
from concourse.bass_utils import run_bass_kernel_spmd
from concourse.bass_core_rust import IndirectOffsetOnAxis
from contextlib import ExitStack

F32 = mybir.dt.float32
BF16 = mybir.dt.bfloat16
I32 = mybir.dt.int32
AF = mybir.ActivationFunctionType
ALU = mybir.AluOpType
AXX = mybir.AxisListType.X

T = 4096
D = 1024
S = 512
NSTRIP = T // S
NA = D // 128

ALL_STAGES = ("ffn0a", "attn0", "ffn0b", "ffn1a", "attn1", "ffn1b")

MOE_PHASE = 3
MOE_NOSC = False
MOE_NOXND = False

CAP = 1536
NSLOT = 8 * CAP
IMROWS = NSLOT + 128

REG_CONSTS = [-12.0, 0.125, 1.0, 0.25, 0.5]


class Sem:
    __slots__ = ("h", "v", "name")

    def __init__(self, nc, es, name):
        self.h = es.enter_context(nc.semaphore(name))
        self.v = 0
        self.name = name


class Sched:

    def __init__(self):
        self.ops = []

    def op(self, eng, fn=None, waits=(), incs=()):
        rincs = []
        for (sem, k) in incs:
            sem.v += k
            rincs.append((sem, k))
        self.ops.append((eng, fn, [(s, int(v)) for (s, v) in waits], rincs))

    def w(self, *sems):
        return [(s, s.v) for s in sems]

    def emit(self, eng_name, eng):
        seen = {}
        for e, fn, waits, incs in self.ops:
            if e != eng_name:
                continue
            for (sem, val) in waits:
                if val > 0 and seen.get(sem.name, -1) < val:
                    eng.wait_ge(sem.h, val)
                    seen[sem.name] = val
            inst = fn(eng) if fn is not None else None
            if incs:
                assert inst is not None and len(incs) == 1, \
                    "each sem inc must ride a real instruction"
                inst.then_inc(incs[0][0].h, incs[0][1])

    def emit_all(self, block):
        for name in ("sync", "tensor", "scalar", "vector", "gpsimd"):
            getattr(block, name)(lambda eng, n=name: self.emit(n, eng))


class WRing:

    def __init__(self, nc, es, P, name, nslots, slot_cols):
        self.sb = es.enter_context(nc.sbuf_tensor(name, [128, nslots * slot_cols], BF16))
        self.nslots = nslots
        self.cols = slot_cols
        self.P = P
        self.sw = [Sem(nc, es, f"{name}_w{i}") for i in range(nslots)]
        self.freed_by = {}
        self.seq = 0

    def load(self, dram_ap, kcnt, mcols, extra_waits=()):
        ncols = kcnt * mcols
        assert ncols <= self.cols
        seq = self.seq
        self.seq += 1
        slot = seq % self.nslots
        ap = self.sb[:, slot * self.cols: slot * self.cols + ncols]
        out = ap.rearrange("p (kt m) -> p kt m", m=mcols)
        waits = list(extra_waits)
        if seq >= self.nslots:
            waits.append(self.freed_by.pop(seq - self.nslots))
        self.P.op("sync", lambda e, out=out, d=dram_ap: e.dma_start(out, d),
                  waits=waits, incs=[(self.sw[slot], 16)])
        return ap, seq

    def use_wait(self, seq):
        return [(self.sw[seq % self.nslots], 16 * (seq // self.nslots + 1))]


def build_nc(stages=ALL_STAGES):
    nc = bass.Bass("TRN2", target_bir_lowering=False, debug=False)
    es = ExitStack()
    P = Sched()

    din = {}

    def dri(name, shape, dt=F32):
        din[name] = nc.dram_tensor(name, list(shape), dt, kind="ExternalInput").ap()
        return din[name]

    xT = dri("xT", (D, T))
    dri("cconst", (128, len(REG_CONSTS)))
    dri("ones_row", (1, 128 + S))
    dri("identc", (128, 128))

    need = set()
    for st in stages:
        need.add(st[:4] if st.startswith("ffn") else st)
    if "ffn0" in need:
        dri("gate", (D, 8))
        dri("moe_w1", (8, D, 2048))
        dri("moe_b1", (128, 8 * 16))
        dri("moe_w2", (8, 2048, D))
        dri("moe_b2", (128, 64))
        dri("moe_tri", (128, 128))
        dri("moe_iot", (128, 65))
    if "attn0" in need:
        dri("a0_wq", (D, D))
        dri("a0_wkv", (D, 128))
        dri("a0_wk", (128, 512))
        dri("a0_wv", (128, 512))
        dri("a0_wo", (768, D))
        dri("a0_qg", (128, 1))
        dri("a0_kg", (128, 1))
        dri("a0_qones", (D, 16))
        dri("a0_kones", (512, 16))
        dri("a0_qsel", (128, 11 * 128))
        dri("a0_ksel", (128, 8 * 128))
        dri("a0_hot", (16, 768))
        dri("dmask", (128, 4 * S))
    if "attn1" in need:
        dri("a1_wqT", (D, 768))
        dri("a1_wkT", (D, 768))
        dri("a1_wvT", (D, D))
        dri("a1_woT", (D, D))
        dri("hmask", (128, 5 * S))
    if "ffn1" in need:
        dri("f1_w1T", (D, 4096))
        dri("f1_w2T", (D, 4096))
        dri("f1_h1T", (4096, 4096))
        dri("f1_h2T", (4096, 4096))
        dri("f1_w3T", (4096, D))
        dri("f1_s1", (128, 32))
        dri("f1_s2", (128, 32))
        dri("f1_sh1", (128, 32))
        dri("f1_sh2", (128, 32))
        dri("f1_s3", (128, 8))

    xresA = nc.dram_tensor("xresA", [D, T], F32, kind="Internal").ap()
    xresB = nc.dram_tensor("xresB", [D, T], F32, kind="Internal").ap()
    outT = nc.dram_tensor("outT", [D, T], F32, kind="ExternalOutput").ap()

    wb = {}

    def drw(name, shape):
        wb[name] = nc.dram_tensor(name + "_b", list(shape), BF16, kind="Internal").ap()
        return wb[name]

    if "ffn0" in need:
        drw("gate", (D, 8))
        drw("moe_w1", (8 * D, 2048))
        drw("moe_w2", (8 * 2048, D))
        ctx_xnd = nc.dram_tensor("xnd", [T, D], BF16, kind="Internal").ap()
        ctx_idxmap = nc.dram_tensor("idxmap", [IMROWS, 2], I32, kind="Internal").ap()
        ctx_yslot = nc.dram_tensor("yslot", [IMROWS, D], BF16, kind="Internal").ap()
    if "attn0" in need:
        for n, sh in [("a0_wq", (D, D)), ("a0_wkv", (D, 128)), ("a0_wk", (128, 512)),
                      ("a0_wv", (128, 512)), ("a0_wo", (768, D))]:
            drw(n, sh)
        ctx_vdram0 = nc.dram_tensor("vdram0", [T, 16 * 33], BF16, kind="Internal").ap()
    if "attn1" in need:
        for n, sh in [("a1_wqT", (D, 768)), ("a1_wkT", (D, 768)),
                      ("a1_wvT", (D, D)), ("a1_woT", (D, D))]:
            drw(n, sh)
        ctx_vdram = nc.dram_tensor("vdram", [T, 16 * 65], BF16, kind="Internal").ap()
    if "ffn1" in need:
        for n, sh in [("f1_w1T", (D, 4096)), ("f1_w2T", (D, 4096)),
                      ("f1_h1T", (4096, 4096)), ("f1_h2T", (4096, 4096)),
                      ("f1_w3T", (4096, D))]:
            drw(n, sh)

    def sb(name, shape, dt=F32):
        return es.enter_context(nc.sbuf_tensor(name, list(shape), dt))

    def psum(name, shape, dt=F32):
        return es.enter_context(nc.psum_tensor(name, list(shape), dt))

    c = {}
    c["cc"] = sb("cc", (128, len(REG_CONSTS)))
    for i, v in enumerate(REG_CONSTS):
        nc.const_aps.register(F32, v, c["cc"][:, i:i + 1])
    c["ones_col_f"] = sb("ones_col_f", (128, 1))
    c["ones_col_b"] = sb("ones_col_b", (128, 1), BF16)
    c["ones_row_f"] = sb("ones_row_f", (1, 128))
    c["ones_row_b"] = sb("ones_row_b", (1, S), BF16)
    c["ident_f"] = sb("ident_f", (128, 128))
    c["ones_mat"] = sb("ones_mat", (128, S), BF16)

    sboot = Sem(nc, es, "boot")
    sboot2 = Sem(nc, es, "boot2")
    sboot3 = Sem(nc, es, "boot3")

    P.op("gpsimd", lambda e: e.dma_start(c["cc"][:, :], din["cconst"]), incs=[(sboot, 16)])
    orow = din["ones_row"]
    P.op("gpsimd", lambda e: e.dma_start(c["ones_col_f"][:, :],
                                         orow[0:1, 0:128].rearrange("a p -> p a")), incs=[(sboot, 16)])
    P.op("gpsimd", lambda e: e.dma_start(c["ones_col_b"][:, :],
                                         orow[0:1, 0:128].rearrange("a p -> p a")), incs=[(sboot, 16)])
    P.op("gpsimd", lambda e: e.dma_start(c["ones_row_f"][:, :], orow[0:1, 0:128]), incs=[(sboot, 16)])
    P.op("gpsimd", lambda e: e.dma_start(c["ones_row_b"][:, :], orow[0:1, 128:128 + S]), incs=[(sboot, 16)])
    P.op("gpsimd", lambda e: e.dma_start(c["ident_f"][:, :], din["identc"]), incs=[(sboot, 16)])
    P.op("gpsimd", lambda e: e.memset(c["ones_mat"][:, :], 1.0), incs=[(sboot2, 1)])
    c["zeros_f"] = sb("zeros_f", (128, S))
    P.op("gpsimd", lambda e: e.memset(c["zeros_f"][:, :], 0.0), incs=[(sboot2, 1)])
    zrdy = (sboot2, sboot2.v)
    if "ffn0" in need:
        c["ident_b"] = sb("ident_b", (128, 128), BF16)
        P.op("gpsimd", lambda e: e.dma_start(c["ident_b"][:, :], din["identc"]),
             incs=[(sboot, 16)])
        P.op("gpsimd", lambda e: e.dma_start(
            ctx_yslot[NSLOT:IMROWS, :], c["zeros_f"].ap().bitcast(BF16)[:, 0:D]),
            waits=[zrdy], incs=[(sboot3, 16)])
    boot_n = sboot.v
    boot2_n = sboot2.v
    boot3_n = sboot3.v

    sres = Sem(nc, es, "resinit")

    prep_done = {}
    order = []
    for st in stages:
        key = st[:4] if st.startswith("ffn") else st
        if key not in order:
            order.append(key)
    group = {"ffn0": ["gate", "moe_w1", "moe_w2"],
             "attn0": ["a0_wq", "a0_wkv", "a0_wk", "a0_wv", "a0_wo"],
             "attn1": ["a1_wqT", "a1_wkT", "a1_wvT", "a1_woT"],
             "ffn1": ["f1_w1T", "f1_w2T", "f1_h1T", "f1_h2T", "f1_w3T"]}
    for key in order:
        sp = Sem(nc, es, "prep_" + key)
        for n in group[key]:
            src = din[n]
            if len(src.shape) == 3:
                src = src.rearrange("e k m -> (e k) m")
            P.op("gpsimd", lambda e, n=n, src=src: e.dma_start(wb[n], src),
                 incs=[(sp, 16)])
        prep_done[key] = (sp, sp.v)

    for e in ("tensor", "scalar", "vector", "sync"):
        P.op(e, None, waits=[(sboot, boot_n), (sboot2, boot2_n), (sboot3, boot3_n)])

    xs = sb("xs", (128, 2 * NA * S))
    x2b = sb("x2b", (128, NA * S), BF16)
    xn = sb("xn", (128, 2 * NA * S), BF16)
    xo = sb("xo", (128, NA * S))
    rws = sb("rws", (1, 4 * S))
    paux = psum("paux", (128, 2 * S))
    abp = psum("abp", (128, 2 * S))
    pp = psum("pp", (128, 4 * S))
    stpA = paux[0:1, 0:S]
    stpB = paux[0:1, S:2 * S]

    ring = WRing(nc, es, P, "wring", 2, 2048)

    sems = {n: Sem(nc, es, n) for n in
            ("sx", "sx1", "evv", "evs", "smm", "sout")}

    ctx = dict(nc=nc, P=P, c=c, din=din, wb=wb, outT=outT,
               vdram=(ctx_vdram if "attn1" in need else None),
               vdram0=(ctx_vdram0 if "attn0" in need else None),
               xnd=(ctx_xnd if "ffn0" in need else None),
               idxmap=(ctx_idxmap if "ffn0" in need else None),
               yslot=(ctx_yslot if "ffn0" in need else None),
               zrdy=zrdy,
               xs=xs, x2b=x2b, xn=xn, xo=xo, rws=rws,
               stpA=stpA, stpB=stpB, abp=abp, pp=pp, aux=paux, ring=ring,
               sems=sems, sres=sres, prep_done=prep_done, es=es)
    ctx["_res_strip"] = [[] for _ in range(NSTRIP)]

    with es:
        block = es.enter_context(nc.Block())
        P.emit_all(block)
        resbufs = [xresA, xresB]
        prev_es = None
        for si, st in enumerate(stages):
            is_last = (si == len(stages) - 1)
            src = xT if si == 0 else resbufs[(si + 1) % 2]
            dst = outT if is_last else resbufs[si % 2]
            if prev_es is not None:
                prev_es.close()
                prev_es = None
            Pb = Sched()
            ctx["P"] = Pb
            ring.P = Pb
            ln0 = LNPipe(ctx, src)
            ln0.pre(0)
            ctx["ln0"] = ln0
            if si > 0:
                rv = [(sems["smm"], sems["smm"].v), (sems["evv"], sems["evv"].v),
                      (sems["evs"], sems["evs"].v)]
                for e in ("tensor", "scalar", "vector", "gpsimd", "sync"):
                    Pb.op(e, None, waits=list(rv))
            Pb.emit_all(block)
            P = Sched()
            ctx["P"] = P
            ring.P = P
            stage_es = ExitStack()
            ctx["stage_es"] = stage_es
            if st in ("ffn1a", "ffn1b"):
                emit_ffn1(ctx, src, dst)
            elif st in ("ffn0a", "ffn0b"):
                emit_ffn0(ctx, src, dst)
            elif st == "attn0":
                emit_attn0(ctx, src, dst)
            elif st == "attn1":
                emit_attn1(ctx, src, dst)
            else:
                raise ValueError(st)
            P.emit_all(block)
            if prev_es is not None:
                prev_es.close()
            prev_es = stage_es

        if prev_es is not None:
            prev_es.close()
        P = Sched()
        P.op("sync", None, waits=P.w(sems["sout"]))
        P.emit_all(block)
    return nc


class LNPipe:
    def __init__(self, ctx, srcdram):
        self.ctx = ctx
        self.src = srcdram
        self.state = {}
        ctx.setdefault("_xs_guard_par", {0: [], 1: []})
        ctx.setdefault("_xn_guard_par", {0: 0, 1: 0})

    def pre(self, s, extra_x_waits=()):
        ctx = self.ctx
        P = ctx["P"]
        c = ctx["c"]
        xs, x2b, rws = ctx["xs"], ctx["x2b"], ctx["rws"]
        stpA, stpB, abp = ctx["stpA"], ctx["stpB"], ctx["abp"]
        sems = ctx["sems"]
        sx, evv, evs, smm = sems["sx"], sems["evv"], sems["evs"], sems["smm"]
        sres = ctx["sres"]
        par = s % 2
        sxp = sx if par == 0 else sems["sx1"]
        xsp = xs[:, par * NA * S:(par + 1) * NA * S]

        src = self.src.rearrange("(a p) t -> p a t", p=128)[:, :, s * S:(s + 1) * S]
        P.op("gpsimd", lambda e, src=src, xsp=xsp: e.dma_start(
            xsp.rearrange("p (a t) -> p a t", a=NA), src),
            waits=list(ctx["_res_strip"][s]) +
                  list(ctx["_xs_guard_par"][par]) + list(extra_x_waits),
            incs=[(sxp, 16)])
        xw = [(sxp, sxp.v)]

        for a_ in range(NA):
            P.op("scalar", lambda e, a=a_, xsp=xsp: e.activation(
                x2b[:, a * S:(a + 1) * S], xsp[:, a * S:(a + 1) * S], AF.Square),
                waits=xw + [(smm, ctx.get("_ln_x2b_g", 0))], incs=[(evs, 1)])
        sqv = evs.v
        rowg = [(evv, ctx.get("_ln_rows_g", 0))]
        for a_ in range(NA):
            P.op("tensor", lambda e, a=a_, xsp=xsp: e.matmul(
                stpA[0:1, :], c["ones_col_f"][:, :], xsp[:, a * S:(a + 1) * S],
                start=(a == 0), stop=(a == NA - 1)),
                waits=xw + rowg, incs=[(smm, 1)] if a_ == NA - 1 else [])
        for a_ in range(NA):
            P.op("tensor", lambda e, a=a_: e.matmul(
                stpB[0:1, :], c["ones_col_b"][:, :], x2b[:, a * S:(a + 1) * S],
                start=(a == 0), stop=(a == NA - 1)),
                waits=[(evs, sqv)] + rowg, incs=[(smm, 1)] if a_ == NA - 1 else [])
        mst = smm.v
        ctx["_ln_x2b_g"] = mst

        m = rws[0:1, 0:S]; arow = rws[0:1, S:2 * S]
        brow = rws[0:1, 2 * S:3 * S]; trow = rws[0:1, 3 * S:4 * S]
        P.op("vector", lambda e: e.tensor_scalar(m, stpA[0:1, :], 1.0 / D, None,
                                                 op0=ALU.mult),
             waits=[(smm, mst)], incs=[(evv, 1)])
        P.op("vector", lambda e: e.tensor_scalar(trow, stpB[0:1, :], 1.0 / D, 1e-5,
                                                 op0=ALU.mult, op1=ALU.add),
             incs=[(evv, 1)])
        ctx["_ln_rows_g"] = evv.v
        P.op("vector", lambda e: e.scalar_tensor_tensor(
            brow, m, -1.0, m, op0=ALU.mult, op1=ALU.mult),
            waits=P.w(evv), incs=[(evv, 1)])
        P.op("vector", lambda e: e.tensor_add(trow, trow, brow),
             waits=P.w(evv), incs=[(evv, 1)])
        P.op("vector", lambda e: e.reciprocal(brow, trow),
             waits=P.w(evv), incs=[(evv, 1)])
        P.op("scalar", lambda e: e.activation(arow, brow, AF.Sqrt),
             waits=P.w(evv), incs=[(evs, 1)])
        P.op("vector", lambda e: e.scalar_tensor_tensor(
            brow, m, -1.0, arow, op0=ALU.mult, op1=ALU.mult),
            waits=P.w(evs), incs=[(evv, 1)])
        evrow = evv.v
        P.op("tensor", lambda e: e.matmul(abp[:, 0:S], c["ones_row_f"][:, :], arow,
                                          start=True, stop=True),
             waits=[(evv, evrow), (evs, evs.v)])
        P.op("tensor", lambda e: e.matmul(abp[:, S:2 * S], c["ones_row_f"][:, :],
                                          brow, start=True, stop=True),
             incs=[(smm, 1)])
        self.state[s] = (smm.v, par, xsp)

    def apply(self, s):
        ctx = self.ctx
        P = ctx["P"]
        sems = ctx["sems"]
        evv, smm = sems["evv"], sems["smm"]
        abp, xn = ctx["abp"], ctx["xn"]
        mbc, par, xsp = self.state.pop(s)
        xh = xn[:, par * NA * S:(par + 1) * NA * S]
        for a_ in range(NA):
            P.op("vector", lambda e, a=a_, xh=xh, xsp=xsp: e.tensor_mul(
                xh[:, a * S:(a + 1) * S], xsp[:, a * S:(a + 1) * S], abp[:, 0:S]),
                waits=[(smm, mbc), (smm, ctx["_xn_guard_par"][par])],
                incs=[(evv, 1)])
            P.op("vector", lambda e, a=a_, xh=xh: e.tensor_add(
                xh[:, a * S:(a + 1) * S], xh[:, a * S:(a + 1) * S], abp[:, S:2 * S]),
                waits=P.w(evv), incs=[(evv, 1)])
        return xh, xsp, par, [(evv, evv.v)]


def emit_gemm(ctx, wdram, nk, nm, src_fn, evac, src_waits, extra_mm=None,
              prep_wait=()):
    P = ctx["P"]
    ring, pp = ctx["ring"], ctx["pp"]
    sems = ctx["sems"]
    smm, evv, evs = sems["smm"], sems["evv"], sems["evs"]
    wre = wdram.rearrange("(kt p) m -> p kt m", p=128)
    kb = 8
    pg = {0: list(ctx.get("_gemm_guard", [])), 1: list(ctx.get("_gemm_guard", []))}
    blk = 0
    for mb in range(0, nm, 2):
        par = blk % 2
        blk += 1
        mcnt = min(2, nm - mb)
        mcols = mcnt * 128
        batches = []
        for k0 in range(0, nk, kb):
            kcnt = min(kb, nk - k0)
            batches.append((k0, kcnt))

        def do_load(i):
            k0, kcnt = batches[i]
            dap = wre[:, k0:k0 + kcnt, mb * 128: mb * 128 + mcols]
            return ring.load(dap, kcnt, mcols, extra_waits=prep_wait)

        loaded = {0: do_load(0)}
        if len(batches) > 1:
            loaded[1] = do_load(1)
        guard = pg[par]
        first = True
        for i, (k0, kcnt) in enumerate(batches):
            ap, seq = loaded.pop(i)
            for ki in range(kcnt):
                k = k0 + ki
                for mi in range(mcnt):
                    lhs = ap[:, ki * mcols + mi * 128: ki * mcols + (mi + 1) * 128]
                    last_of_slot = (ki == kcnt - 1 and mi == mcnt - 1)
                    stop = (k == nk - 1 and extra_mm is None)
                    bank = 2 * par + mi
                    P.op("tensor", lambda e, lhs=lhs, k=k, bank=bank, stop=stop:
                         e.matmul(pp[:, bank * S:(bank + 1) * S], lhs, src_fn(k),
                                  start=(k == 0), stop=stop),
                         waits=(ring.use_wait(seq) +
                                (src_waits + guard if first else [])),
                         incs=[(smm, 1)] if last_of_slot else [])
                    first = False
            ring.freed_by[seq] = (smm, smm.v)
            if i + 2 < len(batches):
                loaded[i + 2] = do_load(i + 2)
        if extra_mm is not None:
            for mi in range(mcnt):
                bank = 2 * par + mi
                extra_mm(mb + mi, pp[:, bank * S:(bank + 1) * S],
                         last=(mi == mcnt - 1))
        gv = smm.v
        for mi in range(mcnt):
            bank = 2 * par + mi
            evac(mb + mi, pp[:, bank * S:(bank + 1) * S], gv)
        pg[par] = [(evv, evv.v), (evs, evs.v)]
    ctx["_gemm_guard"] = [(evv, evv.v), (evs, evs.v)]
    return


def emit_ffn1(ctx, src, dst):
    P, wb, din = ctx["P"], ctx["wb"], ctx["din"]
    sems = ctx["sems"]
    evv, evs, smm, sout = sems["evv"], sems["evs"], sems["smm"], sems["sout"]
    xs, xo, xn = ctx["xs"], ctx["xo"], ctx["xn"]
    pd = [ctx["prep_done"]["ffn1"]]
    tag = str(smm.v)
    hA = ctx["stage_es"].enter_context(
        ctx["nc"].sbuf_tensor("hA" + tag, [128, 32 * S], BF16))
    hB = ctx["stage_es"].enter_context(
        ctx["nc"].sbuf_tensor("hB" + tag, [128, 32 * S], BF16))

    sc = ctx["stage_es"].enter_context(
        ctx["nc"].sbuf_tensor("f1sc" + tag, [128, 32 * 4 + 8 + 8], F32))
    sld = Sem(ctx["nc"], ctx["es"], "f1scl" + tag)
    for i, n in enumerate(("f1_s1", "f1_s2", "f1_sh1", "f1_sh2")):
        P.op("gpsimd", lambda e, i=i, n=n: e.dma_start(sc[:, i * 32:(i + 1) * 32], din[n]),
             incs=[(sld, 16)])
    P.op("gpsimd", lambda e: e.dma_start(sc[:, 128:136], din["f1_s3"]), incs=[(sld, 16)])
    P.op("vector", lambda e: e.tensor_scalar(sc[:, 136:144], sc[:, 128:136],
                                             0.5, None, op0=ALU.mult),
         waits=P.w(sld), incs=[(evv, 1)])
    screato = [(evv, evv.v)]
    s1 = lambda mt: sc[:, 0 * 32 + mt: 0 * 32 + mt + 1]
    s2 = lambda mt: sc[:, 1 * 32 + mt: 1 * 32 + mt + 1]
    sh1 = lambda mt: sc[:, 2 * 32 + mt: 2 * 32 + mt + 1]
    sh2 = lambda mt: sc[:, 3 * 32 + mt: 3 * 32 + mt + 1]
    s3h = lambda mt: sc[:, 136 + mt: 136 + mt + 1]

    nsr = [[] for _ in range(NSTRIP)]
    ln = ctx.pop("ln0")
    for s in range(NSTRIP):
        xh, xsp, par, xnw = ln.apply(s)
        xsrc = lambda k, xh=xh: xh[:, k * S:(k + 1) * S]

        def ev_w1(mt, pap, gv):
            P.op("scalar", lambda e, mt=mt, pap=pap: e.activation(
                hA[:, mt * S:(mt + 1) * S], pap, AF.Silu, scale=s1(mt)),
                waits=[(smm, gv)] + screato, incs=[(evs, 1)])
        emit_gemm(ctx, wb["f1_w1T"], NA, 32, xsrc, ev_w1, xnw, prep_wait=pd)
        w1_done = evs.v
        if s + 1 < NSTRIP:
            ln.pre(s + 1)

        def ev_w2(mt, pap, gv):
            P.op("vector", lambda e, mt=mt, pap=pap: e.scalar_tensor_tensor(
                hB[:, mt * S:(mt + 1) * S], pap, s2(mt), hA[:, mt * S:(mt + 1) * S],
                op0=ALU.mult, op1=ALU.mult),
                waits=[(smm, gv), (evs, w1_done)], incs=[(evv, 1)])
        emit_gemm(ctx, wb["f1_w2T"], NA, 32, xsrc, ev_w2, xnw, prep_wait=pd)
        ctx["_xn_guard_par"][par] = smm.v
        hsrc = lambda k: hB[:, k * S:(k + 1) * S]
        hB_done = [(evv, evv.v)]

        def ev_h1(mt, pap, gv):
            P.op("scalar", lambda e, mt=mt, pap=pap: e.activation(
                hA[:, mt * S:(mt + 1) * S], pap, AF.Gelu, scale=sh1(mt)),
                waits=[(smm, gv)], incs=[(evs, 1)])
        emit_gemm(ctx, wb["f1_h1T"], 32, 32, hsrc, ev_h1, hB_done, prep_wait=pd)
        hA_done = [(evs, evs.v)]
        hsrcA = lambda k: hA[:, k * S:(k + 1) * S]

        def ev_h2(mt, pap, gv):
            P.op("scalar", lambda e, mt=mt, pap=pap: e.activation(
                hB[:, mt * S:(mt + 1) * S], pap, AF.Gelu, scale=sh2(mt)),
                waits=[(smm, gv)], incs=[(evs, 1)])
        emit_gemm(ctx, wb["f1_h2T"], 32, 32, hsrcA, ev_h2, hA_done, prep_wait=pd)
        hB2_done = [(evs, evs.v)]

        xog = [(sout, ctx.get("_xo_guard", 0))]

        def ev_w3(mt, pap, gv, xsp=xsp):
            P.op("vector", lambda e, mt=mt, pap=pap: e.scalar_tensor_tensor(
                xo[:, mt * S:(mt + 1) * S], pap, s3h(mt), xsp[:, mt * S:(mt + 1) * S],
                op0=ALU.mult, op1=ALU.add),
                waits=[(smm, gv)] + xog, incs=[(evv, 1)])
        emit_gemm(ctx, wb["f1_w3T"], 32, NA, hsrc, ev_w3, hB2_done, prep_wait=pd)
        ctx["_xs_guard_par"][par] = [(evv, evv.v)]
        ddst = dst.rearrange("(a p) t -> p a t", p=128)[:, :, s * S:(s + 1) * S]
        P.op("sync", lambda e, ddst=ddst: e.dma_start(
            ddst, xo[:, :].rearrange("p (a t) -> p a t", a=NA)),
            waits=P.w(evv), incs=[(sout, 16)])
        ctx["_xo_guard"] = sout.v
        nsr[s] = [(sout, sout.v)]
    ctx["_res_strip"] = nsr


def _moe_dummy_out(ctx, src, dst):
    P, sems = ctx["P"], ctx["sems"]
    evv, sout = sems["evv"], sems["sout"]
    xs, xo = ctx["xs"], ctx["xo"]
    sxp = {0: sems["sx"], 1: sems["sx1"]}
    nsr = [[] for _ in range(NSTRIP)]
    for s in range(NSTRIP):
        par = s % 2
        xsp = xs[:, par * NA * S:(par + 1) * NA * S]
        srcv = src.rearrange("(a p) t -> p a t", p=128)[:, :, s * S:(s + 1) * S]
        P.op("gpsimd", lambda e, srcv=srcv, xsp=xsp: e.dma_start(
            xsp.rearrange("p (a t) -> p a t", a=NA), srcv),
            waits=list(ctx["_xs_guard_par"][par]), incs=[(sxp[par], 16)])
        xog = [(sout, ctx.get("_xo_guard", 0))]
        for a in range(NA):
            P.op("vector", lambda e, a=a, xsp=xsp: e.tensor_copy(
                xo[:, a * S:(a + 1) * S], xsp[:, a * S:(a + 1) * S]),
                waits=[(sxp[par], sxp[par].v)] + xog, incs=[(evv, 1)])
        ctx["_xs_guard_par"][par] = [(evv, evv.v)]
        ddst = dst.rearrange("(a p) t -> p a t", p=128)[:, :, s * S:(s + 1) * S]
        P.op("sync", lambda e, ddst=ddst: e.dma_start(
            ddst, xo[:, :].rearrange("p (a t) -> p a t", a=NA)),
            waits=P.w(evv), incs=[(sout, 16)])
        ctx["_xo_guard"] = sout.v
        nsr[s] = [(sout, sout.v)]
    ctx["_res_strip"] = nsr
    ctx["_ln_rows_g"] = evv.v


def emit_ffn0(ctx, src, dst):
    nc, P, c, wb, din = ctx["nc"], ctx["P"], ctx["c"], ctx["wb"], ctx["din"]
    sems = ctx["sems"]
    evv, evs, smm, sout = sems["evv"], sems["evs"], sems["smm"], sems["sout"]
    xs, xo, xn = ctx["xs"], ctx["xo"], ctx["xn"]
    abp, aux, pp = ctx["abp"], ctx["aux"], ctx["pp"]
    xnd, idxmap, yslot = ctx["xnd"], ctx["idxmap"], ctx["yslot"]
    pd = [ctx["prep_done"]["ffn0"]]
    es2 = ctx["stage_es"]
    esg = ctx["es"]
    tag = str(smm.v)
    rsl = [list(w) for w in ctx["_res_strip"]]

    def sb2(name, shape, dt=F32):
        return es2.enter_context(nc.sbuf_tensor(name + tag, list(shape), dt))

    hB = sb2("m_h", (128, 16 * S), BF16)
    gate_sb = sb2("m_gate", (128, 64), BF16)
    b1_sb = sb2("m_b1", (128, 128))
    b2_sb = sb2("m_b2", (128, 64))
    tri_sb = sb2("m_tri", (128, 128))
    iot_sb = sb2("m_iot", (128, 65))
    gls = sb2("m_gls", (8, S))
    gw = sb2("m_gw", (128, 4 * 32))
    mkb = sb2("m_mk", (128, 32))
    rkb = sb2("m_rk", (128, 32))
    wgc = sb2("m_wg", (128, 32))
    ofr = sb2("m_ofr", (1, 48))
    obf = sb2("m_obf", (128, 32))
    sel = sb2("m_sel", (128, 224))
    srcb = sb2("m_srcb", (128, 64), I32)
    pkb = sb2("m_pkb", (128, 16), I32)
    tidb = sb2("m_tid", (128, 8), I32)
    xtb = sb2("m_xt", (128, 4 * 1024), BF16)
    gx = sb2("m_gx", (128, 4 * 1024), BF16)
    xg = sb2("m_xg", (128, 8 * S), BF16)
    ysf = sb2("m_ysf", (128, 8 * S), BF16)
    ytk = sb2("m_ytk", (128, 4 * 1024), BF16)
    ixw = sb2("m_ixw", (128, 48), I32)
    yg = sb2("m_yg", (128, 4 * 1024), BF16)
    ysm = sb2("m_ysm", (128, 2 * 1024))

    sgl = Sem(nc, esg, "m_ld" + tag)
    sfl = Sem(nc, esg, "m_fl" + tag)
    ssc = Sem(nc, esg, "m_sc" + tag)
    sxnd = Sem(nc, esg, "m_xnd" + tag)
    six = [Sem(nc, esg, "m_ix%d%s" % (i, tag)) for i in range(2)]
    sgx = [Sem(nc, esg, "m_gx%d%s" % (i, tag)) for i in range(4)]
    svd = Sem(nc, esg, "m_vd" + tag)
    sy3 = [Sem(nc, esg, "m_y3%d%s" % (i, tag)) for i in range(2)]

    P.op("gpsimd", lambda e: e.dma_start(
        gate_sb[:, :].rearrange("p (k e) -> p k e", e=8),
        wb["gate"].rearrange("(k p) e -> p k e", p=128)),
        waits=[pd[0]], incs=[(sgl, 16)])
    P.op("gpsimd", lambda e: e.dma_start(b1_sb[:, :], din["moe_b1"]), incs=[(sgl, 16)])
    P.op("gpsimd", lambda e: e.dma_start(b2_sb[:, :], din["moe_b2"]), incs=[(sgl, 16)])
    P.op("gpsimd", lambda e: e.dma_start(tri_sb[:, :], din["moe_tri"]), incs=[(sgl, 16)])
    P.op("gpsimd", lambda e: e.dma_start(iot_sb[:, :], din["moe_iot"]), incs=[(sgl, 16)])
    if not MOE_NOSC:
        P.op("gpsimd", lambda e: e.dma_start(
            idxmap.rearrange("(g p) cc -> p g cc", p=128),
            c["zeros_f"].ap().bitcast(I32)[:, 0:2 * (IMROWS // 128)]
            .rearrange("p (g cc) -> p g cc", cc=2)),
            waits=list(ctx.get("_moe_prev_ix", [])) + [ctx["zrdy"]],
            incs=[(sfl, 16)])
    mld = [(sgl, sgl.v)]
    P.op("vector", lambda e: e.tensor_copy(tidb[:, 0:1], iot_sb[:, 64:65]),
         waits=P.w(sgl), incs=[(evv, 1)])
    P.op("vector", lambda e: e.memset(ofr[0:1, :], 0.0), incs=[(evv, 1)])

    ln = ctx.pop("ln0")
    pg0 = list(ctx.get("_gemm_guard", []))
    xreg = [list(pg0) for _ in range(2)]
    ctx["_m_rkg"] = list(pg0)
    sscT = ssc.v
    for s in range(NSTRIP):
        xh, xsp, par, xnw = ln.apply(s)
        ctx["_xs_guard_par"][par] = list(xnw)

        glp = aux[0:8, 0:S]
        for k in range(NA):
            P.op("tensor", lambda e, k=k, xh=xh: e.matmul(
                glp, gate_sb[:, k * 8:(k + 1) * 8], xh[:, k * S:(k + 1) * S],
                start=(k == 0), stop=(k == NA - 1)),
                waits=(xnw + mld + [(evv, ctx.get("_moe_aux_g", 0))]),
                incs=[(smm, 1)] if k == NA - 1 else [])
        P.op("vector", lambda e: e.tensor_copy(gls[:, :], glp),
             waits=P.w(smm), incs=[(evv, 1)])
        glv = evv.v
        for j in range(4):
            P.op("tensor", lambda e, j=j: e.transpose(
                aux[0:128, S + j * 8: S + (j + 1) * 8], gls[:, j * 128:(j + 1) * 128],
                c["ident_f"][0:8, 0:8]),
                waits=[(evv, glv), (evv, ctx.get("_moe_aux2_g", 0))],
                incs=[(smm, 1)] if j == 3 else [])
        gtv = smm.v
        for j in range(4):
            eS = gw[:, j * 32 + 0: j * 32 + 8]
            tS = gw[:, j * 32 + 8: j * 32 + 16]
            g9 = gw[:, j * 32 + 16: j * 32 + 25]
            nmx = gw[:, j * 32 + 25: j * 32 + 26]
            m2c = gw[:, j * 32 + 26: j * 32 + 27]
            nrm = gw[:, j * 32 + 27: j * 32 + 28]
            tj = aux[0:128, S + j * 8: S + (j + 1) * 8]
            P.op("vector", lambda e, tj=tj, nmx=nmx: e.reduce_max(
                nmx, tj, axis=AXX, negate=True), waits=[(smm, gtv)], incs=[(evv, 1)])
            P.op("scalar", lambda e, tj=tj, eS=eS, nmx=nmx: e.activation(
                eS, tj, AF.Exp, bias=nmx), waits=P.w(evv), incs=[(evs, 1)])
            P.op("vector", lambda e, eS=eS, tS=tS: e.tensor_scalar(
                tS, eS, 1.0, None, op0=ALU.is_lt), waits=P.w(evs), incs=[(evv, 1)])
            P.op("vector", lambda e, eS=eS, tS=tS: e.tensor_mul(tS, tS, eS),
                 waits=P.w(evv), incs=[(evv, 1)])
            P.op("vector", lambda e, tS=tS, m2c=m2c: e.reduce_max(m2c, tS, axis=AXX),
                 waits=P.w(evv), incs=[(evv, 1)])
            P.op("vector", lambda e, eS=eS, tS=tS, m2c=m2c: e.tensor_scalar(
                tS, eS, m2c, None, op0=ALU.is_ge), waits=P.w(evv), incs=[(evv, 1)])
            P.op("vector", lambda e, tS=tS, j=j: e.tensor_copy(
                mkb[:, j * 8:(j + 1) * 8], tS), waits=P.w(evv), incs=[(evv, 1)])
            P.op("vector", lambda e, eS=eS, tS=tS, g9=g9: e.tensor_mul(
                g9[:, 0:8], tS, eS), waits=P.w(evv), incs=[(evv, 1)])
            P.op("vector", lambda e, m2c=m2c, nrm=nrm: e.tensor_scalar(
                nrm, m2c, 1.0, None, op0=ALU.add), waits=P.w(evv), incs=[(evv, 1)])
            P.op("vector", lambda e, nrm=nrm: e.reciprocal(nrm, nrm),
                 waits=P.w(evv), incs=[(evv, 1)])
            P.op("vector", lambda e, nrm=nrm, g9=g9: e.tensor_scalar(
                g9[:, 8:9], nrm, 0.5, None, op0=ALU.mult), waits=P.w(evv), incs=[(evv, 1)])
            P.op("vector", lambda e, g9=g9: e.tensor_scalar(
                g9[:, 0:8], g9[:, 0:8], g9[:, 8:9], None, op0=ALU.mult),
                waits=P.w(evv), incs=[(evv, 1)])
            P.op("vector", lambda e, g9=g9, j=j: e.tensor_copy(
                wgc[:, j * 8:(j + 1) * 8], g9[:, 0:8]), waits=P.w(evv), incs=[(evv, 1)])
        gdone = evv.v
        ctx["_moe_aux_g"] = glv
        ctx["_moe_aux2_g"] = gdone

        rkg = ctx["_m_rkg"]
        for j in range(4):
            P.op("tensor", lambda e, j=j: e.matmul(
                pp[:, j * 8:(j + 1) * 8], tri_sb[:, :], mkb[:, j * 8:(j + 1) * 8],
                start=True, stop=True),
                waits=([(evv, gdone)] + rkg + mld) if j == 0 else [], incs=[])
            P.op("tensor", lambda e, j=j: e.matmul(
                pp[0:1, 64 + j * 8: 64 + (j + 1) * 8], c["ones_col_f"][:, :],
                mkb[:, j * 8:(j + 1) * 8], start=True, stop=True),
                incs=[(smm, 1)] if j == 3 else [])
        rkv = smm.v
        P.op("vector", lambda e: e.tensor_copy(ofr[0:1, 8:16], ofr[0:1, 0:8]),
             waits=[(smm, rkv)] + P.w(evv), incs=[(evv, 1)])
        for j in range(3):
            P.op("vector", lambda e, j=j: e.tensor_add(
                ofr[0:1, 16 + j * 8: 24 + j * 8], ofr[0:1, 8 + j * 8: 16 + j * 8],
                pp[0:1, 64 + j * 8: 72 + j * 8]), waits=P.w(evv), incs=[(evv, 1)])
        P.op("vector", lambda e: e.tensor_add(
            ofr[0:1, 0:8], ofr[0:1, 32:40], pp[0:1, 88:96]), waits=P.w(evv),
            incs=[(evv, 1)])
        pfv = evv.v
        for j in range(4):
            P.op("tensor", lambda e, j=j: e.matmul(
                pp[:, S + j * 8: S + (j + 1) * 8], c["ones_row_f"][:, :],
                ofr[0:1, 8 + j * 8: 16 + j * 8], start=True, stop=True),
                waits=[(evv, pfv)] if j == 0 else [],
                incs=[(smm, 1)] if j == 3 else [])
        P.op("vector", lambda e: e.tensor_copy(obf[:, :], pp[:, S:S + 32]),
             waits=P.w(smm), incs=[(evv, 1)])
        P.op("vector", lambda e: e.tensor_add(rkb[:, :], pp[:, 0:32], obf[:, :]),
             waits=P.w(evv), incs=[(evv, 1)])
        ctx["_m_rkg"] = [(evv, evv.v)]

        def segv(fn, dslc, sslc):
            P.op("vector", lambda e, fn=fn, dslc=dslc, sslc=sslc: fn(
                e, sel[:, dslc[0]:dslc[1]].rearrange("p (g o) -> p g o", o=1),
                sel[:, sslc[0]:sslc[1]].rearrange("p (g cc) -> p g cc", cc=8)),
                waits=P.w(evv), incs=[(evv, 1)])
        P.op("vector", lambda e: e.tensor_mul(sel[:, 0:32], mkb[:, :],
                                              iot_sb[:, 0:32]), waits=P.w(evv),
             incs=[(evv, 1)])
        segv(lambda e, d, s_: e.reduce_max(d, s_, axis=AXX), (128, 132), (0, 32))
        P.op("vector", lambda e: e.tensor_mul(sel[:, 32:64], mkb[:, :],
                                              iot_sb[:, 32:64]), waits=P.w(evv),
             incs=[(evv, 1)])
        segv(lambda e, d, s_: e.reduce_max(d, s_, axis=AXX), (132, 136), (32, 64))
        P.op("vector", lambda e: e.tensor_scalar(
            sel[:, 136:140], sel[:, 132:136], -1.0, 7.0, op0=ALU.mult, op1=ALU.add),
            waits=P.w(evv), incs=[(evv, 1)])
        for sl, eoff in ((0, 136), (1, 128)):
            ohb = 64 + sl * 32
            for j in range(4):
                P.op("vector", lambda e, j=j, ohb=ohb, eoff=eoff: e.tensor_scalar(
                    sel[:, ohb + j * 8: ohb + (j + 1) * 8], iot_sb[:, 0:8],
                    sel[:, eoff + j: eoff + j + 1], None, op0=ALU.is_equal),
                    waits=P.w(evv), incs=[(evv, 1)])
            P.op("vector", lambda e, ohb=ohb: e.tensor_mul(
                sel[:, 160:192], sel[:, ohb:ohb + 32], rkb[:, :]),
                waits=P.w(evv), incs=[(evv, 1)])
            rko = 140 + sl * 4
            segv(lambda e, d, s_: e.reduce_sum(d, s_, axis=AXX),
                 (rko, rko + 4), (160, 192))
            P.op("vector", lambda e, ohb=ohb: e.tensor_mul(
                sel[:, 160:192], sel[:, ohb:ohb + 32], wgc[:, :]),
                waits=P.w(evv), incs=[(evv, 1)])
            wo_ = 148 + sl * 4
            segv(lambda e, d, s_: e.reduce_sum(d, s_, axis=AXX),
                 (wo_, wo_ + 4), (160, 192))
            P.op("vector", lambda e, rko=rko: e.tensor_scalar(
                sel[:, 192:196], sel[:, rko:rko + 4], 1.0, -1.0, op0=ALU.mult,
                op1=ALU.add), waits=P.w(evv), incs=[(evv, 1)])
            P.op("vector", lambda e, eoff=eoff: e.scalar_tensor_tensor(
                sel[:, 196:200], sel[:, eoff:eoff + 4], float(CAP), sel[:, 192:196],
                op0=ALU.mult, op1=ALU.add), waits=P.w(evv), incs=[(evv, 1)])
            P.op("vector", lambda e, rko=rko: e.tensor_scalar(
                sel[:, 200:204], sel[:, rko:rko + 4], CAP + 0.5, None,
                op0=ALU.is_ge), waits=P.w(evv), incs=[(evv, 1)])
            P.op("vector", lambda e: e.scalar_tensor_tensor(
                sel[:, 204:208], sel[:, 200:204], 30000.0, sel[:, 196:200],
                op0=ALU.mult, op1=ALU.add), waits=P.w(evv), incs=[(evv, 1)])
            P.op("vector", lambda e: e.tensor_scalar(
                sel[:, 208:212], sel[:, 204:208], float(NSLOT), None, op0=ALU.min),
                waits=P.w(evv), incs=[(evv, 1)])
            P.op("vector", lambda e, sl=sl, s=s: e.tensor_copy(
                srcb[:, sl * 32 + s * 4: sl * 32 + s * 4 + 4], sel[:, 208:212]),
                waits=P.w(evv), incs=[(evv, 1)])
        for j in range(4):
            P.op("vector", lambda e, j=j, s=s: e.tensor_scalar(
                tidb[:, 4 + j:5 + j], tidb[:, 0:1], s * S + j * 128, None,
                op0=ALU.add), waits=([(ssc, sscT)] if j == 0 else []) + P.w(evv),
                incs=[(evv, 1)])
        for j in range(4):
            for sl in range(2):
                pc = j * 2 + sl
                P.op("vector", lambda e, j=j, pc=pc: e.tensor_copy(
                    pkb[:, 2 * pc:2 * pc + 1], tidb[:, 4 + j:5 + j]),
                    waits=P.w(evv), incs=[(evv, 1)])
                wo_ = 148 + sl * 4
                P.op("vector", lambda e, pc=pc, wo_=wo_, j=j: e.tensor_copy(
                    pkb.ap().bitcast(F32)[:, 2 * pc + 1:2 * pc + 2],
                    sel[:, wo_ + j:wo_ + j + 1]), waits=P.w(evv), incs=[(evv, 1)])
        pkv = evv.v
        for j in range(4):
            for sl in range(2):
                if MOE_NOSC:
                    break
                pc = j * 2 + sl
                icol = srcb[:, sl * 32 + s * 4 + j: sl * 32 + s * 4 + j + 1]
                P.op("gpsimd", lambda e, pc=pc, icol=icol: e.indirect_dma_start(
                    idxmap, IndirectOffsetOnAxis(icol, 0),
                    pkb[:, 2 * pc:2 * pc + 2], None),
                    waits=[(evv, pkv), (sfl, 16)] if (j == 0 and sl == 0) else [],
                    incs=[(ssc, 16)])
        sscT = ssc.v

        xndp = sxnd.v
        for j in range(4 if not MOE_NOXND else 0):
            for a in range(NA):
                r = (j * NA + a) % 2
                off = (2 + r) * S
                P.op("tensor", lambda e, j=j, a=a, off=off, xh=xh: e.transpose(
                    pp[:, off:off + 64].bitcast(BF16),
                    xh[:, a * S + j * 128: a * S + j * 128 + 128],
                    c["ident_b"][:, :]),
                    waits=xnw + xreg[r], incs=[(smm, 1)])
                tv = smm.v
                P.op("scalar", lambda e, j=j, a=a, off=off: e.copy(
                    xtb[:, j * 1024 + a * 128: j * 1024 + (a + 1) * 128],
                    pp[:, off:off + 64].bitcast(BF16)),
                    waits=[(smm, tv)] + ([(sxnd, xndp)] if (j == 0 and a == 0) else []),
                    incs=[(evs, 1)])
                xreg[r] = [(evs, evs.v)]
            P.op("sync", lambda e, j=j, s=s: e.dma_start(
                xnd[s * S + j * 128: s * S + (j + 1) * 128, :],
                xtb[:, j * 1024:(j + 1) * 1024]),
                waits=P.w(evs), incs=[(sxnd, 16)])
        ctx["_xn_guard_par"][par] = smm.v
        if s + 1 < NSTRIP:
            ln.pre(s + 1)

    ctx["_gemm_guard"] = [(evv, evv.v), (evs, evs.v)]

    sxndT = [(sxnd, sxnd.v)]
    if MOE_PHASE < 2:
        _moe_dummy_out(ctx, src, dst)
        return
    sscW = [(ssc, ssc.v)]
    rbuf = [aux, aux, abp, abp]
    roff = [0, S, 0, S]
    areg = [[(evv, evv.v), (evs, evs.v)] for _ in range(4)]
    xg_t_done = [0, 0, 0, 0]
    ixg = [[], []]
    w2mm_prev = 0
    yT_done = 0
    for e_ in range(8):
        slw = e_ % 2
        ixs = ixw[:, slw * 24: slw * 24 + 24]
        P.op("gpsimd", lambda e, e_=e_, ixs=ixs: e.dma_start(
            ixs.rearrange("p (g cc) -> p g cc", cc=2),
            idxmap[e_ * CAP:(e_ + 1) * CAP, :].rearrange("(g p) cc -> p g cc", p=128)),
            waits=sscW + ixg[slw], incs=[(six[slw], 16)])
        ixv = six[slw].v
        for ch in range(3):
            for sg in range(4):
                gi = ch * 4 + sg
                P.op("gpsimd", lambda e, gi=gi, sg=sg, ixs=ixs: e.indirect_dma_start(
                    gx[:, sg * 1024:(sg + 1) * 1024], None, xnd,
                    IndirectOffsetOnAxis(ixs[:, gi * 2:gi * 2 + 1], 0)),
                    waits=[(six[slw], ixv)] + sxndT + [(smm, xg_t_done[sg])],
                    incs=[(sgx[sg], 16)])
            for sg in range(4):
                for a in range(NA):
                    r = (sg * NA + a) % 4
                    rb, off = rbuf[r], roff[r]
                    P.op("tensor", lambda e, sg=sg, a=a, rb=rb, off=off: e.transpose(
                        rb[:, off:off + 64].bitcast(BF16),
                        gx[:, sg * 1024 + a * 128: sg * 1024 + a * 128 + 128],
                        c["ident_b"][:, :]),
                        waits=[(sgx[sg], sgx[sg].v)] + areg[r], incs=[(smm, 1)])
                    tv = smm.v
                    P.op("scalar", lambda e, sg=sg, a=a, rb=rb, off=off: e.copy(
                        xg[:, a * S + sg * 128: a * S + sg * 128 + 128],
                        rb[:, off:off + 64].bitcast(BF16)),
                        waits=[(smm, tv), (smm, w2mm_prev)], incs=[(evs, 1)])
                    areg[r] = [(evs, evs.v)]
                xg_t_done[sg] = smm.v
            xgv = [(evs, evs.v)]

            def ev_w1(mt, pap, gv, e_=e_):
                P.op("scalar", lambda e, mt=mt, pap=pap, e_=e_: e.activation(
                    hB[:, mt * S:(mt + 1) * S], pap, AF.Relu,
                    bias=b1_sb[:, e_ * 16 + mt: e_ * 16 + mt + 1]),
                    waits=[(smm, gv)] + mld, incs=[(evs, 1)])
            emit_gemm(ctx, wb["moe_w1"][e_ * D:(e_ + 1) * D, :], NA, 16,
                      lambda k: xg[:, k * S:(k + 1) * S], ev_w1, xgv, prep_wait=pd)
            w1ev = evs.v

            def ev_w2(mt, pap, gv, e_=e_, yT=yT_done):
                P.op("vector", lambda e, mt=mt, pap=pap, e_=e_: e.tensor_scalar(
                    ysf[:, mt * S:(mt + 1) * S], pap,
                    b2_sb[:, e_ * 8 + mt: e_ * 8 + mt + 1], None, op0=ALU.add),
                    waits=[(smm, gv), (smm, yT)] + mld, incs=[(evv, 1)])
            emit_gemm(ctx, wb["moe_w2"][e_ * 2048:(e_ + 1) * 2048, :], 16, NA,
                      lambda k: hB[:, k * S:(k + 1) * S], ev_w2, [(evs, w1ev)],
                      prep_wait=pd)
            w2mm_prev = smm.v
            v2 = evv.v
            first_y = True
            for sg in range(4):
                gi = ch * 4 + sg
                wcol = ixw.ap().bitcast(F32)[:, slw * 24 + gi * 2 + 1:
                                             slw * 24 + gi * 2 + 2]
                for mt in range(NA):
                    r = (sg * NA + mt) % 4
                    rb, off = rbuf[r], roff[r]
                    P.op("tensor", lambda e, sg=sg, mt=mt, rb=rb, off=off: e.transpose(
                        rb[:, off:off + 64].bitcast(BF16),
                        ysf[:, mt * S + sg * 128: mt * S + sg * 128 + 128],
                        c["ident_b"][:, :]),
                        waits=[(evv, v2)] + areg[r], incs=[(smm, 1)])
                    tv = smm.v
                    P.op("vector", lambda e, sg=sg, mt=mt, rb=rb, off=off, wcol=wcol:
                         e.tensor_scalar(
                             ytk[:, sg * 1024 + mt * 128: sg * 1024 + (mt + 1) * 128],
                             rb[:, off:off + 64].bitcast(BF16), wcol, None,
                             op0=ALU.mult),
                         waits=([(smm, tv), (six[slw], ixv)] +
                                ([(svd, svd.v)] if first_y else [])),
                         incs=[(evv, 1)])
                    first_y = False
                    areg[r] = [(evv, evv.v)]
            yT_done = smm.v
            P.op("sync", lambda e, e_=e_, ch=ch: e.dma_start(
                yslot[e_ * CAP + ch * 512: e_ * CAP + (ch + 1) * 512, :]
                .rearrange("(sg p) dd -> p sg dd", p=128),
                ytk[:, :].rearrange("p (sg dd) -> p sg dd", dd=1024)),
                waits=P.w(evv), incs=[(svd, 16)])
        ixg[slw] = [(evv, evv.v)]

    if MOE_PHASE < 3:
        _moe_dummy_out(ctx, src, dst)
        return

    svdW = [(svd, svd.v)]
    rbuf = [aux, aux, abp, abp]
    roff = [0, S, 0, S]
    areg = [[(evv, evv.v), (evs, evs.v)] for _ in range(4)]
    ygg = [[], []]
    ysm_t = [0, 0]
    sxp = {0: sems["sx"], 1: sems["sx1"]}
    nsr = [[] for _ in range(NSTRIP)]
    for s in range(NSTRIP):
        par = s % 2
        xsp = xs[:, par * NA * S:(par + 1) * NA * S]
        srcv = src.rearrange("(a p) t -> p a t", p=128)[:, :, s * S:(s + 1) * S]
        P.op("gpsimd", lambda e, srcv=srcv, xsp=xsp: e.dma_start(
            xsp.rearrange("p (a t) -> p a t", a=NA), srcv),
            waits=list(ctx["_xs_guard_par"][par]) + rsl[s],
            incs=[(sxp[par], 16)])
        xw3 = [(sxp[par], sxp[par].v)]
        xog = [(sout, ctx.get("_xo_guard", 0))]
        for j in range(4):
            jp = j % 2
            for sl in range(2):
                icol = srcb[:, sl * 32 + s * 4 + j: sl * 32 + s * 4 + j + 1]
                P.op("gpsimd", lambda e, jp=jp, sl=sl, icol=icol: e.indirect_dma_start(
                    yg[:, (jp * 2 + sl) * 1024:(jp * 2 + sl + 1) * 1024], None,
                    yslot, IndirectOffsetOnAxis(icol, 0)),
                    waits=svdW + ygg[jp], incs=[(sy3[jp], 16)])
            P.op("vector", lambda e, jp=jp: e.tensor_add(
                ysm[:, jp * 1024:(jp + 1) * 1024],
                yg[:, jp * 2048: jp * 2048 + 1024],
                yg[:, jp * 2048 + 1024: jp * 2048 + 2048]),
                waits=[(sy3[jp], sy3[jp].v), (smm, ysm_t[jp])], incs=[(evv, 1)])
            av = evv.v
            ygg[jp] = [(evv, av)]
            for a in range(NA):
                r = (j * NA + a) % 4
                rb, off = rbuf[r], roff[r]
                P.op("tensor", lambda e, jp=jp, a=a, rb=rb, off=off: e.transpose(
                    rb[:, off:off + 128],
                    ysm[:, jp * 1024 + a * 128: jp * 1024 + (a + 1) * 128],
                    c["ident_f"][:, :]),
                    waits=[(evv, av)] + areg[r], incs=[(smm, 1)])
                tv = smm.v
                P.op("vector", lambda e, a=a, j=j, rb=rb, off=off, xsp=xsp:
                     e.tensor_add(
                    xo[:, a * S + j * 128: a * S + j * 128 + 128],
                    rb[:, off:off + 128],
                    xsp[:, a * S + j * 128: a * S + j * 128 + 128]),
                    waits=([(smm, tv)] + xw3 +
                           (xog if (j == 0 and a == 0) else [])),
                    incs=[(evv, 1)])
                areg[r] = [(evv, evv.v)]
            ysm_t[jp] = smm.v
        ddst = dst.rearrange("(a p) t -> p a t", p=128)[:, :, s * S:(s + 1) * S]
        P.op("sync", lambda e, ddst=ddst: e.dma_start(
            ddst, xo[:, :].rearrange("p (a t) -> p a t", a=NA)),
            waits=P.w(evv), incs=[(sout, 16)])
        ctx["_xo_guard"] = sout.v
        ctx["_xs_guard_par"][par] = [(evv, evv.v)]
        nsr[s] = [(sout, sout.v)]
    ctx["_res_strip"] = nsr
    ctx["_ln_rows_g"] = evv.v


def _a0_selmap(kind):
    out = []
    for pt in range(6):
        heads = [h for h in range(3 * pt, min(3 * pt + 3, 16))]
        by_tile = {}
        for h in heads:
            a = h // 2 if kind == "q" else h // 4
            soff = (h % 2) * 64 if kind == "q" else (h % 4) * 32
            doff = (h - 3 * pt) * 32
            by_tile.setdefault(a, []).append((soff, doff))
        for a, lst in sorted(by_tile.items()):
            out.append((pt, a, lst))
    return out


def emit_attn0(ctx, src, dst):
    nc, P, c, wb, din = ctx["nc"], ctx["P"], ctx["c"], ctx["wb"], ctx["din"]
    sems = ctx["sems"]
    evv, evs, smm, sout = sems["evv"], sems["evs"], sems["smm"], sems["sout"]
    xs, xo, xn, x2b = ctx["xs"], ctx["xo"], ctx["xn"], ctx["x2b"]
    aux, abp, pp = ctx["aux"], ctx["abp"], ctx["pp"]
    pd = [ctx["prep_done"]["attn0"]]
    es2 = ctx["stage_es"]

    def sb2(name, shape, dt=BF16):
        return es2.enter_context(nc.sbuf_tensor(name, list(shape), dt))

    psb = sb2("a0_psb", (128, NA * S))
    kvb = sb2("a0_kvb", (128, S))
    knb = sb2("a0_knb", (128, 6 * T))
    vsb0 = sb2("a0_vsb", (128, 4 * 528))
    vh0 = sb2("a0_vh", (128, 2 * 32 * 33))
    qnb = sb2("a0_qnb", (128, 6 * S))
    attn_sb = sb2("a0_attn", (128, 6 * S))
    expA = sb2("a0_expA", (128, 2 * S))
    expB = sb2("a0_expB", (128, 2 * S))
    r16b = sb2("a0_r16b", (16, S))
    qkrow = sb2("a0_qkrow", (16, 2 * S), F32)
    zrow = sb2("a0_zrow", (1, S), F32)
    rbs = sb2("a0_rbs", (128, S))
    psv = sb2("a0_psv", (32, S))
    wkv_sb = sb2("a0_wkv_sb", (128, 8 * 128))
    wk_sb = sb2("a0_wk_sb", (128, 512))
    wv_sb = sb2("a0_wv_sb", (128, 512))
    qsel_sb = sb2("a0_qsel_sb", (128, 11 * 128))
    ksel_sb = sb2("a0_ksel_sb", (128, 8 * 128))
    qones_sb = sb2("a0_qones_sb", (128, 8 * 16))
    kones_sb = sb2("a0_kones_sb", (128, 4 * 16))
    hot_sb = sb2("a0_hot_sb", (16, 768))
    qg_sb = sb2("a0_qg_sb", (128, 1), F32)
    kg_sb = sb2("a0_kg_sb", (128, 1), F32)
    dm_sb = sb2("a0_dm_sb", (128, 4 * S))

    swl = Sem(nc, ctx["es"], "a0_ld")
    loads = [
        (wkv_sb[:, :].rearrange("p (k m) -> p k m", m=128),
         wb["a0_wkv"].rearrange("(k p) m -> p k m", p=128)),
        (wk_sb[:, :], wb["a0_wk"]),
        (wv_sb[:, :], wb["a0_wv"]),
        (qsel_sb[:, :], din["a0_qsel"]),
        (ksel_sb[:, :], din["a0_ksel"]),
        (qones_sb[:, :].rearrange("p (a h) -> p a h", h=16),
         din["a0_qones"].rearrange("(a p) h -> p a h", p=128)),
        (kones_sb[:, :].rearrange("p (a h) -> p a h", h=16),
         din["a0_kones"].rearrange("(a p) h -> p a h", p=128)),
        (hot_sb[:, :], din["a0_hot"]),
        (qg_sb[:, :], din["a0_qg"]),
        (kg_sb[:, :], din["a0_kg"]),
        (dm_sb[:, :], din["dmask"]),
    ]
    for (o, i_) in loads:
        P.op("gpsimd", lambda e, o=o, i_=i_: e.dma_start(o, i_),
             waits=pd, incs=[(swl, 16)])
    wl = [(swl, swl.v)]
    P.op("vector", lambda e: e.memset(attn_sb[:, :], 0.0), incs=[(evv, 1)])
    msdone = [(evv, evv.v)]
    vdram0 = ctx["vdram0"]
    svd0 = Sem(nc, ctx["es"], "a0_vst")
    vhs0 = [Sem(nc, ctx["es"], "a0_vh0"), Sem(nc, ctx["es"], "a0_vh1")]
    vh_free0 = {}
    vflat0 = vdram0.rearrange("t c -> (t c)")
    for i in range((T * 16 * 33) // 65536):
        P.op("gpsimd", lambda e, i=i: e.dma_start(
            vflat0[i * 65536:(i + 1) * 65536].rearrange("(p f) -> p f", p=128),
            c["ones_mat"][:, :]),
            incs=[(svd0, 16)])

    qmap = _a0_selmap("q")
    kmap = _a0_selmap("k")
    nsr = [[] for _ in range(NSTRIP)]
    ln = ctx.pop("ln0")

    for s in range(NSTRIP):
        xh, xsp, par, xnw = ln.apply(s)
        ln_done = xnw[0][1]

        def ev_q(mt, pap, gv):
            P.op("scalar", lambda e, mt=mt, pap=pap: e.copy(
                psb[:, mt * S:(mt + 1) * S], pap),
                waits=[(smm, gv)], incs=[(evs, 1)])
        emit_gemm(ctx, wb["a0_wq"], NA, NA, lambda k, xh=xh: xh[:, k * S:(k + 1) * S],
                  ev_q, xnw, prep_wait=pd)
        qcv = evs.v
        for a in range(NA):
            P.op("vector", lambda e, a=a: e.tensor_mul(
                x2b[:, a * S:(a + 1) * S], psb[:, a * S:(a + 1) * S],
                psb[:, a * S:(a + 1) * S]), waits=[(evs, qcv)], incs=[(evv, 1)])
        qsq = evv.v
        for a in range(NA):
            P.op("tensor", lambda e, a=a: e.matmul(
                aux[0:16, 0:S], qones_sb[:, a * 16:(a + 1) * 16],
                x2b[:, a * S:(a + 1) * S],
                start=(a == 0), stop=(a == NA - 1)),
                waits=[(evv, qsq), (evv, ctx.get("_ln_rows_g", 0))] + wl,
                incs=[(smm, 1)] if a == NA - 1 else [])
        qrv = smm.v

        def rows_rms(inv_n, wv):
            P.op("vector", lambda e, inv_n=inv_n: e.tensor_scalar(
                qkrow[:, 0:S], aux[0:16, 0:S], inv_n, 1e-6,
                op0=ALU.mult, op1=ALU.add),
                waits=[(smm, wv)], incs=[(evv, 1)])
            P.op("vector", lambda e: e.reciprocal(qkrow[:, S:2 * S], qkrow[:, 0:S]),
                 waits=P.w(evv), incs=[(evv, 1)])
            P.op("scalar", lambda e: e.activation(qkrow[:, 0:S], qkrow[:, S:2 * S],
                                                  AF.Sqrt), waits=P.w(evv), incs=[(evs, 1)])
            P.op("vector", lambda e: e.tensor_copy(r16b[:, :], qkrow[:, 0:S]),
                 waits=P.w(evs), incs=[(evv, 1)])
            ctx["_ln_rows_g"] = evv.v
            return evv.v

        rqv = rows_rms(1.0 / 64, qrv)

        def compact(selmap, sel_sb, g_ap, dstfn, rrdy):
            qcp = aux[0:128, S:2 * S]
            byp = {}
            for (pt, a, lst) in selmap:
                byp.setdefault(pt, []).append((a, lst))
            bi = 0
            for pt in sorted(byp):
                blocks = byp[pt]
                for i, (a, lst) in enumerate(blocks):
                    P.op("tensor", lambda e, bi=bi, a=a, i=i, n=len(blocks):
                         e.matmul(qcp, sel_sb[:, bi * 128:(bi + 1) * 128],
                                  psb[:, a * S:(a + 1) * S],
                                  start=(i == 0), stop=(i == n - 1)),
                         waits=[(evv, rrdy), (evv, ctx.get("_a0_ptg", 0))],
                         incs=[(smm, 1)] if i == len(blocks) - 1 else [])
                    bi += 1
                scv = smm.v
                P.op("tensor", lambda e, pt=pt: e.matmul(
                    abp[:, 0:S], hot_sb[:, pt * 128:(pt + 1) * 128], r16b[:, :],
                    start=True, stop=True),
                    waits=[(evv, rrdy), (evv, ctx.get("_a0_rbg", 0)), (evv, ln_done)],
                    incs=[(smm, 1)])
                P.op("vector", lambda e: e.tensor_copy(rbs[:, :], abp[:, 0:S]),
                     waits=P.w(smm), incs=[(evv, 1)])
                ctx["_a0_rbg"] = evv.v
                dap = dstfn(pt)
                P.op("vector", lambda e, dap=dap: e.scalar_tensor_tensor(
                    dap, qcp, g_ap, rbs[:, :], op0=ALU.mult, op1=ALU.mult),
                    waits=[(smm, scv)] + P.w(evv), incs=[(evv, 1)])
                ctx["_a0_ptg"] = evv.v

        compact(qmap, qsel_sb, qg_sb[:, 0:1],
                lambda pt: qnb[:, pt * S:(pt + 1) * S], rqv)
        qn_done = evv.v

        for a in range(NA):
            P.op("tensor", lambda e, a=a, xh=xh: e.matmul(
                abp[:, S:2 * S], wkv_sb[:, a * 128:(a + 1) * 128],
                xh[:, a * S:(a + 1) * S], start=(a == 0), stop=(a == NA - 1)),
                waits=[(evv, ln_done)] + wl + xnw,
                incs=[(smm, 1)] if a == NA - 1 else [])
        ctx["_xn_guard_par"][par] = smm.v
        P.op("vector", lambda e: e.tensor_copy(kvb[:, :], abp[:, S:2 * S]),
             waits=P.w(smm), incs=[(evv, 1)])
        kvv = evv.v
        for mt in range(4):
            P.op("tensor", lambda e, mt=mt: e.matmul(
                pp[:, mt * S:(mt + 1) * S], wk_sb[:, mt * 128:(mt + 1) * 128],
                kvb[:, :], start=True, stop=True),
                waits=[(evv, kvv)] + list(ctx.get("_gemm_guard", [])),
                incs=[(smm, 1)] if mt == 3 else [])
        kmv = smm.v
        for mt in range(4):
            P.op("scalar", lambda e, mt=mt: e.copy(
                psb[:, mt * S:(mt + 1) * S], pp[:, mt * S:(mt + 1) * S]),
                waits=[(smm, kmv), (evv, qn_done)], incs=[(evs, 1)])
        kcv = evs.v
        for a in range(4):
            P.op("vector", lambda e, a=a: e.tensor_mul(
                x2b[:, a * S:(a + 1) * S], psb[:, a * S:(a + 1) * S],
                psb[:, a * S:(a + 1) * S]), waits=[(evs, kcv)], incs=[(evv, 1)])
        ksq = evv.v
        for a in range(4):
            P.op("tensor", lambda e, a=a: e.matmul(
                aux[0:16, 0:S], kones_sb[:, a * 16:(a + 1) * 16],
                x2b[:, a * S:(a + 1) * S], start=(a == 0), stop=(a == 3)),
                waits=[(evv, ksq)], incs=[(smm, 1)] if a == 3 else [])
        rkv = rows_rms(1.0 / 32, smm.v)
        compact(kmap, ksel_sb, kg_sb[:, 0:1],
                lambda pt: knb[:, pt * T + s * S: pt * T + (s + 1) * S], rkv)
        kn_done = evv.v
        for ts in range(4):
            P.op("tensor", lambda e, ts=ts: e.matmul(
                pp[:, ts * S:(ts + 1) * S], kvb[:, ts * 128:(ts + 1) * 128],
                wv_sb[:, :], start=True, stop=True),
                waits=[(evv, kvv), (evs, kcv)],
                incs=[(smm, 1)] if ts == 3 else [])
        vmv = smm.v
        for ts in range(4):
            P.op("scalar", lambda e, ts=ts: e.copy(
                vsb0[:, ts * 528:(ts + 1) * 528].rearrange(
                    "p (h c) -> p h c", c=33)[:, :, 0:32],
                pp[:, ts * S:(ts + 1) * S].rearrange("p (h c) -> p h c", c=32)),
                waits=[(smm, vmv), (svd0, ctx.get("_a0_vsb_g", 0))], incs=[(evs, 1)])
        vcv = evs.v
        for ts in range(4):
            vdst = vdram0[s * 512 + ts * 128: s * 512 + (ts + 1) * 128, :].rearrange(
                "p (h c) -> p h c", c=33)[:, :, 0:32]
            P.op("gpsimd", lambda e, vdst=vdst, ts=ts: e.dma_start(
                vdst, vsb0[:, ts * 528:(ts + 1) * 528].rearrange(
                    "p (h c) -> p h c", c=33)[:, :, 0:32]),
                waits=[(evs, vcv), (svd0, svd0.v)], incs=[(svd0, 16)])
        svv0 = svd0.v
        ctx["_a0_vsb_g"] = svv0
        ctx["_ln_x2b_g"] = smm.v
        ctx["_ln_rows_g"] = evv.v
        if s + 1 < NSTRIP:
            ln.pre(s + 1)

        ntk = 4 * (s + 1)
        pv = pp[:, 2 * S:3 * S]
        zbp = pp[0:32, 3 * S:4 * S]
        expg = {}
        prev_psv = vcv
        prev_rz = evv.v
        for h in range(16):
            pt, oh = h // 3, (h % 3) * 32
            slot0 = h % 2
            vsrc = vdram0[0:ntk * 128, h * 33:(h + 1) * 33].rearrange(
                "(j p) c -> p j c", p=128)
            P.op("gpsimd", lambda e, vsrc=vsrc, slot0=slot0, ntk=ntk: e.dma_start(
                vh0[:, slot0 * 1056:slot0 * 1056 + ntk * 33].rearrange(
                    "p (j c) -> p j c", c=33), vsrc),
                waits=[(svd0, svv0)] + ([vh_free0[slot0]] if slot0 in vh_free0 else []),
                incs=[(vhs0[slot0], 16)])
            vhw0 = [(vhs0[slot0], vhs0[slot0].v)]
            mmv = {}
            for j in range(ntk + 1):
                if j < ntk:
                    bk = j % 2
                    scp = pp[:, bk * S:(bk + 1) * S]
                    P.op("tensor", lambda e, j=j, scp=scp, pt=pt, oh=oh: e.matmul(
                        scp, knb[oh:oh + 32, pt * T + j * 128: pt * T + (j + 1) * 128],
                        qnb[oh:oh + 32, pt * S:(pt + 1) * S], start=True, stop=True),
                        waits=([(evv, qn_done), (evv, kn_done), (evs, vcv)]
                               if (h == 0 and j == 0) else []) + expg.get(bk, []),
                        incs=[(smm, 1)])
                    mmv[j] = smm.v
                    diag = (j >= 4 * s)
                    P.op("scalar", lambda e, bk=bk, scp=scp: e.activation(
                        expA[:, bk * S:(bk + 1) * S], scp, AF.Exp,
                        bias=-12.0, scale=0.125),
                        waits=[(smm, mmv[j])], incs=[(evs, 1)])
                    expg[bk] = [(evs, evs.v)]
                    if diag:
                        P.op("vector", lambda e, bk=bk, jj=j - 4 * s: e.tensor_mul(
                            expB[:, bk * S:(bk + 1) * S],
                            expA[:, bk * S:(bk + 1) * S],
                            dm_sb[:, jj * S:(jj + 1) * S]),
                            waits=P.w(evs), incs=[(evv, 1)])
                        expg[bk] = expg[bk] + [(evv, evv.v)]
                if j >= 1:
                    jj = j - 1
                    bk = jj % 2
                    eb = expB if jj >= 4 * s else expA
                    st_, sp_ = (jj == 0), (jj == ntk - 1)
                    P.op("tensor", lambda e, jj=jj, bk=bk, eb=eb, slot0=slot0,
                         st_=st_, sp_=sp_: e.matmul(
                        pv[0:33, :], vh0[:, slot0 * 1056 + jj * 33: slot0 * 1056 + (jj + 1) * 33],
                        eb[:, bk * S:(bk + 1) * S], start=st_, stop=sp_),
                        waits=expg[bk] + vhw0 +
                              ([(evs, prev_psv), (evv, prev_rz)] if jj == 0 else []),
                        incs=[(smm, 1)] if sp_ else [])
            hv = smm.v
            vh_free0[slot0] = (smm, hv)
            P.op("scalar", lambda e: e.copy(psv[:, :], pv[0:32, :]),
                 waits=[(smm, hv), (evv, prev_rz)], incs=[(evs, 1)])
            prev_psv = evs.v
            P.op("vector", lambda e: e.reciprocal(zrow[:, :], pv[32:33, :]),
                 waits=[(smm, hv)], incs=[(evv, 1)])
            P.op("tensor", lambda e: e.matmul(
                zbp, c["ones_row_f"][0:1, 0:32], zrow[:, :],
                start=True, stop=True),
                waits=P.w(evv) + [(evv, ctx.get("_a0_zbg", 0))], incs=[(smm, 1)])
            P.op("vector", lambda e, pt=pt, oh=oh: e.tensor_mul(
                attn_sb[oh:oh + 32, pt * S:(pt + 1) * S], psv[:, :], zbp),
                waits=P.w(smm) + [(evs, prev_psv)] + msdone, incs=[(evv, 1)])
            ctx["_a0_zbg"] = evv.v
            prev_rz = evv.v
        attn_done = evv.v
        P.op("tensor", None, waits=[(evv, attn_done)])
        ctx["_gemm_guard"] = [(evs, evs.v), (evv, evv.v)]
        xog = [(sout, ctx.get("_xo_guard", 0))]

        def ev_wo(mt, pap, gv, xsp=xsp):
            P.op("vector", lambda e, mt=mt, pap=pap: e.tensor_add(
                xo[:, mt * S:(mt + 1) * S], xsp[:, mt * S:(mt + 1) * S], pap),
                waits=[(smm, gv)] + xog, incs=[(evv, 1)])
        emit_gemm(ctx, wb["a0_wo"], 6, NA,
                  lambda k: attn_sb[:, k * S:(k + 1) * S], ev_wo,
                  [(evv, attn_done)], prep_wait=pd)
        ctx["_xs_guard_par"][par] = [(evv, evv.v)]
        ddst = dst.rearrange("(a p) t -> p a t", p=128)[:, :, s * S:(s + 1) * S]
        P.op("sync", lambda e, ddst=ddst: e.dma_start(
            ddst, xo[:, :].rearrange("p (a t) -> p a t", a=NA)),
            waits=P.w(evv), incs=[(sout, 16)])
        ctx["_xo_guard"] = sout.v
        nsr[s] = [(sout, sout.v)]
        ctx["_ln_rows_g"] = evv.v
    ctx["_res_strip"] = nsr


def emit_attn1(ctx, src, dst):
    nc, P, c, wb, din = ctx["nc"], ctx["P"], ctx["c"], ctx["wb"], ctx["din"]
    sems = ctx["sems"]
    evv, evs, smm, sout = sems["evv"], sems["evs"], sems["smm"], sems["sout"]
    xs, xo, xn = ctx["xs"], ctx["xo"], ctx["xn"]
    aux, pp = ctx["aux"], ctx["pp"]
    vdram = ctx["vdram"]
    pd = [ctx["prep_done"]["attn1"]]
    es2 = ctx["stage_es"]

    def sb2(name, shape, dt=BF16):
        return es2.enter_context(nc.sbuf_tensor(name, list(shape), dt))

    knb = sb2("a1_knb", (128, 6 * T))
    q1b = sb2("a1_q1b", (128, 6 * S))
    vsb = sb2("a1_vsb", (128, 4 * 1024))
    vh = sb2("a1_vh", (128, 2 * 32 * 65))
    sqA = sb2("a1_sqA", (128, 2 * S))
    Ab = sb2("a1_Ab", (128, 2 * S))
    a1o = sb2("a1_out", (128, NA * S))
    psv = sb2("a1_psv", (64, S))
    zrow = sb2("a1_zrow", (1, 2 * S), F32)
    hm_sb = sb2("a1_hm", (128, 5 * S))

    swl = Sem(nc, ctx["es"], "a1_ld")
    P.op("gpsimd", lambda e: e.dma_start(hm_sb[:, :], din["hmask"]),
         waits=pd, incs=[(swl, 16)])
    wl = [(swl, swl.v)]
    svd = Sem(nc, ctx["es"], "a1_vst")
    vhs = [Sem(nc, ctx["es"], "a1_vh0"), Sem(nc, ctx["es"], "a1_vh1")]
    vh_free = {}
    vflat = vdram.rearrange("t c -> (t c)")
    nch = (T * 16 * 65) // 65536
    for i in range(nch):
        P.op("gpsimd", lambda e, i=i: e.dma_start(
            vflat[i * 65536:(i + 1) * 65536].rearrange("(p f) -> p f", p=128),
            c["ones_mat"][:, :]),
            incs=[(svd, 16)])

    nsr = [[] for _ in range(NSTRIP)]
    ln = ctx.pop("ln0")
    for s in range(NSTRIP):
        xh, xsp, par, xnw = ln.apply(s)
        ln_done = xnw[0][1]

        def ev_q(mt, pap, gv):
            P.op("scalar", lambda e, mt=mt, pap=pap: e.copy(
                q1b[:, mt * S:(mt + 1) * S], pap),
                waits=[(smm, gv), (smm, ctx.get("_a1_qg", 0))], incs=[(evs, 1)])
        emit_gemm(ctx, wb["a1_wqT"], NA, 6, lambda k, xh=xh: xh[:, k * S:(k + 1) * S],
                  ev_q, xnw, prep_wait=pd)

        def ev_k(mt, pap, gv, s=s):
            P.op("scalar", lambda e, mt=mt, pap=pap, s=s: e.copy(
                knb[:, mt * T + s * S: mt * T + (s + 1) * S], pap),
                waits=[(smm, gv)], incs=[(evs, 1)])
        emit_gemm(ctx, wb["a1_wkT"], NA, 6, lambda k, xh=xh: xh[:, k * S:(k + 1) * S],
                  ev_k, xnw, prep_wait=pd)
        qk_done = evs.v

        ring = ctx["ring"]
        wvre = wb["a1_wvT"].rearrange("(kt p) m -> p kt m", p=128)
        vguard = list(ctx.get("_gemm_guard", []))
        for half in range(2):
            loaded = {0: ring.load(wvre[:, 0:2, :], 2, 1024, extra_waits=pd),
                      1: ring.load(wvre[:, 2:4, :], 2, 1024, extra_waits=pd)}
            for i in range(4):
                ap, seq = loaded.pop(i)
                for ki in range(2):
                    k = 2 * i + ki
                    for tsl in range(2):
                        ts = 2 * half + tsl
                        pbank = pp[:, tsl * 2 * S: tsl * 2 * S + 2 * S]
                        for nch in range(2):
                            lhs = xh[:, k * S + ts * 128: k * S + (ts + 1) * 128]
                            P.op("tensor", lambda e, lhs=lhs, ap=ap, ki=ki,
                                 nch=nch, pbank=pbank, k=k: e.matmul(
                                     pbank[:, nch * S:(nch + 1) * S], lhs,
                                     ap[:, ki * 1024 + nch * 512: ki * 1024 + (nch + 1) * 512],
                                     start=(k == 0), stop=(k == NA - 1)),
                                 waits=(ring.use_wait(seq) +
                                        ((xnw + vguard +
                                          [(evs, ctx.get("_a1_vg", {}).get(tsl, 0))])
                                         if (i == 0 and ki == 0 and nch == 0) else [])),
                                 incs=[(smm, 1)] if (ki == 1 and tsl == 1 and nch == 1) else [])
                ring.freed_by[seq] = (smm, smm.v)
                if i + 2 < 4:
                    loaded[i + 2] = ring.load(wvre[:, 2 * (i + 2):2 * (i + 2) + 2, :],
                                              2, 1024, extra_waits=pd)
            gv = smm.v
            for tsl in range(2):
                ts = 2 * half + tsl
                pbank = pp[:, tsl * 2 * S: tsl * 2 * S + 2 * S]
                P.op("scalar", lambda e, ts=ts, pbank=pbank: e.copy(
                    vsb[:, ts * 1024:(ts + 1) * 1024], pbank[:, :]),
                    waits=[(smm, gv), (svd, ctx.get("_a1_vsb_g", 0))], incs=[(evs, 1)])
                ctx.setdefault("_a1_vg", {})[tsl] = evs.v
        vdone = evs.v
        ctx["_xn_guard_par"][par] = smm.v
        for ts in range(4):
            vdst = vdram[s * 512 + ts * 128: s * 512 + (ts + 1) * 128, :].rearrange(
                "p (h c) -> p h c", c=65)[:, :, 0:64]
            P.op("gpsimd", lambda e, vdst=vdst, ts=ts: e.dma_start(
                vdst, vsb[:, ts * 1024:(ts + 1) * 1024].rearrange(
                    "p (h c) -> p h c", c=64)),
                waits=[(evs, vdone), (svd, svd.v)], incs=[(svd, 16)])
        svv = svd.v
        ctx["_a1_vg"] = {0: vdone, 1: vdone}
        ctx["_a1_vsb_g"] = svv
        ctx["_ln_x2b_g"] = smm.v
        ctx["_ln_rows_g"] = evv.v
        if s + 1 < NSTRIP:
            ln.pre(s + 1)

        ntk = 4 * (s + 1)
        pv = pp[:, 2 * S:3 * S]
        zbp = pp[0:64, 3 * S:4 * S]
        sqg = {}
        prev_psv = evs.v
        prev_rz = evv.v
        for h in range(16):
            pt, oh = h // 3, (h % 3) * 32
            ot, oo = h // 2, (h % 2) * 64
            slot = h % 2
            vsrc = vdram[0:ntk * 128, h * 65:(h + 1) * 65].rearrange(
                "(j p) c -> p j c", p=128)
            P.op("gpsimd", lambda e, vsrc=vsrc, slot=slot, ntk=ntk: e.dma_start(
                vh[:, slot * 2080:slot * 2080 + ntk * 65].rearrange(
                    "p (j c) -> p j c", c=65), vsrc),
                waits=[(svd, svv)] + ([vh_free[slot]] if slot in vh_free else []),
                incs=[(vhs[slot], 16)])
            vhw = [(vhs[slot], vhs[slot].v)]
            mmv = {}
            for j in range(ntk + 1):
                if j < ntk:
                    bk = j % 2
                    scp = pp[:, bk * S:(bk + 1) * S]
                    P.op("tensor", lambda e, j=j, scp=scp, pt=pt, oh=oh: e.matmul(
                        scp, knb[oh:oh + 16, pt * T + j * 128: pt * T + (j + 1) * 128],
                        q1b[oh:oh + 16, pt * S:(pt + 1) * S], start=True, stop=True),
                        waits=([(evs, vdone)] if (h == 0 and j == 0) else []) +
                              sqg.get(bk, []),
                        incs=[(smm, 1)])
                    mmv[j] = smm.v
                    P.op("scalar", lambda e, bk=bk, scp=scp: e.activation(
                        sqA[:, bk * S:(bk + 1) * S], scp, AF.Square,
                        bias=1.0, scale=0.25),
                        waits=[(smm, mmv[j])], incs=[(evs, 1)])
                    sqg[bk] = [(evs, evs.v)]
                    hmof = (j - 4 * s) if j >= 4 * s else 4
                    P.op("vector", lambda e, bk=bk, hmof=hmof: e.scalar_tensor_tensor(
                        Ab[:, bk * S:(bk + 1) * S], sqA[:, bk * S:(bk + 1) * S],
                        1.0, hm_sb[:, hmof * S:(hmof + 1) * S],
                        op0=ALU.add, op1=ALU.mult),
                        waits=sqg[bk] + wl, incs=[(evv, 1)])
                    sqg[bk] = sqg[bk] + [(evv, evv.v)]
                if j >= 1:
                    jj = j - 1
                    bk = jj % 2
                    st_, sp_ = (jj == 0), (jj == ntk - 1)
                    P.op("tensor", lambda e, jj=jj, bk=bk, slot=slot,
                         st_=st_, sp_=sp_: e.matmul(
                        pv[0:65, :], vh[:, slot * 2080 + jj * 65: slot * 2080 + (jj + 1) * 65],
                        Ab[:, bk * S:(bk + 1) * S], start=st_, stop=sp_),
                        waits=sqg[bk] + vhw +
                              ([(evs, prev_psv), (evv, prev_rz)] if st_ else []),
                        incs=[(smm, 1)] if sp_ else [])
            hv = smm.v
            vh_free[slot] = (smm, hv)
            P.op("scalar", lambda e: e.copy(psv[:, :], pv[0:64, :]),
                 waits=[(smm, hv), (evv, prev_rz)], incs=[(evs, 1)])
            prev_psv = evs.v
            P.op("vector", lambda e: e.tensor_scalar(
                zrow[0:1, 0:S], pv[64:65, :], 1.0, 1e-12, op0=ALU.mult, op1=ALU.add),
                waits=[(smm, hv)], incs=[(evv, 1)])
            P.op("vector", lambda e: e.reciprocal(zrow[0:1, S:2 * S], zrow[0:1, 0:S]),
                 waits=P.w(evv), incs=[(evv, 1)])
            P.op("tensor", lambda e: e.matmul(
                zbp, c["ones_row_f"][0:1, 0:64], zrow[0:1, S:2 * S],
                start=True, stop=True),
                waits=P.w(evv) + [(evv, ctx.get("_a1_zbg", 0))], incs=[(smm, 1)])
            P.op("vector", lambda e, ot=ot, oo=oo: e.tensor_mul(
                a1o[oo:oo + 64, ot * S:(ot + 1) * S], psv[:, :], zbp),
                waits=P.w(smm) + [(evs, prev_psv)], incs=[(evv, 1)])
            ctx["_a1_zbg"] = evv.v
            prev_rz = evv.v
        attn_done = evv.v
        P.op("tensor", None, waits=[(evv, attn_done)])
        ctx["_gemm_guard"] = [(evs, evs.v), (evv, evv.v)]
        xog = [(sout, ctx.get("_xo_guard", 0))]

        def ev_wo(mt, pap, gv, xsp=xsp):
            P.op("vector", lambda e, mt=mt, pap=pap: e.tensor_add(
                xo[:, mt * S:(mt + 1) * S], xsp[:, mt * S:(mt + 1) * S], pap),
                waits=[(smm, gv)] + xog, incs=[(evv, 1)])
        emit_gemm(ctx, wb["a1_woT"], NA, NA,
                  lambda k: a1o[:, k * S:(k + 1) * S], ev_wo,
                  [(evv, attn_done)], prep_wait=pd)
        ctx["_xs_guard_par"][par] = [(evv, evv.v)]
        ctx["_a1_qg"] = smm.v
        ddst = dst.rearrange("(a p) t -> p a t", p=128)[:, :, s * S:(s + 1) * S]
        P.op("sync", lambda e, ddst=ddst: e.dma_start(
            ddst, xo[:, :].rearrange("p (a t) -> p a t", a=NA)),
            waits=P.w(evv), incs=[(sout, 16)])
        ctx["_xo_guard"] = sout.v
        nsr[s] = [(sout, sout.v)]
        ctx["_ln_rows_g"] = evv.v
    ctx["_res_strip"] = nsr


def make_in_map(inputs, core, stages=ALL_STAGES):
    f = np.ascontiguousarray
    m = {}
    m["xT"] = f(np.asarray(inputs["L0_attn_x"][core]).T)
    m["cconst"] = np.tile(np.array(REG_CONSTS, dtype=np.float32), (128, 1))
    m["ones_row"] = np.ones((1, 128 + S), dtype=np.float32)
    m["identc"] = np.eye(128, dtype=np.float32)
    need = set(st[:4] if st.startswith("ffn") else st for st in stages)
    if "ffn0" in need:
        m["gate"] = f(np.asarray(inputs["L0_ffn_gate"], dtype=np.float32))
        m["moe_w1"] = f(np.asarray(inputs["L0_ffn_W1"], dtype=np.float32))
        m["moe_b1"] = f(np.asarray(inputs["L0_ffn_b1"], dtype=np.float32)
                        .reshape(8, 16, 128).transpose(2, 0, 1).reshape(128, 128))
        m["moe_w2"] = f(np.asarray(inputs["L0_ffn_W2"], dtype=np.float32))
        m["moe_b2"] = f(np.asarray(inputs["L0_ffn_b2"], dtype=np.float32)
                        .reshape(8, 8, 128).transpose(2, 0, 1).reshape(128, 64))
        m["moe_tri"] = np.triu(np.ones((128, 128), dtype=np.float32))
        iot = np.zeros((128, 65), dtype=np.float32)
        iot[:, 0:32] = np.tile(np.arange(8, dtype=np.float32), 4)[None, :]
        iot[:, 32:64] = 7.0 - iot[:, 0:32]
        iot[:, 64] = np.arange(128, dtype=np.float32)
        m["moe_iot"] = iot
    if "attn0" in need:
        m["a0_wq"] = f(np.asarray(inputs["L0_attn_Wq"], dtype=np.float32))
        m["a0_wkv"] = f(np.asarray(inputs["L0_attn_Wkv"], dtype=np.float32))
        m["a0_wk"] = f(np.asarray(inputs["L0_attn_Wk"], dtype=np.float32))
        m["a0_wv"] = f(np.asarray(inputs["L0_attn_Wv"], dtype=np.float32))
        wo = np.asarray(inputs["L0_attn_Wo"], dtype=np.float32)
        wop = np.zeros((768, D), dtype=np.float32)
        for h in range(16):
            pt, off = h // 3, (h % 3) * 32
            wop[pt * 128 + off: pt * 128 + off + 32, :] = wo[h * 64: h * 64 + 32, :]
        m["a0_wo"] = wop
        m["a0_qg"] = f(np.tile(np.asarray(inputs["L0_attn_q_g"], dtype=np.float32)[:32], 4)[:, None])
        m["a0_kg"] = f(np.tile(np.asarray(inputs["L0_attn_k_g"], dtype=np.float32), 4)[:, None])
        qo = np.zeros((D, 16), dtype=np.float32)
        for h in range(16):
            qo[h * 64:(h + 1) * 64, h] = 1.0
        m["a0_qones"] = qo
        ko = np.zeros((512, 16), dtype=np.float32)
        for h in range(16):
            ko[h * 32:(h + 1) * 32, h] = 1.0
        m["a0_kones"] = ko
        for kind, nb in (("q", 11), ("k", 8)):
            sel = np.zeros((128, nb * 128), dtype=np.float32)
            for bi, (pt, a, lst) in enumerate(_a0_selmap(kind)):
                for (soff, doff) in lst:
                    for r in range(32):
                        sel[soff + r, bi * 128 + doff + r] = 1.0
            m["a0_" + kind + "sel"] = sel
        hot = np.zeros((16, 768), dtype=np.float32)
        for h in range(16):
            pt, off = h // 3, (h % 3) * 32
            hot[h, pt * 128 + off: pt * 128 + off + 32] = 1.0
        m["a0_hot"] = hot
        dm = np.zeros((128, 4 * S), dtype=np.float32)
        for j in range(4):
            tk = j * 128 + np.arange(128)[:, None]
            tq = np.arange(S)[None, :]
            dm[:, j * S:(j + 1) * S] = (tk <= tq).astype(np.float32)
        m["dmask"] = dm
    if "attn1" in need:
        def pad_a1(w):
            wt = np.asarray(w, dtype=np.float32).T
            out = np.zeros((D, 768), dtype=np.float32)
            for h in range(16):
                pt, off = h // 3, (h % 3) * 32
                out[:, pt * 128 + off: pt * 128 + off + 16] = wt[:, h * 16:(h + 1) * 16]
            return out
        m["a1_wqT"] = pad_a1(inputs["L1_attn_Wq"])
        m["a1_wkT"] = pad_a1(inputs["L1_attn_Wk"])
        m["a1_wvT"] = f(np.asarray(inputs["L1_attn_Wv"], dtype=np.float32).T)
        m["a1_woT"] = f(np.asarray(inputs["L1_attn_Wo"], dtype=np.float32).T)
        hm = np.zeros((128, 5 * S), dtype=np.float32)
        for j in range(4):
            tk = j * 128 + np.arange(128)[:, None]
            tq = np.arange(S)[None, :]
            hm[:, j * S:(j + 1) * S] = 0.5 * (tk <= tq)
        hm[:, 4 * S:] = 0.5
        m["hmask"] = hm
    if "ffn1" in need:
        m["f1_w1T"] = f(np.asarray(inputs["L1_ffn_w1_w"], dtype=np.float32).T)
        m["f1_w2T"] = f(np.asarray(inputs["L1_ffn_w2_w"], dtype=np.float32).T)
        m["f1_h1T"] = f(np.asarray(inputs["L1_ffn_h1_w"], dtype=np.float32).T)
        m["f1_h2T"] = f(np.asarray(inputs["L1_ffn_h2_w"], dtype=np.float32).T)
        m["f1_w3T"] = f(np.asarray(inputs["L1_ffn_w3_w"], dtype=np.float32).T)
        m["f1_s1"] = f(np.asarray(inputs["L1_ffn_w1_s"], dtype=np.float32).reshape(32, 128).T)
        m["f1_s2"] = f(np.asarray(inputs["L1_ffn_w2_s"], dtype=np.float32).reshape(32, 128).T)
        m["f1_sh1"] = f(np.asarray(inputs["L1_ffn_h1_s"], dtype=np.float32).reshape(32, 128).T)
        m["f1_sh2"] = f(np.asarray(inputs["L1_ffn_h2_s"], dtype=np.float32).reshape(32, 128).T)
        m["f1_s3"] = f(np.asarray(inputs["L1_ffn_w3_s"], dtype=np.float32).reshape(8, 128).T)
    return m


def kernel(**inputs):
    nc = build_nc(ALL_STAGES)
    in_maps = [make_in_map(inputs, cid) for cid in range(8)]
    res = run_bass_kernel_spmd(nc, in_maps, core_ids=list(range(8)))
    out = np.stack([np.asarray(r["outT"]).T for r in res.results], axis=0)
    return out.astype(np.float32)
